# version 11
# baseline (speedup 1.0000x reference)
import math
from contextlib import ExitStack
from functools import partial

import numpy as np
import ml_dtypes

import concourse.bass as bass
import concourse.mybir as mybir
from concourse.bass_utils import run_bass_kernel_spmd

F32 = mybir.dt.float32
BF16 = mybir.dt.bfloat16
AF = mybir.ActivationFunctionType
ALU = mybir.AluOpType
AXX = None

NCORES = 8
S = 16384
T = S // NCORES
D = 4096
DH = 1024
EPS = 1e-5
NCH = D // 128

PI = float(np.pi)
TWO_PI = float(2 * np.pi)
INV_2PI = float(1.0 / (2 * np.pi))
PI2_HI = 6.28125
PI2_LO = float(TWO_PI - PI2_HI)
MAGIC = 12582912.0

ENGINES = ("sync", "gpsimd", "tensor", "vector", "scalar")


class Ctr:
    __slots__ = ("sem", "val", "name")

    def __init__(self, sem, name):
        self.sem = sem
        self.val = 0
        self.name = name

    @property
    def now(self):
        return (self, self.val)


class Prog:

    CHAIN = ("vector", "scalar", "gpsimd")

    def __init__(self):
        self.nc = bass.Bass("TRN2", target_bir_lowering=False, debug=False)
        self.es = ExitStack()
        self.steps = {e: [] for e in ENGINES}
        self._nsem = 0
        self._auto = {}
        self._last = {}
        self._ctrs = []
        self._names = {}
        self._scopes = []
        self.phase_gate = ()

    def _uniq(self, name):
        k = self._names.get(name)
        if k is None:
            self._names[name] = 1
            return name
        self._names[name] = k + 1
        return f"{name}_{k}"

    def ctr(self, name=None):
        name = name or f"c{self._nsem}"
        self._nsem += 1
        c = Ctr(self.es.enter_context(self.nc.semaphore(self._uniq(name))),
                name)
        self._ctrs.append(c)
        return c

    def barrier(self):
        return tuple((c, c.val) for c in self._ctrs if c.val > 0)

    def push_scope(self):
        self._scopes.append(ExitStack())

    def pop_scope(self):
        self._scopes.pop().close()

    def sb(self, name, shape, dt):
        st = self._scopes[-1] if self._scopes else self.es
        return st.enter_context(self.nc.sbuf_tensor(self._uniq(name), shape,
                                                    dt))

    def nsb(self, name, shape, dt):
        return self.nc.sbuf_tensor(self._uniq(name), shape, dt)

    def nps(self, name, shape, dt):
        return self.nc.psum_tensor(self._uniq(name), shape, dt)

    def din(self, name, shape, dt):
        return self.nc.dram_tensor(name, shape, dt, kind="ExternalInput").ap()

    def dout(self, name, shape, dt):
        return self.nc.dram_tensor(name, shape, dt, kind="ExternalOutput").ap()

    def dtmp(self, name, shape, dt, **kw):
        return self.nc.dram_tensor(name, shape, dt, **kw).ap()

    def emit(self, eng, fn, after=(), inc=None, n=1, dma=False,
             nogate=False):
        after = tuple(after)
        if dma and self.phase_gate and not nogate:
            after = after + self.phase_gate
        if eng in self.CHAIN and not dma:
            prev = self._last.get(eng)
            if prev is not None:
                after = after + (prev,)
            if inc is None:
                if eng not in self._auto:
                    self._auto[eng] = self.ctr(f"auto_{eng}")
                inc = self._auto[eng]
        self.steps[eng].append((after, fn, inc, n))
        if inc is not None:
            inc.val += n
            if eng in self.CHAIN and not dma:
                self._last[eng] = (inc, inc.val)

    def dma(self, eng, dst, src, ctr, after=(), nogate=False, **kw):
        self.emit(eng, lambda e, d=dst, s=src: e.dma_start(out=d, in_=s, **kw),
                  after=after, inc=ctr, n=16, dma=True, nogate=nogate)

    def build(self):
        import os as _os
        self.check()
        if _os.environ.get("KCHECK"):
            self.check_dma_order()
        with self.nc.Block() as block:
            for eng in ENGINES:
                getattr(block, eng)(partial(self._replay, eng))
        self.es.close()
        return self.nc

    def _replay(self, eng, e):
        seen = {}
        for after, fn, inc, n in self.steps[eng]:
            for (c, v) in after:
                if v > 0 and seen.get(id(c), -1) < v:
                    e.wait_ge(c.sem, v)
                    seen[id(c)] = v
            inst = fn(e)
            if inc is not None:
                inst.then_inc(inc.sem, n)

    def check(self):
        vals = {}
        pos = {e: 0 for e in ENGINES}
        total = sum(len(v) for v in self.steps.values())
        done = 0
        progress = True
        while progress:
            progress = False
            for e in ENGINES:
                while pos[e] < len(self.steps[e]):
                    after, fn, inc, n = self.steps[e][pos[e]]
                    if any(vals.get(id(c), 0) < v for (c, v) in after):
                        break
                    if inc is not None:
                        vals[id(inc)] = vals.get(id(inc), 0) + n
                    pos[e] += 1
                    done += 1
                    progress = True
        if done != total:
            for e in ENGINES:
                if pos[e] < len(self.steps[e]):
                    after, fn, inc, n = self.steps[e][pos[e]]
                    blocked = [(c.name, v, vals.get(id(c), 0)) for (c, v) in after
                               if vals.get(id(c), 0) < v]
                    raise RuntimeError(
                        f"HANG: engine {e} step {pos[e]}/{len(self.steps[e])} "
                        f"blocked on {blocked}")
        return True

    def check_dma_order(self):
        prod = {}
        engs = {}
        for e in ENGINES:
            for idx, (after, fn, inc, n) in enumerate(self.steps[e]):
                if inc is not None:
                    prod.setdefault(id(inc), []).append((e, idx, n))
                    engs.setdefault(id(inc), set()).add(e)
        names = {}
        for e in ENGINES:
            for idx, (after, fn, inc, n) in enumerate(self.steps[e]):
                if inc is not None:
                    names[id(inc)] = inc.name
        for k, es in engs.items():
            plist = prod[k]
            if len(es) > 1 and any(n == 16 for (_, _, n) in plist):
                raise RuntimeError(f"dma ctr {names[k]} produced by {es}")
        succ = {}

        def add(a, b):
            succ.setdefault(a, []).append(b)

        nominal = {}
        for cid, plist in prod.items():
            cum = 0
            for (e, idx, n) in plist:
                cum += n
                nominal[(cid, cum)] = (e, idx)
        for e in ENGINES:
            for idx in range(1, len(self.steps[e])):
                add((e, idx - 1), (e, idx))
            for idx, (after, fn, inc, n) in enumerate(self.steps[e]):
                for (c, v) in after:
                    if v <= 0:
                        continue
                    plist = prod.get(id(c))
                    if plist is None:
                        raise RuntimeError(f"wait on unproduced ctr {c.name}")
                    cum = 0
                    pn = None
                    for (pe, pidx, n2) in plist:
                        cum += n2
                        if cum >= v:
                            pn = (pe, pidx)
                            break
                    if pn is None:
                        raise RuntimeError(
                            f"wait {c.name}>={v} never reached")
                    add(pn, (e, idx))
        import collections

        def reaches(src, dst):
            seen = set()
            dq = collections.deque([src])
            while dq:
                u = dq.popleft()
                if u == dst:
                    return True
                for w in succ.get(u, ()):
                    if w not in seen:
                        seen.add(w)
                        dq.append(w)
            return False

        bad = []
        for e in ENGINES:
            for idx, (after, fn, inc, n) in enumerate(self.steps[e]):
                for (c, v) in after:
                    plist = prod.get(id(c))
                    if not plist or plist[0][2] != 16:
                        continue
                    k = v // 16
                    if k >= len(plist):
                        continue
                    ue, uidx, _ = plist[k]
                    if not reaches((e, idx), (ue, uidx)):
                        bad.append((e, idx, c.name, v, f"{ue}[{uidx}]"))
        if bad:
            msg = "\n".join(
                f"  UNSOUND dma wait: {e}[{i}] waits {nm}>={v}; "
                f"uncovered dma at {u}" for (e, i, nm, v, u) in bad[:20])
            raise RuntimeError(
                f"{len(bad)} unsound DMA waits (completions can arrive "
                f"out of order):\n{msg}")
        return True


def make_identity(nc, identity):
    nc.gpsimd.memset(identity, 0.0)
    nc.gpsimd.affine_select(
        out=identity, in_=identity, compare_op=ALU.not_equal, fill=1.0,
        base=0, pattern=[[-1, identity.shape[0]]], channel_multiplier=1,
    )


AXX = mybir.AxisListType.X



def _pin(p, io, name, shape, dt):
    if io and name in io:
        return io[name]
    return p.din(name, shape, dt)


def _pout(p, io, name, shape, dt):
    if io and name in io:
        return io[name]
    return p.dout(name, shape, dt)


def build_l0a(p=None, io=None):
    standalone = p is None
    if standalone:
        p = Prog()
    nc = p.nc
    xT = _pin(p, io, "xT", [D, T], F32)
    Wqkv = _pin(p, io, "Wqkv", [D, 3 * DH], F32)
    bqkv = _pin(p, io, "bqkv", [3 * DH], F32)
    o_xnT = _pout(p, io, "xnT", [D, T], BF16)
    o_qT = _pout(p, io, "qT", [DH, T], BF16)
    o_kT = _pout(p, io, "kT", [DH, T], BF16)
    o_V = _pout(p, io, "V", [T, DH], BF16)
    o_qm = _pout(p, io, "qm", [1, 1], F32)
    o_km = _pout(p, io, "km", [1, 1], F32)

    shared_xn = False
    xball = p.sb("xball", [128, NCH * T], BF16)
    sqb = p.sb("sqb", [128, T], BF16)
    ones_col_bf = p.sb("ones_col_bf", [128, 1], BF16)
    ones_row_f = p.sb("ones_row_f", [1, 128], F32)
    row16 = p.sb("row16", [1, 16], F32)
    eps_cell = row16[0:1, 8:9]
    bvrow_f = p.sb("bvrow_f", [1, DH], F32)
    bqcols = p.sb("bqcols", [128, 16], F32)
    identbf = p.sb("identbf", [128, 128], BF16)

    c_x = p.ctr("x")
    c_xb = p.ctr("xb")
    c_sq = p.ctr("sq")
    c_st = p.ctr("st")
    c_fin = p.ctr("fin")
    c_xn = p.ctr("xn")
    c_w = p.ctr("w")
    c_mm = p.ctr("mm")
    c_ev = p.ctr("ev")
    c_out = p.ctr("out")
    c_gp = p.ctr("gp")
    c_tr = p.ctr("tr")
    c_kv = p.ctr("kv")
    c_vo = p.ctr("vo")

    p.emit("gpsimd", lambda e: e.memset(ones_col_bf[:, :], 1.0))
    p.emit("gpsimd", lambda e: e.memset(ones_row_f[:, :], 1.0))
    p.emit("gpsimd", lambda e: e.memset(eps_cell, EPS), inc=c_gp)

    p.emit("gpsimd", lambda e: e.memset(identbf[:, :], 0.0), inc=c_gp)
    p.emit("gpsimd", lambda e: e.affine_select(
        out=identbf[:, :], in_=identbf[:, :], compare_op=ALU.not_equal,
        fill=1.0, base=0, pattern=[[-1, 128]], channel_multiplier=1),
        after=((c_gp, c_gp.val),), inc=c_gp)
    p.dma("sync", bvrow_f[:, :], bqkv[None, 2 * DH:3 * DH], c_x)
    bqrow_scope = nc.sbuf_tensor("bqrow_bf", [1, 2 * DH], BF16)
    bqrow_bf = bqrow_scope.__enter__()
    p.dma("gpsimd", bqrow_bf[:, :], bqkv[None, 0:2 * DH], c_w)
    with nc.psum_tensor("psT", [128, 32], BF16) as psT:
        for j in range(16):
            p.emit("tensor",
                   lambda e, j=j: e.transpose(psT[:, 2 * j:2 * j + 1],
                                              bqrow_bf[0:1, j * 128:(j + 1) * 128],
                                              identbf[0:1, 0:1]),
                   after=(c_w.now, c_gp.now), inc=c_tr)
        p.emit("vector", lambda e: e.tensor_copy(bqcols[:, :], psT[:, 0:32:2]),
               after=(c_tr.now,), inc=c_ev)
    bqrow_scope.__exit__(None, None, None)

    with nc.sbuf_tensor("xf0", [128, T], F32) as xf0, \
         nc.sbuf_tensor("xf1", [128, T], F32) as xf1, \
         nc.sbuf_tensor("tmp4", [128, T], F32) as tmp, \
         nc.sbuf_tensor("ba", [128, T], F32) as ba, \
         nc.sbuf_tensor("bb", [128, T], F32) as bb, \
         nc.sbuf_tensor("l0rows", [1, 3 * T], F32) as rows:
        xf = [xf0, xf1]
        row_a = rows[0:1, 0:T]
        row_b = rows[0:1, T:2 * T]
        row_c = rows[0:1, 2 * T:3 * T]
        with nc.psum_tensor("psA", [128, 2048], F32) as psA, \
             nc.psum_tensor("psB", [128, 2048], F32) as psB:
            for c in range(2):
                p.dma("sync", xf[c % 2][:, :], xT[c * 128:(c + 1) * 128, :],
                      c_x, after=((c_tr, 16),))
            for c in range(NCH):
                buf = xf[c % 2]
                xld = c_x.val
                if c + 2 < NCH:
                    p.dma("sync", xf[(c + 2) % 2][:, :],
                          xT[(c + 2) * 128:(c + 3) * 128, :], c_x,
                          after=((c_xb, c + 1),))
                p.emit("vector",
                       lambda e, c=c, buf=buf: e.tensor_copy(
                           xball[:, c * T:(c + 1) * T], buf[:, :]),
                       after=((c_x, xld), (c_ev, 1)), inc=c_xb)
                p.emit("scalar",
                       lambda e, c=c: e.activation(
                           sqb[:, :], xball[:, c * T:(c + 1) * T], AF.Square),
                       after=((c_xb, c + 1), (c_st, 8 * c)), inc=c_sq)
                for j in range(4):
                    p.emit("tensor",
                           lambda e, c=c, j=j: e.matmul(
                               psA[0:1, j * 512:(j + 1) * 512], ones_col_bf[:, :],
                               xball[:, c * T + j * 512:c * T + (j + 1) * 512],
                               start=(c == 0), stop=(c == NCH - 1)),
                           after=((c_xb, c + 1),), inc=c_st)
                    p.emit("tensor",
                           lambda e, c=c, j=j: e.matmul(
                               psB[0:1, j * 512:(j + 1) * 512], ones_col_bf[:, :],
                               sqb[:, j * 512:(j + 1) * 512],
                               start=(c == 0), stop=(c == NCH - 1)),
                           after=((c_sq, c + 1),), inc=c_st)
            st_all = c_st.val
            p.emit("vector", lambda e: e.tensor_scalar(
                row_a, psA[0:1, :], 1.0 / D, None, ALU.mult),
                after=((c_st, st_all),), inc=c_fin)
            p.emit("vector", lambda e: e.tensor_scalar(
                row_b, psB[0:1, :], 1.0 / D, None, ALU.mult), inc=c_fin)
            p.emit("scalar", lambda e: e.activation(row_c, row_a, AF.Square),
                   after=((c_fin, c_fin.val),), inc=c_sq)
            p.emit("vector", lambda e: e.tensor_sub(row_b, row_b, row_c),
                   after=((c_sq, c_sq.val),), inc=c_fin)
            p.emit("scalar", lambda e: e.activation(row_b, row_b, AF.Ln, bias=eps_cell),
                   after=((c_fin, c_fin.val), (c_gp, 2)), inc=c_sq)
            p.emit("scalar", lambda e: e.activation(row_b, row_b, AF.Exp, scale=-0.5),
                   inc=c_sq)
            p.emit("vector", lambda e: e.tensor_mul(row_c, row_a, row_b),
                   after=((c_sq, c_sq.val),), inc=c_fin)
            p.emit("vector", lambda e: e.tensor_scalar(
                row_c, row_c, -1.0, None, ALU.mult), inc=c_fin)

        fin_done = c_fin.val
        with nc.psum_tensor("psA2", [128, 2048], F32) as psA2, \
             nc.psum_tensor("psB2", [128, 2048], F32) as psB2:
            for j in range(4):
                p.emit("tensor", lambda e, j=j: e.matmul(
                    psA2[:, j * 512:(j + 1) * 512], ones_row_f[:, :],
                    row_b[0:1, j * 512:(j + 1) * 512], start=True, stop=True),
                    after=((c_fin, fin_done),), inc=c_mm)
                p.emit("tensor", lambda e, j=j: e.matmul(
                    psB2[:, j * 512:(j + 1) * 512], ones_row_f[:, :],
                    row_c[0:1, j * 512:(j + 1) * 512], start=True, stop=True),
                    inc=c_mm)
            p.emit("vector", lambda e: e.tensor_copy(ba[:, :], psA2[:, :]),
                   after=((c_mm, c_mm.val),), inc=c_ev)
            p.emit("vector", lambda e: e.tensor_copy(bb[:, :], psB2[:, :]), inc=c_ev)

            bc_done = c_ev.val
            for c in range(NCH):
                p.emit("vector", lambda e, c=c: e.tensor_mul(
                    tmp[:, :], xball[:, c * T:(c + 1) * T], ba[:, :]),
                    after=((c_ev, bc_done),))
                p.emit("vector", lambda e, c=c: e.tensor_add(
                    xball[:, c * T:(c + 1) * T], tmp[:, :], bb[:, :]), inc=c_xn)
                if not shared_xn:
                    p.dma("sync", o_xnT[c * 128:(c + 1) * 128, :],
                          xball[:, c * T:(c + 1) * T], c_out,
                          after=((c_xn, c + 1),))
    xn_all = c_xn.val
    if shared_xn:
        io["xn_ready"] = (c_xn, xn_all)

    with nc.sbuf_tensor("wq0", [128, NCH * 128], BF16) as wq0, \
         nc.sbuf_tensor("wq1", [128, NCH * 128], BF16) as wq1, \
         nc.sbuf_tensor("rot0", [128, T], BF16) as rot0, \
         nc.sbuf_tensor("rot1", [128, T], BF16) as rot1, \
         nc.sbuf_tensor("wv", [128, NCH * 256], BF16) as wv, \
         nc.sbuf_tensor("sqacc_q", [128, T], BF16) as sqacc_q, \
         nc.sbuf_tensor("sqacc_k", [128, T], BF16) as sqacc_k:
        wq = [wq0, wq1]
        rot = [rot0, rot1]
        rot_guard = {}
        p.emit("gpsimd", lambda e: e.memset(sqacc_q[:, :], 0.0),
               after=((c_xn, xn_all),), inc=c_gp)
        p.emit("gpsimd", lambda e: e.memset(sqacc_k[:, :], 0.0), inc=c_gp)
        gp_sq = c_gp.val
        with nc.psum_tensor("psQA", [128, 2048], F32) as psQA, \
             nc.psum_tensor("psQB", [128, 2048], F32) as psQB:
            pq = [psQA, psQB]

            def ldwq(f, aft):
                buf2 = wq[f % 2]
                for h in range(2):
                    src = Wqkv[h * 2048:(h + 1) * 2048, f * 128:(f + 1) * 128]
                    src = src.rearrange("(c p) f -> p c f", p=128)
                    dst = buf2[:, h * 2048:(h + 1) * 2048].rearrange(
                        "p (c f) -> p c f", f=128)
                    p.dma("gpsimd", dst, src, c_w, after=aft)
                    aft = ()
            ldwq(0, ((c_xn, xn_all),))
            mm_at_f = {}
            for f in range(16):
                buf = wq[f % 2]
                if f + 1 < 16:
                    ldwq(f + 1, ((c_mm, mm_at_f.get(f - 1, 0)),))
                wq_ld = c_w.val
                ps = pq[f % 2]
                for c in range(NCH):
                    for lg in range(4):
                        p.emit("tensor", lambda e, c=c, lg=lg, buf=buf, ps=ps: e.matmul(
                            ps[:, lg * 512:(lg + 1) * 512],
                            buf[:, c * 128:(c + 1) * 128],
                            xball[:, c * T + lg * 512:c * T + (lg + 1) * 512],
                            start=(c == 0), stop=(c == NCH - 1)),
                            after=((c_w, wq_ld), (c_xn, xn_all),
                                   (c_ev, c_ev.val)),
                            inc=(c_mm if (c == NCH - 1 and lg == 3) else None))
                mm_at_f[f] = c_mm.val
                rb = rot[f % 2]
                is_q = f < 8
                mm_done = c_mm.val
                war = ((c_kv, c_kv.val), (c_sq, c_sq.val))
                for lg in range(4):
                    if is_q:
                        p.emit("vector", lambda e, lg=lg, f=f, ps=ps, rb=rb: e.tensor_scalar(
                            rb[:, lg * 512:(lg + 1) * 512],
                            ps[:, lg * 512:(lg + 1) * 512],
                            bqcols[:, f:f + 1], 1.0 / 32.0, ALU.add, ALU.mult),
                            after=((c_mm, mm_done),) + war,
                            inc=(c_ev if lg == 3 else None))
                    else:
                        p.emit("vector", lambda e, lg=lg, f=f, ps=ps, rb=rb: e.tensor_scalar(
                            rb[:, lg * 512:(lg + 1) * 512],
                            ps[:, lg * 512:(lg + 1) * 512],
                            bqcols[:, f:f + 1], None, ALU.add),
                            after=((c_mm, mm_done),) + war,
                            inc=(c_ev if lg == 3 else None))
                dst = o_qT if is_q else o_kT
                r0 = (f % 8) * 128
                p.dma("sync", dst[r0:r0 + 128, :], rb[:, :], c_kv,
                      after=((c_ev, c_ev.val),))
                p.emit("scalar", lambda e, rb=rb: e.activation(
                    sqb[:, :], rb[:, :], AF.Square),
                    after=((c_ev, c_ev.val), (c_sq, c_sq.val)), inc=c_sq)
                acc = sqacc_q if is_q else sqacc_k
                p.emit("vector", lambda e, acc=acc: e.tensor_add(
                    acc[:, :], acc[:, :], sqb[:, :]),
                    after=((c_sq, c_sq.val), (c_gp, gp_sq)), inc=c_ev)

        with nc.psum_tensor("psVA", [128, 2048], F32) as psVA, \
             nc.psum_tensor("psVB", [128, 2048], F32) as psVB:
            pv = [psVA, psVB]
            for vg in range(4):
                aft = ((c_ev, c_ev.val), (c_mm, c_mm.val))
                for h in range(2):
                    src = Wqkv[h * 2048:(h + 1) * 2048,
                               2048 + vg * 256:2048 + (vg + 1) * 256]
                    src = src.rearrange("(c p) f -> p c f", p=128)
                    dst = wv[:, h * 16 * 256:(h + 1) * 16 * 256].rearrange(
                        "p (c f) -> p c f", f=256)
                    p.dma("gpsimd", dst, src, c_w, after=aft)
                for lt in range(16):
                    ri = 16 + vg * 16 + lt
                    ps = pv[lt % 2][:, ((lt // 2) % 4) * 512:((lt // 2) % 4) * 512 + 256]
                    for c in range(NCH):
                        p.emit("tensor", lambda e, lt=lt, c=c, ps=ps: e.matmul(
                            ps, xball[:, c * T + lt * 128:c * T + lt * 128 + 128],
                            wv[:, c * 256:(c + 1) * 256],
                            start=(c == 0), stop=False),
                            after=((c_w, c_w.val), (c_ev, c_ev.val)))
                    p.emit("tensor", lambda e, vg=vg, ps=ps: e.matmul(
                        ps, ones_row_f[:, :],
                        bvrow_f[0:1, vg * 256:(vg + 1) * 256],
                        start=False, stop=True), inc=c_mm)
                    rb = rot[ri % 2]
                    p.emit("vector", lambda e, ps=ps, rb=rb: e.tensor_copy(
                        rb[:, 0:256], ps),
                        after=((c_mm, c_mm.val), (c_kv, c_kv.val),
                               (c_vo, c_vo.val)), inc=c_ev)
                    p.dma("sync",
                          o_V[lt * 128:(lt + 1) * 128, vg * 256:(vg + 1) * 256],
                          rb[:, 0:256], c_vo, after=((c_ev, c_ev.val),))

        with nc.psum_tensor("psN", [128, 2048], F32) as psN:
            for idx, acc in enumerate((sqacc_q, sqacc_k)):
                p.emit("vector", lambda e, acc=acc: e.tensor_copy(
                    sqb[:, :], acc[:, :]),
                    after=((c_ev, c_ev.val), (c_mm, c_mm.val),
                           (c_sq, c_sq.val), (c_fin, c_fin.val)), inc=c_ev)
                for j in range(4):
                    p.emit("tensor", lambda e, j=j: e.matmul(
                        psN[0:1, j * 512:(j + 1) * 512], ones_col_bf[:, :],
                        sqb[:, j * 512:(j + 1) * 512],
                        start=True, stop=True),
                        after=((c_ev, c_ev.val), (c_fin, c_fin.val)),
                        inc=(c_mm if j == 3 else None))
                p.emit("vector", lambda e, idx=idx: e.reduce_max(
                    row16[0:1, idx:idx + 1], psN[0:1, 0:2048], axis=AXX),
                    after=((c_mm, c_mm.val),), inc=c_fin)
            p.emit("vector", lambda e: e.tensor_scalar(
                row16[0:1, 0:2], row16[0:1, 0:2], 1.05, None, ALU.mult),
                after=((c_fin, c_fin.val),), inc=c_fin)
            p.dma("sync", o_qm, row16[0:1, 0:1], c_out, after=((c_fin, c_fin.val),))
            p.dma("sync", o_km, row16[0:1, 1:2], c_out)

    if io is not None:
        io["kv_stored"] = (c_kv, c_kv.val)
        io["v_stored"] = (c_vo, c_vo.val)
    p.emit("sync", lambda e: e.nop(),
           after=((c_out, c_out.val), (c_kv, c_kv.val), (c_vo, c_vo.val)))
    if standalone:
        return p.build()
    return c_out



def build_l0b(upto="all", p=None, io=None):
    standalone = p is None
    if standalone:
        p = Prog()
    nc = p.nc
    xnT_in = _pin(p, io, "xnT", [D, T], BF16)
    qT_in = _pin(p, io, "qT", [DH, T], BF16)
    kTg = _pin(p, io, "kTg", [NCORES * DH, T], BF16)
    Vg = _pin(p, io, "Vg", [S, DH], BF16)
    qm_in = _pin(p, io, "qm", [1, 1], F32)
    kms_in = _pin(p, io, "kms", [NCORES, 1], F32)
    xT = _pin(p, io, "xT", [D, T], F32)
    WbT_in = _pin(p, io, "WbT", [D, D], F32)
    b_base = _pin(p, io, "b_base", [D], F32)
    WgtT_in = _pin(p, io, "WgtT", [D, 9], F32)
    b_thr = _pin(p, io, "b_thr", [1], F32)
    AT_in = _pin(p, io, "AT", [D, 128], F32)
    BmT_in = _pin(p, io, "BmT", [128, D], F32)
    Wo_in = _pin(p, io, "Wo", [DH, D], F32)
    bo_in = _pin(p, io, "bo", [D], F32)
    o_x2T = _pout(p, io, "x2T", [D, T], F32)

    wo_bf = p.dtmp("wo_bf", [DH, D], BF16)
    yffn_hbm = p.dtmp("yffn_hbm", [D, T], BF16)
    ot_hbm = p.dtmp("ot_hbm", [DH, T], BF16)

    ones_col_bf = p.sb("ones_col_bf", [128, 1], BF16)
    ones_row_f = p.sb("ones_row_f", [1, 512], F32)
    ones_row_bf = p.sb("ones_row_bf", [1, 512], BF16)
    identbf = p.sb("identbf", [128, 128], BF16)
    cells = p.sb("cells", [1, 16], F32)
    krow = p.sb("krow", [1, 8], F32)
    negc_col = p.sb("negc_col", [128, 1], F32)
    thrb_col = p.sb("thrb_col", [128, 1], F32)

    c_x = p.ctr("x")
    c_xs = p.ctr("xs")
    c_wb = p.ctr("wb")
    c_vb = p.ctr("vb")
    c_xe = p.ctr("xe")
    c_cast = p.ctr("cast")
    c_w = p.ctr("w")
    c_mm = p.ctr("mm")
    c_ev = p.ctr("ev")
    c_sc = p.ctr("sc")
    c_gp = p.ctr("gp")
    c_out = p.ctr("out")
    c_fin = p.ctr("fin")

    p.emit("gpsimd", lambda e: e.memset(ones_col_bf[:, :], 1.0))
    p.emit("gpsimd", lambda e: e.memset(ones_row_f[:, :], 1.0))
    p.emit("gpsimd", lambda e: e.memset(ones_row_bf[:, :], 1.0))

    p.emit("gpsimd", lambda e: e.memset(identbf[:, :], 0.0), inc=c_gp)
    p.emit("gpsimd", lambda e: e.affine_select(
        out=identbf[:, :], in_=identbf[:, :], compare_op=ALU.not_equal,
        fill=1.0, base=0, pattern=[[-1, 128]], channel_multiplier=1),
        after=((c_gp, c_gp.val),), inc=c_gp)

    p.dma("gpsimd", wo_bf[:, :], Wo_in[:, :], c_cast)
    wo_cast = c_cast.val

    shared_xn = False
    with nc.sbuf_tensor("xn", [128, NCH * T], BF16) as xn:
        for c in range(NCH):
            p.dma("sync", xn[:, c * T:(c + 1) * T],
                  xnT_in[c * 128:(c + 1) * 128, :], c_x)
        p.dma("sync", cells[0:1, 0:1], qm_in, c_x)
        p.dma("sync", krow[:, :], kms_in.rearrange("a b -> b a"), c_x)
        p.dma("sync", cells[0:1, 4:5], b_thr[None, :], c_x)
        cs_ld = c_x.val
        xn_ld = c_x.val
        xnrdy = io["xn_ready"] if shared_xn else (c_x, xn_ld)
        p.emit("vector", lambda e: e.reduce_max(cells[0:1, 1:2], krow[:, :],
                                                axis=AXX),
               after=((c_x, cs_ld),), inc=c_fin)
        p.emit("scalar", lambda e: e.activation(cells[0:1, 0:1], cells[0:1, 0:1],
                                                AF.Ln),
               after=((c_fin, c_fin.val),), inc=c_sc)
        p.emit("scalar", lambda e: e.activation(cells[0:1, 1:2], cells[0:1, 1:2],
                                                AF.Ln), inc=c_sc)
        p.emit("vector", lambda e: e.tensor_add(cells[0:1, 2:3], cells[0:1, 0:1],
                                                cells[0:1, 1:2]),
               after=((c_sc, c_sc.val),), inc=c_fin)
        p.emit("scalar", lambda e: e.activation(cells[0:1, 2:3], cells[0:1, 2:3],
                                                AF.Exp, scale=0.5),
               after=((c_fin, c_fin.val),), inc=c_sc)
        p.emit("vector", lambda e: e.tensor_scalar(cells[0:1, 3:4], cells[0:1, 2:3],
                                                   -1.0, None, ALU.mult),
               after=((c_sc, c_sc.val),), inc=c_fin)
        with nc.psum_tensor("psC", [128, 16], F32) as psC:
            p.emit("tensor", lambda e: e.matmul(psC[:, 0:1], ones_row_f[0:1, 0:128],
                                                cells[0:1, 3:4], start=True,
                                                stop=False),
                   after=((c_fin, c_fin.val), (c_x, cs_ld), (c_gp, c_gp.val)),
                   inc=c_mm)
            p.emit("tensor", lambda e: e.matmul(psC[:, 1:2], ones_row_f[0:1, 0:128],
                                                cells[0:1, 4:5], start=False,
                                                stop=True), inc=c_mm)
            p.emit("vector", lambda e: e.tensor_copy(negc_col[:, :], psC[:, 0:1]),
                   after=((c_mm, c_mm.val),), inc=c_ev)
            p.emit("vector", lambda e: e.tensor_copy(thrb_col[:, :], psC[:, 1:2]),
                   inc=c_ev)
        cshift_done = c_ev.val

        hts_scope = nc.sbuf_tensor("hts", [128, T], BF16)
        hts = hts_scope.__enter__()
        bbrow_scope = nc.sbuf_tensor("bbrow2", [1, D], BF16)
        bbrow2 = bbrow_scope.__enter__()
        bm_scope = nc.sbuf_tensor("bm_sb", [128, D], BF16)
        bm_sb = bm_scope.__enter__()
        with nc.sbuf_tensor("wgtT", [128, NCH * 16], BF16) as wgtT, \
             nc.sbuf_tensor("wT_sb", [8, T], BF16) as wT_sb, \
             nc.sbuf_tensor("s8", [8, 128], BF16) as s8, \
             nc.sbuf_tensor("s8b", [8, 128], BF16) as s8b, \
             nc.sbuf_tensor("wrep2", [128, T], BF16) as wrep2, \
             nc.sbuf_tensor("atT", [128, NCH * 128], BF16) as atT, \
             nc.sbuf_tensor("gbuf", [128, 16], F32) as gbuf, \
             nc.sbuf_tensor("gsc", [128, 8], F32) as gsc, \
             nc.sbuf_tensor("wbf_t", [128, 8], BF16) as wbf_t:

            src = WgtT_in.rearrange("(c p) f -> p c f", p=128)
            dstv = wgtT[:, 0:NCH * 16].rearrange("p (c f) -> p c f", f=16)
            p.dma("gpsimd", dstv[:, :, 0:9], src, c_cast)
            srcA = AT_in.rearrange("(c p) f -> p c f", p=128)
            dstA = atT[:, :].rearrange("p (c f) -> p c f", f=128)
            p.dma("gpsimd", dstA, srcA, c_cast)
            p.dma("gpsimd", bm_sb[:, :], BmT_in[:, :], c_cast)
            p.dma("gpsimd", bbrow2[:, :], b_base[None, :], c_cast)
            cast0 = c_cast.val

            p.emit("gpsimd", lambda e: e.memset(s8[:, :], 1.0), inc=c_gp)
            p.emit("gpsimd", lambda e: e.affine_select(
                out=s8[:, :], in_=s8[:, :], compare_op=ALU.is_ge, fill=0.0,
                base=0, pattern=[[1, 128]], channel_multiplier=-16),
                after=((c_gp, c_gp.val),), inc=c_gp)
            p.emit("gpsimd", lambda e: e.memset(s8b[:, :], 1.0), inc=c_gp)
            p.emit("gpsimd", lambda e: e.affine_select(
                out=s8b[:, :], in_=s8b[:, :], compare_op=ALU.is_ge, fill=0.0,
                base=-16, pattern=[[1, 128]], channel_multiplier=-16),
                after=((c_gp, c_gp.val),), inc=c_gp)
            p.emit("gpsimd", lambda e: e.tensor_sub(s8[:, :], s8[:, :], s8b[:, :]),
                   after=((c_gp, c_gp.val),), inc=c_gp)
            s8_done = c_gp.val

            with nc.psum_tensor("psG", [128, 1024], F32) as psG, \
                 nc.psum_tensor("psG2", [128, 2048], F32) as psG2:
                with nc.psum_tensor("psW", [8, T], BF16) as psW:
                    for lt in range(16):
                        ps = psG[:, (lt % 2) * 512:(lt % 2) * 512 + 16]
                        for c in range(NCH):
                            p.emit("tensor", lambda e, lt=lt, c=c, ps=ps: e.matmul(
                                ps[:, 0:9],
                                xn[:, c * T + lt * 128:c * T + lt * 128 + 128],
                                wgtT[:, c * 16:c * 16 + 9],
                                start=(c == 0), stop=(c == NCH - 1)),
                                after=((c_ev, c_ev.val), xnrdy,
                                       (c_cast, cast0)),
                                inc=(c_mm if c == NCH - 1 else None))
                        mm_d = c_mm.val
                        p.emit("vector", lambda e, ps=ps: e.tensor_copy(
                            gbuf[:, 0:9], ps[:, 0:9]), after=((c_mm, mm_d),),
                            inc=c_ev)
                        p.emit("vector", lambda e: e.reduce_max(
                            gsc[:, 0:1], gbuf[:, 0:8], axis=AXX, negate=True),
                            inc=c_ev)
                        p.emit("scalar", lambda e: e.activation(
                            gbuf[:, 0:8], gbuf[:, 0:8], AF.Exp, bias=gsc[:, 0:1]),
                            after=((c_ev, c_ev.val),), inc=c_sc)
                        p.emit("vector", lambda e: e.reduce_sum(
                            gsc[:, 1:2], gbuf[:, 0:8], axis=AXX),
                            after=((c_sc, c_sc.val),), inc=c_ev)
                        p.emit("vector", lambda e: e.reciprocal(
                            gsc[:, 2:3], gsc[:, 1:2]), inc=c_ev)
                        p.emit("scalar", lambda e: e.activation(
                            gsc[:, 3:4], gbuf[:, 8:9], AF.Sigmoid,
                            bias=thrb_col[:, 0:1]),
                            after=((c_ev, cshift_done),), inc=c_sc)
                        p.emit("vector", lambda e: e.tensor_scalar(
                            gsc[:, 3:4], gsc[:, 3:4], 0.125, None, ALU.mult),
                            after=((c_sc, c_sc.val),), inc=c_ev)
                        p.emit("vector", lambda e: e.tensor_scalar(
                            gbuf[:, 0:8], gbuf[:, 0:8], gsc[:, 2:3], gsc[:, 3:4],
                            ALU.mult, ALU.subtract), inc=c_ev)
                        p.emit("vector", lambda e: e.tensor_scalar(
                            gbuf[:, 0:8], gbuf[:, 0:8], 0.0, None, ALU.max),
                            inc=c_ev)
                        p.emit("vector", lambda e: e.reduce_sum(
                            gsc[:, 4:5], gbuf[:, 0:8], axis=AXX), inc=c_ev)
                        p.emit("vector", lambda e: e.tensor_scalar(
                            gsc[:, 5:6], gsc[:, 4:5], 0.0, None, ALU.is_equal),
                            inc=c_ev)
                        p.emit("vector", lambda e: e.tensor_add(
                            gsc[:, 4:5], gsc[:, 4:5], gsc[:, 5:6]), inc=c_ev)
                        p.emit("vector", lambda e: e.reciprocal(
                            gsc[:, 5:6], gsc[:, 4:5]), inc=c_ev)
                        p.emit("vector", lambda e: e.tensor_scalar(
                            wbf_t[:, 0:8], gbuf[:, 0:8], gsc[:, 5:6], None,
                            ALU.mult), inc=c_ev)
                        p.emit("tensor", lambda e, lt=lt: e.transpose(
                            psW[:, lt * 128:(lt + 1) * 128], wbf_t[:, 0:8],
                            identbf[:, :]),
                            after=((c_ev, c_ev.val),), inc=c_mm)
                        p.emit("vector", lambda e, lt=lt: e.tensor_copy(
                            wT_sb[:, lt * 128:(lt + 1) * 128],
                            psW[:, lt * 128:(lt + 1) * 128]),
                            after=((c_mm, c_mm.val),), inc=c_ev)

                for j in range(4):
                    p.emit("tensor", lambda e, j=j: e.matmul(
                        psG2[:, j * 512:(j + 1) * 512], s8[:, :],
                        wT_sb[:, j * 512:(j + 1) * 512], start=True, stop=True),
                        after=((c_ev, c_ev.val), (c_gp, s8_done)), inc=c_mm)
                p.emit("scalar", lambda e: e.activation(
                    wrep2[:, :], psG2[:, 0:T], AF.Copy, scale=2.0),
                    after=((c_mm, c_mm.val),), inc=c_sc)
                wrep_done = c_sc.val
                wrep2_keep = wrep2
                s8_keep = s8
                wT_keep = wT_sb
                wbf_keep = wbf_t
                gsc_keep = gsc
                gbuf_keep = gbuf

                for c in range(NCH):
                    for lg in range(4):
                        p.emit("tensor", lambda e, c=c, lg=lg: e.matmul(
                            psG2[:, lg * 512:(lg + 1) * 512],
                            atT[:, c * 128:(c + 1) * 128],
                            xn[:, c * T + lg * 512:c * T + (lg + 1) * 512],
                            start=(c == 0), stop=(c == NCH - 1)),
                            after=((c_cast, cast0), (c_sc, wrep_done)),
                            inc=(c_mm if (c == NCH - 1 and lg == 3) else None))
                p.emit("vector", lambda e: e.tensor_copy(hts[:, :], psG2[:, 0:T]),
                       after=((c_mm, c_mm.val),), inc=c_ev)
                p.emit("vector", lambda e: e.tensor_mul(
                    hts[:, :], hts[:, :], wrep2[:, :]), inc=c_ev)
            hw_done = c_ev.val




        with nc.sbuf_tensor("wbT0", [128, NCH * 256], BF16) as wbT0, \
             nc.sbuf_tensor("wbT1", [128, NCH * 256], BF16) as wbT1, \
             nc.sbuf_tensor("evb0", [128, 512], BF16) as evb0, \
             nc.sbuf_tensor("evb1", [128, 512], BF16) as evb1:
            wbT = [wbT0, wbT1]
            evb = [evb0, evb1]
            bbrow = bbrow2
            guard = {}
            with nc.psum_tensor("psF0", [128, 2048], F32) as psF0, \
                 nc.psum_tensor("psF1", [128, 2048], F32) as psF1:
                psF = [psF0, psF1]
                import os as _os
                n_og = int(_os.environ.get("L0B_OGS", "16"))

                def ldwb(og, aft):
                    buf2 = wbT[og % 2]
                    for h in range(2):
                        srcW = WbT_in[h * 2048:(h + 1) * 2048,
                                      og * 256:(og + 1) * 256]
                        srcW = srcW.rearrange("(c p) f -> p c f", p=128)
                        dstW = buf2[:, h * 16 * 256:
                                    (h + 1) * 16 * 256].rearrange(
                            "p (c f) -> p c f", f=256)
                        p.dma("gpsimd", dstW, srcW, c_wb, after=aft)
                        aft = ()
                if n_og and upto != "prep":
                    ldwb(0, ((c_ev, hw_done),))
                mm_at_og = {}
                for og in range(n_og if upto != "prep" else 0):
                    buf = wbT[og % 2]
                    if og + 1 < n_og:
                        ldwb(og + 1, ((c_mm, mm_at_og.get(og - 1, 0)),))
                    w_ld2 = c_wb.val
                    for ot in range(2):
                        ps = psF[ot]
                        o_glob = og * 2 + ot
                        for dc in range(NCH):
                            for lg in range(4):
                                p.emit("tensor",
                                       lambda e, buf=buf, ps=ps, ot=ot, dc=dc,
                                       lg=lg: e.matmul(
                                           ps[:, lg * 512:(lg + 1) * 512],
                                           buf[:, dc * 256 + ot * 128:
                                               dc * 256 + (ot + 1) * 128],
                                           xn[:, dc * T + lg * 512:
                                              dc * T + (lg + 1) * 512],
                                           start=(dc == 0), stop=False),
                                       after=((c_wb, w_ld2),
                                              guard.get(("p", og, ot),
                                                        (c_ev, hw_done))))
                        for lg in range(4):
                            p.emit("tensor",
                                   lambda e, ps=ps, o_glob=o_glob, lg=lg:
                                   e.matmul(
                                       ps[:, lg * 512:(lg + 1) * 512],
                                       bm_sb[:, o_glob * 128:(o_glob + 1) * 128],
                                       hts[:, lg * 512:(lg + 1) * 512],
                                       start=False, stop=False),
                                   after=((c_cast, cast0), (c_ev, hw_done)))
                            p.emit("tensor",
                                   lambda e, ps=ps, o_glob=o_glob, lg=lg:
                                   e.matmul(
                                       ps[:, lg * 512:(lg + 1) * 512],
                                       bbrow[0:1, o_glob * 128:(o_glob + 1) * 128],
                                       ones_row_bf[0:1, 0:512],
                                       start=False, stop=True),
                                   after=((c_cast, cast0), (c_gp, 1)),
                                   inc=(c_mm if lg == 3 else None))
                        eb = evb[o_glob % 2]
                        mm_d = c_mm.val
                        for lg in range(4):
                            p.emit("vector",
                                   lambda e, ps=ps, eb=eb, lg=lg: e.tensor_copy(
                                       eb[:, :], ps[:, lg * 512:(lg + 1) * 512]),
                                   after=((c_mm, mm_d), (c_out, c_out.val)),
                                   inc=c_ev)
                            p.dma("sync",
                                  yffn_hbm[o_glob * 128:(o_glob + 1) * 128,
                                           lg * 512:(lg + 1) * 512],
                                  eb[:, :], c_out, after=((c_ev, c_ev.val),))
                        guard[("p", og + 1, ot)] = (c_ev, c_ev.val)
                    mm_at_og[og] = c_mm.val
        bm_scope.__exit__(None, None, None)
        bbrow_scope.__exit__(None, None, None)
        hts_scope.__exit__(None, None, None)
    ffn_mm = c_mm.val
    ffn_out = c_out.val

    if upto == "prep":
        p.dma("gpsimd", o_x2T[0:128, :], hts[:, :], c_cast,
              after=((c_ev, c_ev.val),))
        p.dma("gpsimd", o_x2T[128:256, :], bm_sb[:, 0:T], c_cast)
        p.dma("gpsimd", o_x2T[256:384, :], wrep2_keep[:, :], c_cast)
        p.dma("gpsimd", o_x2T[384:392, 0:128], s8_keep[:, :], c_cast)
        p.dma("gpsimd", o_x2T[392:400, :], wT_keep[:, :], c_cast)
        p.dma("gpsimd", o_x2T[400:528, 0:8], wbf_keep[:, :], c_cast)
        p.dma("gpsimd", o_x2T[528:656, 0:8], gsc_keep[:, :], c_cast)
        p.dma("gpsimd", o_x2T[656:784, 0:16], gbuf_keep[:, :], c_cast)
        p.emit("sync", lambda e: e.nop(),
               after=((c_ev, c_ev.val), (c_cast, c_cast.val), (c_out, c_out.val)))
        return p.build()
    if upto == "ffn2":
        p.emit("sync", lambda e: e.nop(), after=((c_out, ffn_out),))
        return p.build()
    if upto == "ffn":
        for i in range(8):
            p.dma("gpsimd", o_x2T[i * 512:(i + 1) * 512, :],
                  yffn_hbm[i * 512:(i + 1) * 512, :], c_cast,
                  after=((c_out, ffn_out),))
        p.emit("sync", lambda e: e.nop(), after=((c_cast, c_cast.val),))
        return p.build()

    HQ = 1024
    NSB = S // 2048
    with nc.sbuf_tensor("kb0", [128, 8 * 2048], BF16) as kb0, \
         nc.sbuf_tensor("kb1", [128, 8 * 2048], BF16) as kb1, \
         nc.sbuf_tensor("vb", [128, 16 * DH], BF16) as vb, \
         nc.sbuf_tensor("pb", [128, 16 * HQ], BF16) as pb, \
         nc.sbuf_tensor("otacc", [128, 8 * HQ], F32) as otacc, \
         nc.sbuf_tensor("qh", [128, 8 * HQ], BF16) as qh, \
         nc.sbuf_tensor("denrow", [1, 2 * HQ], F32) as denrow, \
         nc.sbuf_tensor("denb", [128, HQ], F32) as denb:
        kb = [kb0, kb1]
        obf = qh
        att_guard = ((c_mm, ffn_mm), (c_out, ffn_out), (c_ev, c_ev.val))

        with nc.psum_tensor("psP0", [128, 1024], F32) as psP0, \
             nc.psum_tensor("psP1", [128, 1024], F32) as psP1, \
             nc.psum_tensor("psD", [1, 1024], F32) as psD, \
             nc.psum_tensor("psDB", [128, 1024], F32) as psDB:
            psP = [psP0, psP1]
            for half in range(2):
                q0 = half * HQ
                for dc in range(8):
                    aft = att_guard if (half == 0 and dc == 0) else \
                        (((c_ev, c_ev.val), (c_out, c_out.val)) if dc == 0 else ())
                    p.dma("scalar", qh[:, dc * HQ:(dc + 1) * HQ],
                          qT_in[dc * 128:(dc + 1) * 128, q0:q0 + HQ], c_xs,
                          after=aft)
                qh_ld = c_xs.val
                p.emit("gpsimd", lambda e: e.memset(otacc[:, :], 0.0),
                       after=((c_ev, c_ev.val),), inc=c_gp)
                p.emit("gpsimd", lambda e: e.memset(denrow[0:1, 0:HQ], 0.0),
                       inc=c_gp)
                acc_z = c_gp.val

                for sb in range(NSB):
                    k0 = sb * 2048
                    kbuf = kb[sb % 2]
                    rank, kc0 = k0 // T, k0 % T
                    for dc in range(8):
                        p.dma("scalar", kbuf[:, dc * 2048:(dc + 1) * 2048],
                              kTg[rank * DH + dc * 128:
                                  rank * DH + (dc + 1) * 128, kc0:kc0 + 2048],
                              c_xs,
                              after=(((c_mm, c_mm.val),) if dc == 0 else ()))
                    k_ld = c_xs.val
                    for kt in range(16):
                        p.dma("scalar", vb[:, kt * DH:(kt + 1) * DH],
                              Vg[k0 + kt * 128:k0 + (kt + 1) * 128, :], c_vb,
                              after=(((c_mm, c_mm.val),) if kt == 0 else ()))
                    v_ld = c_vb.val

                    for kt in range(16):
                        ps = psP[kt % 2]
                        for dc in range(8):
                            for qg in range(2):
                                p.emit("tensor",
                                       lambda e, kbuf=kbuf, ps=ps, kt=kt, dc=dc,
                                       qg=qg: e.matmul(
                                           ps[:, qg * 512:(qg + 1) * 512],
                                           kbuf[:, dc * 2048 + kt * 128:
                                                dc * 2048 + (kt + 1) * 128],
                                           qh[:, dc * HQ + qg * 512:
                                              dc * HQ + (qg + 1) * 512],
                                           start=(dc == 0), stop=(dc == 7)),
                                       after=((c_xs, k_ld),
                                              (c_sc, c_sc.val),
                                              (c_ev, c_ev.val)),
                                       inc=(c_mm if (dc == 7 and qg == 1)
                                            else None))
                        p.emit("scalar", lambda e, ps=ps, kt=kt: e.activation(
                            pb[:, kt * HQ:(kt + 1) * HQ], ps[:, 0:HQ], AF.Exp,
                            bias=negc_col[:, 0:1]),
                            after=((c_mm, c_mm.val), (c_ev, cshift_done)),
                            inc=c_sc)
                    expall = c_sc.val

                    for dv in range(8):
                        ps = psP[dv % 2]
                        for kc in range(16):
                            for qg in range(2):
                                p.emit("tensor",
                                       lambda e, ps=ps, dv=dv, kc=kc, qg=qg:
                                       e.matmul(
                                           ps[:, qg * 512:(qg + 1) * 512],
                                           vb[:, kc * DH + dv * 128:
                                              kc * DH + (dv + 1) * 128],
                                           pb[:, kc * HQ + qg * 512:
                                              kc * HQ + (qg + 1) * 512],
                                           start=(kc == 0), stop=(kc == 15)),
                                       after=((c_vb, v_ld), (c_sc, expall),
                                              (c_ev, c_ev.val)),
                                       inc=(c_mm if (kc == 15 and qg == 1)
                                            else None))
                        p.emit("vector", lambda e, ps=ps, dv=dv: e.tensor_add(
                            otacc[:, dv * HQ:(dv + 1) * HQ],
                            otacc[:, dv * HQ:(dv + 1) * HQ], ps[:, 0:HQ]),
                            after=((c_mm, c_mm.val), (c_gp, acc_z)), inc=c_ev)
                    for kc in range(16):
                        for qg in range(2):
                            p.emit("tensor", lambda e, kc=kc, qg=qg: e.matmul(
                                psD[0:1, qg * 512:(qg + 1) * 512],
                                ones_col_bf[:, :],
                                pb[:, kc * HQ + qg * 512:
                                   kc * HQ + (qg + 1) * 512],
                                start=(kc == 0), stop=(kc == 15)),
                                after=((c_sc, expall), (c_ev, c_ev.val)),
                                inc=(c_mm if (kc == 15 and qg == 1) else None))
                    p.emit("vector", lambda e: e.tensor_add(
                        denrow[0:1, 0:HQ], denrow[0:1, 0:HQ], psD[0:1, 0:HQ]),
                        after=((c_mm, c_mm.val), (c_gp, acc_z)), inc=c_ev)

                p.emit("vector", lambda e: e.reciprocal(
                    denrow[0:1, HQ:2 * HQ], denrow[0:1, 0:HQ]),
                    after=((c_ev, c_ev.val),), inc=c_ev)
                for j in range(2):
                    p.emit("tensor", lambda e, j=j: e.matmul(
                        psDB[:, j * 512:(j + 1) * 512], ones_row_f[0:1, 0:128],
                        denrow[0:1, HQ + j * 512:HQ + (j + 1) * 512],
                        start=True, stop=True),
                        after=((c_ev, c_ev.val),), inc=c_mm)
                p.emit("vector", lambda e: e.tensor_copy(denb[:, :], psDB[:, 0:HQ]),
                       after=((c_mm, c_mm.val),), inc=c_ev)
                for dv in range(8):
                    p.emit("vector", lambda e, dv=dv: e.tensor_mul(
                        obf[:, dv * HQ:(dv + 1) * HQ],
                        otacc[:, dv * HQ:(dv + 1) * HQ], denb[:, :]),
                        after=((c_ev, c_ev.val), (c_out, c_out.val)), inc=c_ev)
                    p.dma("sync", ot_hbm[dv * 128:(dv + 1) * 128, q0:q0 + HQ],
                          obf[:, dv * HQ:(dv + 1) * HQ], c_out,
                          after=((c_ev, c_ev.val),))
    att_out = c_out.val

    if upto == "att":
        p.dma("gpsimd", o_x2T[0:DH, :], ot_hbm[:, :], c_cast,
              after=((c_out, att_out),))
        p.emit("sync", lambda e: e.nop(), after=((c_cast, c_cast.val),))
        return p.build()

    with nc.sbuf_tensor("wo_sb", [128, 8 * D], BF16) as wo_sb, \
         nc.sbuf_tensor("ob_sb", [128, 8 * T], BF16) as ob_sb, \
         nc.sbuf_tensor("borow", [1, D], F32) as borow, \
         nc.sbuf_tensor("yfb0", [128, 512], BF16) as yfb0, \
         nc.sbuf_tensor("yfb1", [128, 512], BF16) as yfb1, \
         nc.sbuf_tensor("xfb0", [128, 512], F32) as xfb0, \
         nc.sbuf_tensor("xfb1", [128, 512], F32) as xfb1, \
         nc.sbuf_tensor("etmp", [128, 512], F32) as etmp, \
         nc.sbuf_tensor("exout0", [128, 512], F32) as exout0, \
         nc.sbuf_tensor("exout1", [128, 512], F32) as exout1:
        yfb = [yfb0, yfb1]
        xfb = [xfb0, xfb1]
        exout = [exout0, exout1]
        p.dma("sync", borow[:, :], bo_in[None, :], c_w, after=((c_out, att_out),))
        for dc in range(8):
            p.dma("sync", wo_sb[:, dc * D:(dc + 1) * D],
                  wo_bf[dc * 128:(dc + 1) * 128, :], c_w,
                  after=((c_cast, cast0), (c_out, att_out)))
        for dc in range(8):
            p.dma("sync", ob_sb[:, dc * T:(dc + 1) * T],
                  ot_hbm[dc * 128:(dc + 1) * 128, :], c_w,
                  after=((c_out, att_out),))
        wo_ld = c_w.val
        guard2 = {}
        ev_at_add = {}

        def ldep(j, aft):
            ct2, lg2 = j // 4, j % 4
            i2 = j % 2
            p.dma("scalar", yfb[i2][:, :],
                  yffn_hbm[ct2 * 128:(ct2 + 1) * 128,
                           lg2 * 512:(lg2 + 1) * 512], c_xe, after=aft)
            p.dma("scalar", xfb[i2][:, :],
                  xT[ct2 * 128:(ct2 + 1) * 128, lg2 * 512:(lg2 + 1) * 512],
                  c_xe, after=())
        with nc.psum_tensor("psE0", [128, 2048], F32) as psE0, \
             nc.psum_tensor("psE1", [128, 2048], F32) as psE1:
            psE = [psE0, psE1]
            for ct in range(32):
                ps = psE[ct % 2]
                for dc in range(8):
                    for lg in range(4):
                        p.emit("tensor",
                               lambda e, ps=ps, ct=ct, dc=dc, lg=lg: e.matmul(
                                   ps[:, lg * 512:(lg + 1) * 512],
                                   wo_sb[:, dc * D + ct * 128:
                                         dc * D + (ct + 1) * 128],
                                   ob_sb[:, dc * T + lg * 512:
                                         dc * T + (lg + 1) * 512],
                                   start=(dc == 0), stop=False),
                               after=((c_w, wo_ld),
                                      guard2.get(("p", ct), (c_w, wo_ld))))
                for lg in range(4):
                    p.emit("tensor", lambda e, ps=ps, ct=ct, lg=lg: e.matmul(
                        ps[:, lg * 512:(lg + 1) * 512],
                        borow[0:1, ct * 128:(ct + 1) * 128],
                        ones_row_f[0:1, 0:512], start=False, stop=True),
                        after=((c_w, wo_ld),),
                        inc=(c_mm if lg == 3 else None))
                mm_d = c_mm.val
                for lg in range(4):
                    j = ct * 4 + lg
                    i = j % 2
                    if j == 0:
                        ldep(0, ((c_out, att_out),))
                    if j + 1 < 128:
                        ldep(j + 1, ((c_ev, ev_at_add.get(j - 1, 0)),))
                    ld = c_xe.val
                    p.emit("vector", lambda e, ps=ps, lg=lg, i=i: e.tensor_add(
                        etmp[:, :], ps[:, lg * 512:(lg + 1) * 512],
                        yfb[i][:, :]),
                        after=((c_mm, mm_d), (c_xe, ld)), inc=c_ev)
                    p.emit("vector", lambda e, i=i: e.tensor_add(
                        exout[i][:, :], etmp[:, :], xfb[i][:, :]),
                        after=((c_out, c_out.val),), inc=c_ev)
                    ev_at_add[j] = c_ev.val
                    p.dma("sync", o_x2T[ct * 128:(ct + 1) * 128,
                                        lg * 512:(lg + 1) * 512],
                          exout[i][:, :], c_out, after=((c_ev, c_ev.val),))
                guard2[("p", ct + 2)] = (c_ev, c_ev.val)

    p.emit("sync", lambda e: e.nop(), after=((c_out, c_out.val),))
    if standalone:
        return p.build()
    return c_out


G4 = 4
DIM = 512
DFF = 2048
PC_I = 56


def _wm(p, ones_f32_row, psw, aft):
    n = min(512, int(ones_f32_row.shape[1]))
    p.emit("tensor", lambda e: e.matmul(
        psw[:, 0:n], ones_f32_row[0:1, 0:128], ones_f32_row[0:1, 0:n],
        start=True, stop=True), after=aft)


def _pcol(i, name, g=0):
    off = {"ln1_w": 0, "ln1_b": 4, "phz_re": 8, "phz_im": 12, "lc_re": 16,
           "lc_im": 20, "sc_lin": 24, "ln2_w": 28, "ln2_b": 32, "b2": 36,
           "b1": 40}[name]
    return i * PC_I + off + g


PCOLS = PC_I * 4 + 40
PCOL_LNLW = PC_I * 4
PCOL_LNLB = PC_I * 4 + 4
PCOL_BOUT = PC_I * 4 + 8


class L1Env:

    def __init__(self, p, with_tables=True, with_rows=True):
        nc = p.nc
        self.p = p
        self.l1p = p.sb("l1p", [128, PCOLS], F32)
        self.ones_col_bf = p.sb("ones_col_bf", [128, 1], BF16)
        self.ones_row_f = p.sb("ones_row_f", [1, 512], F32)
        self.ones_row_bf = p.sb("ones_row_bf", [1, 512], BF16)
        if with_rows:
            self.rows = p.sb("rows", [1, 3 * T], F32)
        self.cells = p.sb("cells", [1, 16], F32)
        self.colsc = p.sb("colsc", [128, 24], F32)
        self.c_x = p.ctr("x")
        self.c_cast = p.ctr("cast")
        self.c_w = p.ctr("w")
        self.c_mm = p.ctr("mm")
        self.c_ev = p.ctr("ev")
        self.c_sc = p.ctr("sc")
        self.c_gp = p.ctr("gp")
        self.c_out = p.ctr("out")
        if with_tables:
            self.jt = p.sb("jt", [128, T], F32)
            p.emit("gpsimd", lambda e: e.iota(
                self.jt[:, :], pattern=[[1, T]], base=0, channel_multiplier=0,
                allow_small_or_imprecise_dtypes=True), inc=self.c_gp)
        p.emit("gpsimd", lambda e: e.memset(self.ones_col_bf[:, :], 1.0),
               inc=self.c_gp)
        p.emit("gpsimd", lambda e: e.memset(self.ones_row_f[:, :], 1.0),
               inc=self.c_gp)
        p.emit("gpsimd", lambda e: e.memset(self.ones_row_bf[:, :], 1.0),
               inc=self.c_gp)
        self.halfpi = p.sb("halfpi", [128, 1], F32)
        p.emit("gpsimd", lambda e: e.memset(self.halfpi[:, :], PI / 2),
               inc=self.c_gp)
        self.gp0 = self.c_gp.val


def emit_main_ln(p, env, xin_f32, out_bf, o_mp=None):
    nc = p.nc
    c_x, c_ev, c_sc, c_mm, c_fin = env.c_x, env.c_ev, env.c_sc, env.c_mm, p.ctr("lnf")
    c_xb = p.ctr("lnxb")
    row_a = env.rows[0:1, 0:T]
    row_b = env.rows[0:1, T:2 * T]
    row_c = env.rows[0:1, 2 * T:3 * T]
    eps_cell = env.cells[0:1, 8:9]
    p.emit("gpsimd", lambda e: e.memset(eps_cell, EPS), inc=env.c_gp)
    gp_eps = env.c_gp.val
    with nc.sbuf_tensor("lx0", [128, T], F32) as lx0, \
         nc.sbuf_tensor("lx1", [128, T], F32) as lx1, \
         nc.sbuf_tensor("lsq", [128, T], BF16) as lsq, \
         nc.sbuf_tensor("ltmp", [128, T], F32) as ltmp, \
         nc.sbuf_tensor("lba", [128, T], F32) as lba, \
         nc.sbuf_tensor("lbb", [128, T], F32) as lbb:
        xf = [lx0, lx1]
        with nc.psum_tensor("lpsA", [128, 2048], F32) as psA, \
             nc.psum_tensor("lpsB", [128, 2048], F32) as psB:
            for c in range(2):
                p.dma("sync", xf[c % 2][:, :], xin_f32[c * 128:(c + 1) * 128, :],
                      c_x)
            for c in range(NCH):
                buf = xf[c % 2]
                xld = c_x.val
                if c + 2 < NCH:
                    p.dma("sync", xf[(c + 2) % 2][:, :],
                          xin_f32[(c + 2) * 128:(c + 3) * 128, :], c_x,
                          after=((c_xb, c + 1),))
                p.emit("vector", lambda e, c=c, buf=buf: e.tensor_copy(
                    out_bf[:, c * T:(c + 1) * T], buf[:, :]),
                    after=((c_x, xld),), inc=c_xb)
                p.emit("scalar", lambda e, c=c: e.activation(
                    lsq[:, :], out_bf[:, c * T:(c + 1) * T], AF.Square),
                    after=((c_xb, c + 1), (env.c_mm, env.c_mm.val)), inc=c_sc)
                for j in range(4):
                    p.emit("tensor", lambda e, c=c, j=j: e.matmul(
                        psA[0:1, j * 512:(j + 1) * 512], env.ones_col_bf[:, :],
                        out_bf[:, c * T + j * 512:c * T + (j + 1) * 512],
                        start=(c == 0), stop=(c == NCH - 1)),
                        after=((c_xb, c + 1),), inc=c_mm)
                    p.emit("tensor", lambda e, c=c, j=j: e.matmul(
                        psB[0:1, j * 512:(j + 1) * 512], env.ones_col_bf[:, :],
                        lsq[:, j * 512:(j + 1) * 512],
                        start=(c == 0), stop=(c == NCH - 1)),
                        after=((c_sc, c_sc.val),), inc=c_mm)
            st_all = c_mm.val
            p.emit("vector", lambda e: e.tensor_scalar(
                row_a, psA[0:1, :], 1.0 / D, None, ALU.mult),
                after=((c_mm, st_all),), inc=c_fin)
            p.emit("vector", lambda e: e.tensor_scalar(
                row_b, psB[0:1, :], 1.0 / D, None, ALU.mult), inc=c_fin)
            p.emit("scalar", lambda e: e.activation(row_c, row_a, AF.Square),
                   after=((c_fin, c_fin.val),), inc=c_sc)
            p.emit("vector", lambda e: e.tensor_sub(row_b, row_b, row_c),
                   after=((c_sc, c_sc.val),), inc=c_fin)
            p.emit("scalar", lambda e: e.activation(
                row_b, row_b, AF.Ln, bias=eps_cell),
                after=((c_fin, c_fin.val), (env.c_gp, gp_eps)), inc=c_sc)
            p.emit("scalar", lambda e: e.activation(
                row_b, row_b, AF.Exp, scale=-0.5), inc=c_sc)
            p.emit("vector", lambda e: e.tensor_mul(row_c, row_a, row_b),
                   after=((c_sc, c_sc.val),), inc=c_fin)
            p.emit("vector", lambda e: e.tensor_scalar(
                row_c, row_c, -1.0, None, ALU.mult), inc=c_fin)
        fin_done = c_fin.val
        with nc.psum_tensor("lpsA2", [128, 2048], F32) as psA2, \
             nc.psum_tensor("lpsB2", [128, 2048], F32) as psB2:
            for j in range(4):
                p.emit("tensor", lambda e, j=j: e.matmul(
                    psA2[:, j * 512:(j + 1) * 512], env.ones_row_f[0:1, 0:128],
                    row_b[0:1, j * 512:(j + 1) * 512], start=True, stop=True),
                    after=((c_fin, fin_done),), inc=c_mm)
                p.emit("tensor", lambda e, j=j: e.matmul(
                    psB2[:, j * 512:(j + 1) * 512], env.ones_row_f[0:1, 0:128],
                    row_c[0:1, j * 512:(j + 1) * 512], start=True, stop=True),
                    inc=c_mm)
            p.emit("vector", lambda e: e.tensor_copy(lba[:, :], psA2[:, :]),
                   after=((c_mm, c_mm.val),), inc=c_ev)
            p.emit("vector", lambda e: e.tensor_copy(lbb[:, :], psB2[:, :]),
                   inc=c_ev)
            bc_done = c_ev.val
            for c in range(NCH):
                p.emit("vector", lambda e, c=c: e.tensor_mul(
                    ltmp[:, :], out_bf[:, c * T:(c + 1) * T], lba[:, :]),
                    after=((c_ev, bc_done),))
                p.emit("vector", lambda e, c=c: e.tensor_add(
                    out_bf[:, c * T:(c + 1) * T], ltmp[:, :], lbb[:, :]),
                    inc=c_ev)
                if o_mp is not None:
                    p.emit("vector", lambda e, c=c: e.reduce_sum(
                        o_mp[:, c:c + 1], out_bf[:, c * T:(c + 1) * T],
                        axis=AXX), inc=c_ev)
    return c_ev.val


def emit_inner_ln(p, env, h, ybuf, wcol, bcol, silu, tag):
    nc = p.nc
    c_mm, c_ev, c_sc = env.c_mm, env.c_ev, env.c_sc
    c_fin = p.ctr(f"ilnf{tag}")
    row_a = env.rows[0:1, 0:T]
    row_b = env.rows[0:1, T:2 * T]
    row_c = env.rows[0:1, 2 * T:3 * T]
    eps_cell = env.cells[0:1, 8:9]
    p.emit("gpsimd", lambda e: e.memset(eps_cell, EPS), inc=env.c_gp)
    gp_eps = env.c_gp.val
    with nc.sbuf_tensor(f"ihbf{tag}", [128, T], BF16) as hbf, \
         nc.sbuf_tensor(f"ihsq{tag}", [128, T], BF16) as hsq, \
         nc.sbuf_tensor(f"iba{tag}", [128, T], F32) as iba, \
         nc.sbuf_tensor(f"ibb{tag}", [128, T], F32) as ibb, \
         nc.sbuf_tensor(f"itmp{tag}", [128, T], F32) as itmp:
        with nc.psum_tensor(f"ipsA{tag}", [128, 2048], F32) as psA, \
             nc.psum_tensor(f"ipsB{tag}", [128, 2048], F32) as psB:
            for g in range(G4):
                p.emit("vector", lambda e, g=g: e.tensor_copy(
                    hbf[:, :], h[:, g * T:(g + 1) * T]), inc=c_ev)
                p.emit("scalar", lambda e: e.activation(
                    hsq[:, :], hbf[:, :], AF.Square),
                    after=((c_ev, c_ev.val),), inc=c_sc)
                for j in range(4):
                    p.emit("tensor", lambda e, g=g, j=j: e.matmul(
                        psA[0:1, j * 512:(j + 1) * 512], env.ones_col_bf[:, :],
                        hbf[:, j * 512:(j + 1) * 512],
                        start=(g == 0), stop=(g == G4 - 1)),
                        after=((c_ev, c_ev.val),), inc=c_mm)
                    p.emit("tensor", lambda e, g=g, j=j: e.matmul(
                        psB[0:1, j * 512:(j + 1) * 512], env.ones_col_bf[:, :],
                        hsq[:, j * 512:(j + 1) * 512],
                        start=(g == 0), stop=(g == G4 - 1)),
                        after=((c_sc, c_sc.val),), inc=c_mm)
                p.emit("vector", lambda e: e.engine_nop(),
                       after=((c_mm, c_mm.val),), inc=c_ev)
            st_all = c_mm.val
            p.emit("vector", lambda e: e.tensor_scalar(
                row_a, psA[0:1, :], 1.0 / DIM, None, ALU.mult),
                after=((c_mm, st_all),), inc=c_fin)
            p.emit("vector", lambda e: e.tensor_scalar(
                row_b, psB[0:1, :], 1.0 / DIM, None, ALU.mult), inc=c_fin)
            p.emit("scalar", lambda e: e.activation(row_c, row_a, AF.Square),
                   after=((c_fin, c_fin.val),), inc=c_sc)
            p.emit("vector", lambda e: e.tensor_sub(row_b, row_b, row_c),
                   after=((c_sc, c_sc.val),), inc=c_fin)
            p.emit("scalar", lambda e: e.activation(
                row_b, row_b, AF.Ln, bias=eps_cell),
                after=((c_fin, c_fin.val), (env.c_gp, gp_eps)), inc=c_sc)
            p.emit("scalar", lambda e: e.activation(
                row_b, row_b, AF.Exp, scale=-0.5), inc=c_sc)
            p.emit("vector", lambda e: e.tensor_mul(row_c, row_a, row_b),
                   after=((c_sc, c_sc.val),), inc=c_fin)
            p.emit("vector", lambda e: e.tensor_scalar(
                row_c, row_c, -1.0, None, ALU.mult), inc=c_fin)
        fin_done = c_fin.val
        with nc.psum_tensor(f"ipsA2{tag}", [128, 2048], F32) as psA2, \
             nc.psum_tensor(f"ipsB2{tag}", [128, 2048], F32) as psB2:
            for j in range(4):
                p.emit("tensor", lambda e, j=j: e.matmul(
                    psA2[:, j * 512:(j + 1) * 512], env.ones_row_f[0:1, 0:128],
                    row_b[0:1, j * 512:(j + 1) * 512], start=True, stop=True),
                    after=((c_fin, fin_done),), inc=c_mm)
                p.emit("tensor", lambda e, j=j: e.matmul(
                    psB2[:, j * 512:(j + 1) * 512], env.ones_row_f[0:1, 0:128],
                    row_c[0:1, j * 512:(j + 1) * 512], start=True, stop=True),
                    inc=c_mm)
            p.emit("vector", lambda e: e.tensor_copy(iba[:, :], psA2[:, :]),
                   after=((c_mm, c_mm.val),), inc=c_ev)
            p.emit("vector", lambda e: e.tensor_copy(ibb[:, :], psB2[:, :]),
                   inc=c_ev)
            env.last_iba, env.last_ibb = iba, ibb
            for g in range(G4):
                p.emit("vector", lambda e, g=g: e.tensor_mul(
                    itmp[:, :], h[:, g * T:(g + 1) * T], iba[:, :]),
                    after=((c_ev, c_ev.val), (c_sc, c_sc.val)), inc=c_ev)
                p.emit("vector", lambda e, g=g: e.tensor_add(
                    itmp[:, :], itmp[:, :], ibb[:, :]), inc=c_ev)
                if wcol is not None:
                    p.emit("vector", lambda e, g=g: e.tensor_scalar(
                        itmp[:, :], itmp[:, :], wcol(g), bcol(g),
                        ALU.mult, ALU.add), inc=c_ev)
                if silu:
                    p.emit("scalar", lambda e, g=g: e.activation(
                        ybuf[:, g * T:(g + 1) * T], itmp[:, :], AF.Silu),
                        after=((c_ev, c_ev.val),), inc=c_sc)
                else:
                    p.emit("vector", lambda e, g=g: e.tensor_copy(
                        ybuf[:, g * T:(g + 1) * T], itmp[:, :]), inc=c_ev)
    return c_ev.val, c_sc.val


def emit_angle_tables(p, env, i, g, cth, sth, Ct, St, tag):
    c_ev, c_sc = env.c_ev, env.c_sc
    colsc = env.colsc
    pswctx = p.nps(f"pswt{tag}", [128, 512], F32)
    psw = pswctx.__enter__()
    wrdy = (env.c_gp, env.gp0)

    def W():
        _wm(p, env.ones_row_f, psw, ((c_ev, c_ev.val), wrdy))
    th = colsc[:, 8:9]
    thH = colsc[:, 9:10]
    thL = colsc[:, 10:11]
    tq = colsc[:, 11:12]
    uq = colsc[:, 12:13]
    vq = colsc[:, 13:14]
    mq = colsc[:, 14:15]
    p.emit("scalar", lambda e: e.activation(uq, cth, AF.Abs),
           after=((c_ev, c_ev.val),), inc=c_sc)
    p.emit("vector", lambda e: e.tensor_scalar(
        tq, uq, 1.0, None, ALU.add), after=((c_sc, c_sc.val),), inc=c_ev)
    p.emit("vector", lambda e: e.reciprocal(tq, tq), inc=c_ev)
    p.emit("vector", lambda e: e.tensor_mul(vq, sth, tq), inc=c_ev)
    p.emit("scalar", lambda e: e.activation(vq, vq, AF.Arctan),
           after=((c_ev, c_ev.val),), inc=c_sc)
    W()
    p.emit("vector", lambda e: e.tensor_scalar(
        mq, cth, 0.0, None, ALU.is_ge), after=((c_sc, c_sc.val),), inc=c_ev)
    p.emit("vector", lambda e: e.tensor_scalar(
        tq, mq, 2.0, -1.0, ALU.mult, ALU.add), inc=c_ev)
    p.emit("vector", lambda e: e.tensor_scalar(
        th, vq, 2.0, None, ALU.mult), inc=c_ev)
    p.emit("vector", lambda e: e.tensor_mul(th, th, tq), inc=c_ev)
    p.emit("vector", lambda e: e.tensor_scalar(
        mq, mq, -PI, PI, ALU.mult, ALU.add), inc=c_ev)
    p.emit("scalar", lambda e: e.activation(uq, sth, AF.Sign),
           after=((c_ev, c_ev.val),), inc=c_sc)
    p.emit("vector", lambda e: e.tensor_mul(mq, mq, uq),
           after=((c_sc, c_sc.val),), inc=c_ev)
    p.emit("vector", lambda e: e.tensor_add(th, th, mq), inc=c_ev)
    W()
    p.emit("vector", lambda e: e.tensor_scalar(
        thH, th, 256.0, MAGIC, ALU.mult, ALU.add), inc=c_ev)
    p.emit("vector", lambda e: e.tensor_scalar(
        thH, thH, MAGIC, 1.0 / 256.0, ALU.subtract, ALU.mult), inc=c_ev)
    p.emit("vector", lambda e: e.tensor_sub(thL, th, thH), inc=c_ev)
    W()
    p.emit("vector", lambda e: e.tensor_scalar(
        Ct[:, :], env.jt[:, :], env.rk_col, thH, ALU.add, ALU.mult),
        after=((env.c_gp, env.gp0),), inc=c_ev)
    p.emit("vector", lambda e: e.tensor_scalar(
        St[:, :], env.jt[:, :], env.rk_col, thL, ALU.add, ALU.mult), inc=c_ev)
    W()
    tm = env.tabm
    p.emit("vector", lambda e: e.tensor_scalar(
        tm[:, :], Ct[:, :], INV_2PI, None, ALU.mult), inc=c_ev)
    p.emit("vector", lambda e: e.scalar_tensor_tensor(
        tm[:, :], St[:, :], INV_2PI, tm[:, :], ALU.mult, ALU.add), inc=c_ev)
    p.emit("vector", lambda e: e.tensor_scalar(
        tm[:, :], tm[:, :], MAGIC, MAGIC, ALU.add, ALU.subtract), inc=c_ev)
    W()
    p.emit("vector", lambda e: e.scalar_tensor_tensor(
        Ct[:, :], tm[:, :], -PI2_HI, Ct[:, :], ALU.mult, ALU.add), inc=c_ev)
    p.emit("vector", lambda e: e.scalar_tensor_tensor(
        Ct[:, :], tm[:, :], -PI2_LO, Ct[:, :], ALU.mult, ALU.add), inc=c_ev)
    p.emit("vector", lambda e: e.tensor_add(Ct[:, :], Ct[:, :], St[:, :]),
           inc=c_ev)
    W()
    p.emit("vector", lambda e: e.tensor_scalar(
        Ct[:, :], Ct[:, :], -PI, PI, ALU.max, ALU.min), inc=c_ev)
    p.emit("scalar", lambda e: e.activation(St[:, :], Ct[:, :], AF.Sin),
           after=((c_ev, c_ev.val),), inc=c_sc)
    p.emit("scalar", lambda e: e.activation(tm[:, :], Ct[:, :], AF.Abs),
           inc=c_sc)
    p.emit("scalar", lambda e: e.activation(
        Ct[:, :], tm[:, :], AF.Sin, scale=-1.0, bias=env.halfpi[:, 0:1]),
        after=((env.c_gp, env.gp0),), inc=c_sc)
    _wm(p, env.ones_row_f, psw, ((c_sc, c_sc.val - 1), wrdy))
    pswctx.__exit__(None, None, None)
    return c_sc.val


def emit_chan_consts(p, env, i, g):
    c_ev, c_sc = env.c_ev, env.c_sc
    colsc = env.colsc
    pr = env.l1p[:, _pcol(i, "phz_re", g):_pcol(i, "phz_re", g) + 1]
    pi_ = env.l1p[:, _pcol(i, "phz_im", g):_pcol(i, "phz_im", g) + 1]
    r2, r, rho, lnrho = colsc[:, 0:1], colsc[:, 1:2], colsc[:, 2:3], colsc[:, 3:4]
    cth, sth, invr, ln2k = colsc[:, 4:5], colsc[:, 5:6], colsc[:, 6:7], colsc[:, 7:8]
    p.emit("vector", lambda e: e.tensor_mul(r2, pr, pr),
           after=(env.pl_wait,), inc=c_ev)
    p.emit("vector", lambda e: e.scalar_tensor_tensor(
        r2, pi_, pi_, r2, ALU.mult, ALU.add), inc=c_ev)
    p.emit("vector", lambda e: e.tensor_scalar(
        r2, r2, 1e-37, None, ALU.max), inc=c_ev)
    p.emit("scalar", lambda e: e.activation(r, r2, AF.Ln),
           after=((c_ev, c_ev.val),), inc=c_sc)
    p.emit("scalar", lambda e: e.activation(r, r, AF.Exp, scale=0.5), inc=c_sc)
    p.emit("scalar", lambda e: e.activation(rho, r, AF.Tanh), inc=c_sc)
    p.emit("scalar", lambda e: e.activation(lnrho, rho, AF.Ln), inc=c_sc)
    p.emit("vector", lambda e: e.reciprocal(invr, r),
           after=((c_sc, c_sc.val),), inc=c_ev)
    p.emit("vector", lambda e: e.tensor_mul(cth, pr, invr), inc=c_ev)
    p.emit("vector", lambda e: e.tensor_mul(sth, pi_, invr), inc=c_ev)
    p.emit("vector", lambda e: e.tensor_scalar(
        ln2k, lnrho, 2048.0, None, ALU.mult), inc=c_ev)
    return c_ev.val, c_sc.val


def emit_scan_part(p, env, i, ybuf, gre, gim, Ebuf):
    c_ev, c_sc = env.c_ev, env.c_sc
    CtL, StL = env.Ct, env.St
    pswctx = p.nps(f"pswsc{i}", [128, 512], F32)
    psws = pswctx.__enter__()

    def WS():
        _wm(p, env.ones_row_f, psws, ((c_ev, c_ev.val), (env.c_gp, env.gp0)))
    for g in range(G4):
        emit_chan_consts(p, env, i, g)
        rho = env.colsc[:, 2:3]
        emit_angle_tables(p, env, i, g, env.colsc[:, 4:5], env.colsc[:, 5:6],
                          CtL, StL, f"s{i}g{g}")
        ys = ybuf[:, g * T:(g + 1) * T]
        grs = gre[:, g * T:(g + 1) * T]
        gis = gim[:, g * T:(g + 1) * T]
        import os as _os2
        if _os2.environ.get("L1_NOROT"):
            p.emit("vector", lambda e, ys=ys, gis=gis: e.tensor_copy(gis, ys),
                   after=((c_sc, c_sc.val),), inc=c_ev)
            p.emit("vector", lambda e, ys=ys, grs=grs: e.tensor_copy(grs, ys),
                   inc=c_ev)
        else:
            p.emit("vector", lambda e, ys=ys, gis=gis: e.scalar_tensor_tensor(
                gis, StL[:, :], -1.0, ys, ALU.mult, ALU.mult),
                after=((c_sc, c_sc.val),), inc=c_ev)
            p.emit("vector", lambda e, ys=ys, grs=grs: e.tensor_mul(
                grs, ys, CtL[:, :]), inc=c_ev)
        WS()
        import os as _os
        if not _os.environ.get("L1_NOSCAN"):
            p.emit("vector", lambda e, grs=grs, rho=rho: e.tensor_tensor_scan(
                grs, rho.broadcast_to([128, T]), grs, 0.0, ALU.mult, ALU.add),
                inc=c_ev)
            WS()
            p.emit("vector", lambda e, gis=gis, rho=rho: e.tensor_tensor_scan(
                gis, rho.broadcast_to([128, T]), gis, 0.0, ALU.mult, ALU.add),
                inc=c_ev)
        p.emit("vector", lambda e, g=g, grs=grs: e.tensor_copy(
            Ebuf[:, 2 * g:2 * g + 1], grs[:, T - 1:T]), inc=c_ev)
        p.emit("vector", lambda e, g=g, gis=gis: e.tensor_copy(
            Ebuf[:, 2 * g + 1:2 * g + 2], gis[:, T - 1:T]), inc=c_ev)
        WS()
    pswctx.__exit__(None, None, None)
    return c_ev.val


def emit_combine_part(p, env, i, h, gre, gim, Eall, kvecb):
    c_ev, c_sc = env.c_ev, env.c_sc
    colsc = env.colsc
    coefs = env.coefs
    CtL, StL, tabmL, tabrL = env.Ct, env.St, env.tabm, env.tabr
    pswctx = p.nps(f"pswcb{i}", [128, 512], F32)
    pswc = pswctx.__enter__()

    def WC():
        _wm(p, env.ones_row_f, pswc, ((c_ev, c_ev.val), (env.c_gp, env.gp0)))
    for g in range(G4):
        emit_chan_consts(p, env, i, g)
        WC()
        rho, lnrho, cth, sth, ln2k = (colsc[:, 2:3], colsc[:, 3:4],
                                      colsc[:, 4:5], colsc[:, 5:6],
                                      colsc[:, 7:8])
        lcr = env.l1p[:, _pcol(i, "lc_re", g):_pcol(i, "lc_re", g) + 1]
        lci = env.l1p[:, _pcol(i, "lc_im", g):_pcol(i, "lc_im", g) + 1]
        scl = env.l1p[:, _pcol(i, "sc_lin", g):_pcol(i, "sc_lin", g) + 1]
        Gre, Gim, c0 = colsc[:, 16:17], colsc[:, 17:18], colsc[:, 18:19]
        t1 = colsc[:, 19:20]
        p.emit("vector", lambda e: e.tensor_scalar(
            coefs[:, :], kvecb[:, :], ln2k, None, ALU.mult),
            after=((env.c_x, env.p_ld), env.pl_wait), inc=c_ev)
        p.emit("scalar", lambda e: e.activation(coefs[:, :], coefs[:, :],
                                                AF.Exp),
               after=((c_ev, c_ev.val),), inc=c_sc)
        p.emit("vector", lambda e, g=g: e.tensor_mul(
            env.ecol[:, 0:8], coefs[:, :], Eall[:, 2 * g, :]),
            after=((c_sc, c_sc.val),), inc=c_ev)
        p.emit("vector", lambda e: e.reduce_sum(Gre, env.ecol[:, 0:8],
                                                axis=AXX), inc=c_ev)
        p.emit("vector", lambda e, g=g: e.tensor_mul(
            env.ecol[:, 0:8], coefs[:, :], Eall[:, 2 * g + 1, :]), inc=c_ev)
        p.emit("vector", lambda e: e.reduce_sum(Gim, env.ecol[:, 0:8],
                                                axis=AXX), inc=c_ev)
        WC()
        p.emit("vector", lambda e: e.tensor_scalar(
            c0, ln2k, env.rkn_col, None, ALU.mult), inc=c_ev)
        p.emit("scalar", lambda e: e.activation(c0, c0, AF.Exp),
               after=((c_ev, c_ev.val),), inc=c_sc)
        t2 = colsc[:, 20:21]
        p.emit("vector", lambda e, lcr=lcr: e.tensor_mul(t1, lcr, cth),
               after=((c_sc, c_sc.val),), inc=c_ev)
        p.emit("vector", lambda e, lci=lci: e.tensor_mul(t2, lci, sth),
               inc=c_ev)
        p.emit("vector", lambda e: e.tensor_sub(t1, t1, t2), inc=c_ev)
        p.emit("vector", lambda e: e.scalar_tensor_tensor(
            Gre, t1, c0, Gre, ALU.mult, ALU.add), inc=c_ev)
        p.emit("vector", lambda e, lcr=lcr: e.tensor_mul(t1, lcr, sth),
               inc=c_ev)
        p.emit("vector", lambda e, lci=lci: e.scalar_tensor_tensor(
            t1, lci, cth, t1, ALU.mult, ALU.add), inc=c_ev)
        p.emit("vector", lambda e: e.scalar_tensor_tensor(
            Gim, t1, c0, Gim, ALU.mult, ALU.add), inc=c_ev)
        emit_angle_tables(p, env, i, g, cth, sth, CtL, StL, f"c{i}g{g}")
        rp = tabrL
        p.emit("vector", lambda e: e.tensor_scalar(
            rp[:, :], env.jt[:, :], 1.0, lnrho, ALU.add, ALU.mult),
            after=((c_sc, c_sc.val),), inc=c_ev)
        p.emit("scalar", lambda e: e.activation(rp[:, :], rp[:, :], AF.Exp),
               after=((c_ev, c_ev.val),), inc=c_sc)
        grs = gre[:, g * T:(g + 1) * T]
        gis = gim[:, g * T:(g + 1) * T]
        hs = h[:, g * T:(g + 1) * T]
        p.emit("vector", lambda e, grs=grs: e.scalar_tensor_tensor(
            grs, rp[:, :], Gre, grs, ALU.mult, ALU.add),
            after=((c_sc, c_sc.val),), inc=c_ev)
        p.emit("vector", lambda e, gis=gis: e.scalar_tensor_tensor(
            gis, rp[:, :], Gim, gis, ALU.mult, ALU.add), inc=c_ev)
        WC()
        p.emit("vector", lambda e, grs=grs: e.tensor_mul(
            rp[:, :], grs, CtL[:, :]), inc=c_ev)
        p.emit("vector", lambda e, gis=gis: e.tensor_mul(
            tabmL[:, :], gis, StL[:, :]), inc=c_ev)
        WC()
        p.emit("vector", lambda e: e.tensor_sub(
            rp[:, :], rp[:, :], tabmL[:, :]), inc=c_ev)
        p.emit("vector", lambda e, hs=hs, scl=scl: e.scalar_tensor_tensor(
            hs, rp[:, :], scl, hs, ALU.mult, ALU.add), inc=c_ev)
        WC()
    pswctx.__exit__(None, None, None)
    return c_ev.val


def emit_ffn_part(p, env, i, h, W1T_in, W2T_in, gate=()):
    nc = p.nc
    c_ev, c_sc, c_mm, c_cast = env.c_ev, env.c_sc, env.c_mm, env.c_cast
    with nc.sbuf_tensor(f"fy2{i}", [128, G4 * T], BF16) as y2bf:
        wc = lambda g: env.l1p[:, _pcol(i, "ln2_w", g):_pcol(i, "ln2_w", g) + 1]
        bc = lambda g: env.l1p[:, _pcol(i, "ln2_b", g):_pcol(i, "ln2_b", g) + 1]
        emit_inner_ln(p, env, h, y2bf, wc, bc, False, f"f{i}")
        ln_done = env.c_ev.val
        ffn_scope = ExitStack()
        w1t = ffn_scope.enter_context(
            nc.sbuf_tensor(f"fw1{i}", [128, G4 * T], BF16))
        zbf = ffn_scope.enter_context(
            nc.sbuf_tensor(f"fzb{i}", [128, 16 * T], BF16))
        w2t = ffn_scope.enter_context(
            nc.sbuf_tensor(f"fw2{i}", [128, 16 * 512], BF16))
        c_w1 = p.ctr(f"w1c{i}")
        c_w2c = p.ctr(f"w2c{i}")
        ln_gate = ((c_ev, ln_done), (c_sc, c_sc.val))
        w1bf = getattr(env, "w1bf", None)
        if w1bf is not None:
            src = w1bf[i].rearrange("(c p) f -> p c f", p=128)
            dst = w1t[:, :].rearrange("p (c f) -> p c f", f=DFF)
            p.dma("scalar", dst, src, c_w1, after=ln_gate + (env.wpre,))
            src2 = env.w2bf[i].rearrange("(c p) f -> p c f", p=128)
            dst2 = w2t[:, :].rearrange("p (c f) -> p c f", f=512)
            p.dma("scalar", dst2, src2, c_w2c, after=ln_gate + (env.wpre,))
        else:
            src = W1T_in[i].rearrange("(c p) f -> p c f", p=128)
            dst = w1t[:, :].rearrange("p (c f) -> p c f", f=DFF)
            p.dma("gpsimd", dst, src, c_w1, after=ln_gate)
            src2 = W2T_in[i].rearrange("(c p) f -> p c f", p=128)
            dst2 = w2t[:, :].rearrange("p (c f) -> p c f", f=512)
            p.dma("gpsimd", dst2, src2, c_w2c, after=ln_gate)
        w1_ld = c_w1.val
        with nc.psum_tensor(f"fps0{i}", [128, 2048], F32) as ps0, \
             nc.psum_tensor(f"fps1{i}", [128, 2048], F32) as ps1:
            pps = [ps0, ps1]
            for ft in range(16):
                ps = pps[ft % 2]
                for dc in range(G4):
                    for lg in range(4):
                        p.emit("tensor", lambda e, ft=ft, dc=dc, lg=lg, ps=ps:
                               e.matmul(
                                   ps[:, lg * 512:(lg + 1) * 512],
                                   w1t[:, dc * DFF + ft * 128:
                                       dc * DFF + (ft + 1) * 128],
                                   y2bf[:, dc * T + lg * 512:
                                        dc * T + (lg + 1) * 512],
                                   start=(dc == 0), stop=(dc == G4 - 1)),
                               after=((c_w1, w1_ld), (c_ev, c_ev.val),
                                      (c_sc, c_sc.val)),
                               inc=(c_mm if (dc == G4 - 1 and lg == 3)
                                    else None))
                mm_d = c_mm.val
                b1c = env.l1p[:, _pcol(i, "b1") + ft:_pcol(i, "b1") + ft + 1]
                for lg in range(4):
                    p.emit("scalar", lambda e, ft=ft, lg=lg, ps=ps, b1c=b1c:
                           e.activation(
                               zbf[:, ft * T + lg * 512:ft * T + (lg + 1) * 512],
                               ps[:, lg * 512:(lg + 1) * 512], AF.Silu,
                               bias=b1c),
                           after=((c_mm, mm_d),), inc=c_sc)
            w2_ld = c_w2c.val
            z_done = c_sc.val
            for ct in range(G4):
                ps = pps[ct % 2]
                for fc in range(16):
                    for lg in range(4):
                        p.emit("tensor", lambda e, ct=ct, fc=fc, lg=lg, ps=ps:
                               e.matmul(
                                   ps[:, lg * 512:(lg + 1) * 512],
                                   w2t[:, fc * 512 + ct * 128:
                                       fc * 512 + (ct + 1) * 128],
                                   zbf[:, fc * T + lg * 512:
                                       fc * T + (lg + 1) * 512],
                                   start=(fc == 0), stop=(fc == 15)),
                               after=((c_w2c, w2_ld), (c_sc, z_done),
                                      (c_ev, c_ev.val)),
                               inc=(c_mm if (fc == 15 and lg == 3) else None))
                mm_d = c_mm.val
                b2c = env.l1p[:, _pcol(i, "b2", ct):_pcol(i, "b2", ct) + 1]
                for lg in range(4):
                    p.emit("vector", lambda e, ct=ct, lg=lg, ps=ps, b2c=b2c:
                           e.scalar_tensor_tensor(
                               h[:, ct * T + lg * 512:ct * T + (lg + 1) * 512],
                               ps[:, lg * 512:(lg + 1) * 512], b2c,
                               h[:, ct * T + lg * 512:ct * T + (lg + 1) * 512],
                               ALU.add, ALU.add),
                           after=((c_mm, mm_d),), inc=c_ev)
        ffn_scope.close()
    return c_ev.val


def _l1_common_inputs(p):
    ins = {}
    ins["l1p_in"] = p.din("l1p_in", [128, PCOLS], F32)
    ins["kvec"] = p.din("kvec", [1, 8], F32)
    ins["rkc"] = p.din("rkc", [1, 2], F32)
    return ins


def _l1_env_setup(p, env, ins):
    nc = p.nc
    c_pl = p.ctr("pl")
    env.krow_sb = p.sb("krow_sb2", [1, 8], F32)
    p.dma("sync", env.krow_sb[0:1, :], ins["kvec"], c_pl)
    p.dma("sync", env.l1p[:, :], ins["l1p_in"], c_pl)
    p.dma("sync", env.cells[0:1, 0:2], ins["rkc"], c_pl)
    env.pl_wait = (c_pl, c_pl.val)
    env.rkcols = p.sb("rkcols", [128, 2], F32)
    env.kvecb = p.sb("kvecb", [128, 8], F32)
    env.ecol = p.sb("ecol", [128, 8], F32)
    env.coefs = p.sb("coefs", [128, 8], F32)
    with nc.psum_tensor("psRK", [128, 16], F32) as psRK:
        p.emit("tensor", lambda e: e.matmul(
            psRK[:, 0:2], env.ones_row_f[0:1, 0:128], env.cells[0:1, 0:2],
            start=True, stop=False),
            after=(env.pl_wait, (env.c_gp, env.gp0)), inc=env.c_mm)
        p.emit("tensor", lambda e: e.matmul(
            psRK[:, 2:10], env.ones_row_f[0:1, 0:128],
            env.krow_sb[0:1, 0:8], start=False, stop=True),
            inc=env.c_mm)
        p.emit("vector", lambda e: e.tensor_copy(env.rkcols[:, :], psRK[:, 0:2]),
               after=((env.c_mm, env.c_mm.val),), inc=env.c_ev)
        p.emit("vector", lambda e: e.tensor_copy(env.kvecb[:, :], psRK[:, 2:10]),
               inc=env.c_ev)
    env.rk_col = env.rkcols[:, 0:1]
    env.rkn_col = env.rkcols[:, 1:2]
    env.setup_ev = env.c_ev.val


def build_l1a(p=None, io=None, env=None):
    standalone = p is None
    if standalone:
        p = Prog()
    nc = p.nc
    x2T = _pin(p, io, "x2T", [D, T], F32)
    WinT_in = _pin(p, io, "WinT", [D, DIM], F32)
    bin_in = _pin(p, io, "bin", [DIM], F32)
    if io and "l1p_in" in io:
        ins = {k: io[k] for k in ("l1p_in", "kvec", "rkc")}
    else:
        ins = _l1_common_inputs(p)
    o_xn2T = _pout(p, io, "xn2T", [D, T], BF16)
    o_mp = _pout(p, io, "mp", [128, NCH], F32)
    if not (io and "h_sb" in io):
        o_hT = _pout(p, io, "hT", [DIM, T], F32)
        o_gre = _pout(p, io, "gre", [DIM, T], F32)
    o_gim = _pout(p, io, "gim", [DIM, T], F32)
    o_E = _pout(p, io, "E", [128, 8], F32)

    if env is None:
        env = L1Env(p)
        _l1_env_setup(p, env, ins)
    res_sb = bool(io) and "h_sb" in io
    mp_sb = p.sb("mp_sb", [128, NCH], F32)
    if res_sb:
        h, gre = io["h_sb"], io["gre_sb"]
        gim = p.sb("gim_s", [128, G4 * T], F32)
    else:
        h = p.sb("h", [128, G4 * T], F32)
        gre = p.sb("gre_s", [128, G4 * T], F32)
        gim = p.sb("gim_s", [128, G4 * T], F32)
    Ebuf = p.sb("Ebuf", [128, 8], F32)

    emit_main_ln_stream(p, env, x2T, o_xn2T, mp_sb)
    p.dma("sync", o_mp, mp_sb[:, :], env.c_out,
          after=((env.c_ev, env.c_ev.val),))
    ln_out = env.c_out.val

    with nc.sbuf_tensor("wint", [128, NCH * 512], BF16) as wint, \
         nc.sbuf_tensor("binr", [1, DIM], BF16) as binr, \
         nc.sbuf_tensor("xnb0", [128, T], BF16) as xnb0, \
         nc.sbuf_tensor("xnb1", [128, T], BF16) as xnb1:
        xnb = [xnb0, xnb1]
        src = WinT_in.rearrange("(c p) f -> p c f", p=128)
        dst = wint[:, :].rearrange("p (c f) -> p c f", f=DIM)
        p.dma("gpsimd", dst, src, env.c_cast,
              after=((env.c_out, ln_out), (env.c_ev, env.c_ev.val)))
        p.dma("gpsimd", binr[0:1, :], bin_in[None, :], env.c_cast)
        w_ld = env.c_cast.val
        c_xn2 = p.ctr("xn2st")
        mm_at = {}
        with nc.psum_tensor("psH0", [128, 2048], F32) as psH0, \
             nc.psum_tensor("psH1", [128, 2048], F32) as psH1:
            pps = [psH0, psH1]
            for dt_ in range(G4):
                ps = pps[dt_ % 2]
                for vc in range(NCH):
                    k = dt_ * NCH + vc
                    if k == 0:
                        p.dma("scalar", xnb[0][:, :], o_xn2T[0:128, :],
                              c_xn2, after=((env.c_out, ln_out),))
                    if k + 1 < G4 * NCH:
                        vc2 = (k + 1) % NCH
                        p.dma("scalar", xnb[(k + 1) % 2][:, :],
                              o_xn2T[vc2 * 128:(vc2 + 1) * 128, :], c_xn2,
                              after=((env.c_mm, mm_at.get(k - 1, 0)),))
                    x_ld = c_xn2.val
                    for lg in range(4):
                        p.emit("tensor",
                               lambda e, dt_=dt_, vc=vc, lg=lg, ps=ps, k=k:
                               e.matmul(
                                   ps[:, lg * 512:(lg + 1) * 512],
                                   wint[:, vc * 512 + dt_ * 128:
                                        vc * 512 + (dt_ + 1) * 128],
                                   xnb[k % 2][:, lg * 512:(lg + 1) * 512],
                                   start=(vc == 0), stop=False),
                               after=((env.c_cast, w_ld), (c_xn2, x_ld),
                                      (env.c_ev, env.c_ev.val)),
                               inc=(env.c_mm if lg == 3 else None))
                    mm_at[k] = env.c_mm.val
                for lg in range(4):
                    p.emit("tensor", lambda e, dt_=dt_, lg=lg, ps=ps:
                           e.matmul(
                               ps[:, lg * 512:(lg + 1) * 512],
                               binr[0:1, dt_ * 128:(dt_ + 1) * 128],
                               env.ones_row_bf[0:1, 0:512],
                               start=False, stop=True),
                           after=((env.c_cast, w_ld), (env.c_gp, env.gp0)),
                           inc=(env.c_mm if lg == 3 else None))
                mm_d = env.c_mm.val
                mm_at[dt_ * NCH + NCH - 1] = mm_d
                for lg in range(4):
                    p.emit("vector", lambda e, dt_=dt_, lg=lg, ps=ps:
                           e.tensor_copy(
                               h[:, dt_ * T + lg * 512:dt_ * T + (lg + 1) * 512],
                               ps[:, lg * 512:(lg + 1) * 512]),
                           after=((env.c_mm, mm_d),), inc=env.c_ev)
    import os as _os3
    l1a_upto = _os3.environ.get("L1A_UPTO", "all")
    with nc.sbuf_tensor("Ct", [128, T], F32) as Ct, \
         nc.sbuf_tensor("St", [128, T], F32) as St, \
         nc.sbuf_tensor("tabm", [128, T], F32) as tabm:
        env.Ct, env.St, env.tabm = Ct, St, tabm
        ybuf = gre
        if l1a_upto in ("ln1", "scan", "all"):
            wc = lambda g: env.l1p[:, _pcol(0, "ln1_w", g):
                                   _pcol(0, "ln1_w", g) + 1]
            bc = lambda g: env.l1p[:, _pcol(0, "ln1_b", g):
                                   _pcol(0, "ln1_b", g) + 1]
            emit_inner_ln(p, env, h, ybuf, wc, bc, True, "a0")
        if l1a_upto in ("scan", "all"):
            emit_scan_part(p, env, 0, ybuf, gre, gim, Ebuf)
        done = ((env.c_ev, env.c_ev.val), (env.c_sc, env.c_sc.val))
        for g in range(G4):
            if not res_sb:
                p.dma("sync", o_hT[g * 128:(g + 1) * 128, :],
                      h[:, g * T:(g + 1) * T], env.c_out, after=done)
                p.dma("sync", o_gre[g * 128:(g + 1) * 128, :],
                      gre[:, g * T:(g + 1) * T], env.c_out, after=done)
            p.dma("sync", o_gim[g * 128:(g + 1) * 128, :],
                  gim[:, g * T:(g + 1) * T], env.c_out, after=done)
        p.dma("sync", o_E, Ebuf[:, :], env.c_out, after=done)
    p.emit("sync", lambda e: e.nop(), after=((env.c_out, env.c_out.val),))
    if standalone:
        return p.build()
    return env.c_out


def build_l1s(i, last, p=None, io=None, env=None):
    standalone = p is None
    if standalone:
        p = Prog()
    nc = p.nc
    res_sb = bool(io) and "h_sb" in io
    if not res_sb:
        hT_in = _pin(p, io, "hT", [DIM, T], F32)
        gre_in = _pin(p, io, "gre", [DIM, T], F32)
    gim_in = _pin(p, io, "gim", [DIM, T], F32)
    eall_ag = bool(io) and "Eall_ag" in io
    if eall_ag:
        Eall_in = io["Eall_ag"]
    else:
        Eall_in = _pin(p, io, "Eall", [128, 8, 8], F32)
    W1T_in = _pin(p, io, "W1T", [4, DIM, DFF], F32)
    W2T_in = _pin(p, io, "W2T", [4, DFF, DIM], F32)
    if not (io and "l1p_in" in io) and env is None:
        ins = _l1_common_inputs(p)
    if last:
        WoutT_in = _pin(p, io, "WoutT", [DIM, D], F32)
        o_ya = _pout(p, io, "yaT", [D, T], BF16)
    else:
        if not res_sb:
            o_hT = _pout(p, io, "hT_o", [DIM, T], F32)
            o_gre = _pout(p, io, "gre_o", [DIM, T], F32)
        o_gim = _pout(p, io, "gim_o", [DIM, T], F32)
        o_E = _pout(p, io, "E", [128, 8], F32)

    if env is None:
        env = L1Env(p)
        _l1_env_setup(p, env, ins)
    if res_sb:
        h = io["h_sb"]
    else:
        h = p.sb("h", [128, G4 * T], F32)
        for g in range(G4):
            p.dma("sync", h[:, g * T:(g + 1) * T],
                  hT_in[g * 128:(g + 1) * 128, :], env.c_x)
    h_ld = env.c_x.val

    from contextlib import nullcontext as _nullctx2
    with (_nullctx2(io["gre_sb"]) if res_sb
          else p.nsb("gre1", [128, G4 * T], F32)) as gre1, \
         p.nsb("gim1", [128, G4 * T], F32) as gim1, \
         p.nsb("Eall_sb", [128, 8 * 8], F32) as Eall, \
         p.nsb("Ct1", [128, T], F32) as Ct1, \
         p.nsb("St1", [128, T], F32) as St1, \
         p.nsb("tabm1", [128, T], F32) as tabm1, \
         p.nsb("tabr1", [128, T], F32) as tabr1:
        env.Ct, env.St, env.tabm, env.tabr = Ct1, St1, tabm1, tabr1
        for g in range(G4):
            if not res_sb:
                p.dma("sync", gre1[:, g * T:(g + 1) * T],
                      gre_in[g * 128:(g + 1) * 128, :], env.c_x)
            p.dma("sync", gim1[:, g * T:(g + 1) * T],
                  gim_in[g * 128:(g + 1) * 128, :], env.c_x)
        if eall_ag:
            p.dma("sync",
                  Eall[:, :].rearrange("p (m a) -> p m a", a=8),
                  Eall_in.rearrange("(m p) a -> p m a", p=128), env.c_x)
            EallV = Eall.rearrange("p (m a) -> p a m", a=8)
        else:
            p.dma("sync", Eall[:, :], Eall_in.rearrange("p a b -> p (a b)"),
                  env.c_x)
            EallV = Eall.rearrange("p (a b) -> p a b", b=8)
        env.p_ld = env.c_x.val
        emit_combine_part(p, env, i, h, gre1, gim1, EallV, env.kvecb)
    comb_done = ((env.c_ev, env.c_ev.val), (env.c_sc, env.c_sc.val))

    import os as _os4
    if not _os4.environ.get("L1S_NOFFN"):
        emit_ffn_part(p, env, i, h, W1T_in, W2T_in, gate=comb_done)

    if not last:
        with (_nullctx2(io["gre_sb"]) if res_sb
              else p.nsb("gre2", [128, G4 * T], F32)) as gre2, \
             p.nsb("gim2", [128, G4 * T], F32) as gim2, \
             p.nsb("Ebuf", [128, 8], F32) as Ebuf, \
             p.nsb("Ct2", [128, T], F32) as Ct2, \
             p.nsb("St2", [128, T], F32) as St2, \
             p.nsb("tabm2", [128, T], F32) as tabm2:
            env.Ct, env.St, env.tabm = Ct2, St2, tabm2
            ybuf = gre2
            wc = lambda g: env.l1p[:, _pcol(i + 1, "ln1_w", g):
                                   _pcol(i + 1, "ln1_w", g) + 1]
            bc = lambda g: env.l1p[:, _pcol(i + 1, "ln1_b", g):
                                   _pcol(i + 1, "ln1_b", g) + 1]
            emit_inner_ln(p, env, h, ybuf, wc, bc, True, f"s{i}")
            emit_scan_part(p, env, i + 1, ybuf, gre2, gim2, Ebuf)
            done = ((env.c_ev, env.c_ev.val), (env.c_sc, env.c_sc.val))
            for g in range(G4):
                if not res_sb:
                    p.dma("sync", o_hT[g * 128:(g + 1) * 128, :],
                          h[:, g * T:(g + 1) * T], env.c_out, after=done)
                    p.dma("sync", o_gre[g * 128:(g + 1) * 128, :],
                          gre2[:, g * T:(g + 1) * T], env.c_out, after=done)
                p.dma("sync", o_gim[g * 128:(g + 1) * 128, :],
                      gim2[:, g * T:(g + 1) * T], env.c_out, after=done)
            p.dma("sync", o_E, Ebuf[:, :], env.c_out, after=done)
    else:
        with nc.sbuf_tensor("hFbf", [128, G4 * T], BF16) as hFbf:
            wc = lambda g: env.l1p[:, PCOL_LNLW + g:PCOL_LNLW + g + 1]
            bc = lambda g: env.l1p[:, PCOL_LNLB + g:PCOL_LNLB + g + 1]
            emit_inner_ln(p, env, h, hFbf, wc, bc, False, "fin")
            ln_done = env.c_ev.val
            with nc.sbuf_tensor("wout_sb", [128, G4 * D], BF16) as wout_sb, \
                 nc.sbuf_tensor("yev0", [128, 512], BF16) as yev0, \
                 nc.sbuf_tensor("yev1", [128, 512], BF16) as yev1:
                yev = [yev0, yev1]
                src = WoutT_in.rearrange("(c p) f -> p c f", p=128)
                dst = wout_sb[:, :].rearrange("p (c f) -> p c f", f=D)
                c_woc = p.ctr("woc")
                p.dma("gpsimd", dst, src, c_woc,
                      after=((env.c_ev, ln_done), (env.c_sc, env.c_sc.val)))
                w_ld = c_woc.val
                with nc.psum_tensor("psY0", [128, 2048], F32) as psY0, \
                     nc.psum_tensor("psY1", [128, 2048], F32) as psY1:
                    pps = [psY0, psY1]
                    ev_vt = {}
                    for vt in range(32):
                        ps = pps[vt % 2]
                        for dc in range(G4):
                            for lg in range(4):
                                p.emit("tensor",
                                       lambda e, vt=vt, dc=dc, lg=lg, ps=ps:
                                       e.matmul(
                                           ps[:, lg * 512:(lg + 1) * 512],
                                           wout_sb[:, dc * D + vt * 128:
                                                   dc * D + (vt + 1) * 128],
                                           hFbf[:, dc * T + lg * 512:
                                                dc * T + (lg + 1) * 512],
                                           start=(dc == 0),
                                           stop=(dc == G4 - 1)),
                                       after=((c_woc, w_ld),
                                              (env.c_ev,
                                               ev_vt.get(vt - 2, ln_done)),
                                              (env.c_sc, env.c_sc.val)),
                                       inc=(env.c_mm
                                            if (dc == G4 - 1 and lg == 3)
                                            else None))
                        mm_d = env.c_mm.val
                        bcol = env.l1p[:, PCOL_BOUT + vt:PCOL_BOUT + vt + 1]
                        for lg in range(4):
                            yv = yev[(vt * 4 + lg) % 2]
                            p.emit("vector",
                                   lambda e, lg=lg, ps=ps, bcol=bcol, yv=yv:
                                   e.tensor_scalar(
                                       yv[:, :],
                                       ps[:, lg * 512:(lg + 1) * 512],
                                       bcol, None, ALU.add),
                                   after=((env.c_mm, mm_d),
                                          (env.c_out, env.c_out.val)),
                                   inc=env.c_ev)
                            p.dma("sync",
                                  o_ya[vt * 128:(vt + 1) * 128,
                                       lg * 512:(lg + 1) * 512],
                                  yv[:, :], env.c_out,
                                  after=((env.c_ev, env.c_ev.val),))
                        ev_vt[vt] = env.c_ev.val
    p.emit("sync", lambda e: e.nop(), after=((env.c_out, env.c_out.val),))
    if standalone:
        return p.build()
    return env.c_out


def emit_main_ln_stream(p, env, xin_f32, o_xnT_dram, o_mp_sb):
    nc = p.nc
    c_x, c_ev, c_sc, c_mm = env.c_x, env.c_ev, env.c_sc, env.c_mm
    c_fin = p.ctr("slnf")
    c_xb = p.ctr("slnxb")
    row_a = env.rows[0:1, 0:T]
    row_b = env.rows[0:1, T:2 * T]
    row_c = env.rows[0:1, 2 * T:3 * T]
    eps_cell = env.cells[0:1, 8:9]
    p.emit("gpsimd", lambda e: e.memset(eps_cell, EPS), inc=env.c_gp)
    gp_eps = env.c_gp.val
    with nc.sbuf_tensor("sx0", [128, T], F32) as sx0, \
         nc.sbuf_tensor("sx1", [128, T], F32) as sx1, \
         nc.sbuf_tensor("sbf", [128, T], BF16) as sbf, \
         nc.sbuf_tensor("ssq", [128, T], BF16) as ssq, \
         nc.sbuf_tensor("so0", [128, T], BF16) as so0, \
         nc.sbuf_tensor("so1", [128, T], BF16) as so1, \
         nc.sbuf_tensor("stmp", [128, T], F32) as stmp, \
         nc.sbuf_tensor("sba", [128, T], F32) as sba, \
         nc.sbuf_tensor("sbb", [128, T], F32) as sbb:
        xf = [sx0, sx1]
        so = [so0, so1]
        with nc.psum_tensor("spsA", [128, 2048], F32) as psA, \
             nc.psum_tensor("spsB", [128, 2048], F32) as psB:
            for c in range(2):
                p.dma("sync", xf[c % 2][:, :],
                      xin_f32[c * 128:(c + 1) * 128, :], c_x)
            for c in range(NCH):
                buf = xf[c % 2]
                xld = c_x.val
                if c + 2 < NCH:
                    p.dma("sync", xf[(c + 2) % 2][:, :],
                          xin_f32[(c + 2) * 128:(c + 3) * 128, :], c_x,
                          after=((c_xb, c + 1),))
                p.emit("vector", lambda e, buf=buf: e.tensor_copy(
                    sbf[:, :], buf[:, :]),
                    after=((c_x, xld), (c_mm, c_mm.val)), inc=c_xb)
                p.emit("scalar", lambda e: e.activation(
                    ssq[:, :], sbf[:, :], AF.Square),
                    after=((c_xb, c + 1), (c_mm, c_mm.val)), inc=c_sc)
                for j in range(4):
                    p.emit("tensor", lambda e, c=c, j=j: e.matmul(
                        psA[0:1, j * 512:(j + 1) * 512], env.ones_col_bf[:, :],
                        sbf[:, j * 512:(j + 1) * 512],
                        start=(c == 0), stop=(c == NCH - 1)),
                        after=((c_xb, c + 1),), inc=c_mm)
                    p.emit("tensor", lambda e, c=c, j=j: e.matmul(
                        psB[0:1, j * 512:(j + 1) * 512], env.ones_col_bf[:, :],
                        ssq[:, j * 512:(j + 1) * 512],
                        start=(c == 0), stop=(c == NCH - 1)),
                        after=((c_sc, c_sc.val),), inc=c_mm)
            st_all = c_mm.val
            p.emit("vector", lambda e: e.tensor_scalar(
                row_a, psA[0:1, :], 1.0 / D, None, ALU.mult),
                after=((c_mm, st_all),), inc=c_fin)
            p.emit("vector", lambda e: e.tensor_scalar(
                row_b, psB[0:1, :], 1.0 / D, None, ALU.mult), inc=c_fin)
            p.emit("scalar", lambda e: e.activation(row_c, row_a, AF.Square),
                   after=((c_fin, c_fin.val),), inc=c_sc)
            p.emit("vector", lambda e: e.tensor_sub(row_b, row_b, row_c),
                   after=((c_sc, c_sc.val),), inc=c_fin)
            p.emit("scalar", lambda e: e.activation(
                row_b, row_b, AF.Ln, bias=eps_cell),
                after=((c_fin, c_fin.val), (env.c_gp, gp_eps)), inc=c_sc)
            p.emit("scalar", lambda e: e.activation(
                row_b, row_b, AF.Exp, scale=-0.5), inc=c_sc)
            p.emit("vector", lambda e: e.tensor_mul(row_c, row_a, row_b),
                   after=((c_sc, c_sc.val),), inc=c_fin)
            p.emit("vector", lambda e: e.tensor_scalar(
                row_c, row_c, -1.0, None, ALU.mult), inc=c_fin)
        fin_done = c_fin.val
        with nc.psum_tensor("spsA2", [128, 2048], F32) as psA2, \
             nc.psum_tensor("spsB2", [128, 2048], F32) as psB2:
            for j in range(4):
                p.emit("tensor", lambda e, j=j: e.matmul(
                    psA2[:, j * 512:(j + 1) * 512], env.ones_row_f[0:1, 0:128],
                    row_b[0:1, j * 512:(j + 1) * 512], start=True, stop=True),
                    after=((c_fin, fin_done),), inc=c_mm)
                p.emit("tensor", lambda e, j=j: e.matmul(
                    psB2[:, j * 512:(j + 1) * 512], env.ones_row_f[0:1, 0:128],
                    row_c[0:1, j * 512:(j + 1) * 512], start=True, stop=True),
                    inc=c_mm)
            p.emit("vector", lambda e: e.tensor_copy(sba[:, :], psA2[:, :]),
                   after=((c_mm, c_mm.val),), inc=c_ev)
            p.emit("vector", lambda e: e.tensor_copy(sbb[:, :], psB2[:, :]),
                   inc=c_ev)
        c_x2 = p.ctr("slnx2")
        c_xb2 = p.ctr("slnxb2")
        with p.nps("psw_sln", [128, 512], F32) as psw:
            for c in range(2):
                p.dma("scalar", xf[c % 2][:, :],
                      xin_f32[c * 128:(c + 1) * 128, :], c_x2,
                      after=((c_fin, fin_done),))
            for c in range(NCH):
                buf = xf[c % 2]
                ob = so[c % 2]
                xld = c_x2.val
                if c + 2 < NCH:
                    p.dma("scalar", xf[(c + 2) % 2][:, :],
                          xin_f32[(c + 2) * 128:(c + 3) * 128, :], c_x2,
                          after=((c_xb2, c + 1),))
                p.emit("vector", lambda e, buf=buf: e.tensor_mul(
                    stmp[:, :], buf[:, :], sba[:, :]),
                    after=((c_x2, xld), (c_ev, c_ev.val),
                           (env.c_out, env.c_out.val)), inc=c_xb2)
                _wm(p, env.ones_row_f, psw,
                    ((c_xb2, c + 1), (env.c_gp, env.gp0)))
                p.emit("vector", lambda e, ob=ob: e.tensor_add(
                    ob[:, :], stmp[:, :], sbb[:, :]), inc=c_ev)
                p.emit("vector", lambda e, c=c, ob=ob: e.reduce_sum(
                    o_mp_sb[:, c:c + 1], ob[:, :], axis=AXX), inc=c_ev)
                _wm(p, env.ones_row_f, psw,
                    ((c_ev, c_ev.val), (env.c_gp, env.gp0)))
                p.dma("sync", o_xnT_dram[c * 128:(c + 1) * 128, :], ob[:, :],
                      env.c_out, after=((c_ev, c_ev.val),))
    return env.c_out.val


def build_l1f(p=None, io=None, env=None):
    standalone = p is None
    if standalone:
        p = Prog()
    nc = p.nc
    x2T = _pin(p, io, "x2T", [D, T], F32)
    yaT = _pin(p, io, "yaT", [D, T], BF16)
    xn2T = _pin(p, io, "xn2T", [D, T], BF16)
    mps_ag = bool(io) and "mps_ag" in io
    if mps_ag:
        mps = io["mps_ag"]
    else:
        mps = _pin(p, io, "mps", [128, NCH, 8], F32)
    WorgT_in = _pin(p, io, "WorgT", [D, D], F32)
    downT_in = _pin(p, io, "downT", [D, 512], F32)
    upF_in = _pin(p, io, "upF", [512, D], F32)
    keysT_in = _pin(p, io, "keysT", [128, 8], F32)
    WqT_in = _pin(p, io, "WqT", [D, 128], F32)
    o_out = _pout(p, io, "outT", [D, T], F32)

    if env is None:
        env = L1Env(p, with_tables=False, with_rows=False)
    c_x, c_cast, c_mm, c_ev, c_sc = (env.c_x, env.c_cast, env.c_mm, env.c_ev,
                                     env.c_sc)
    c_out, c_gp = env.c_out, env.c_gp
    eps_cell = env.cells[0:1, 8:9]

    xn2 = p.sb("xn2f", [128, NCH * T], BF16)
    for c in range(NCH):
        p.dma("sync", xn2[:, c * T:(c + 1) * T],
              xn2T[c * 128:(c + 1) * 128, :], c_x)

    meanc = p.sb("meanc", [128, NCH], F32)
    meanbf = p.sb("meanbf", [128, NCH], BF16)
    qbf = p.sb("qbf", [128, 1], BF16)
    crow = p.sb("crow", [1, 8], F32)
    crow_bf = p.sb("crow_bf", [1, 8], BF16)
    ccol_bf = p.sb("ccol_bf", [8, 1], BF16)
    coefrep = p.sb("coefrep", [128, 4], F32)
    s64 = p.sb("s64", [8, 512], BF16)
    s64b = p.sb("s64b", [8, 512], BF16)
    keys_sb = p.sb("keys_sb", [128, 8], BF16)
    identbf2 = p.sb("identbf2", [128, 128], BF16)
    scol = p.sb("scol", [8, 1], F32)
    with nc.sbuf_tensor("mps_sb", [128, NCH * 8], F32) as mps_sb, \
         nc.sbuf_tensor("wqt", [128, NCH * 128], BF16) as wqt:
        if mps_ag:
            p.dma("sync", mps_sb[:, :].rearrange("p (m a) -> p m a", a=NCH),
                  mps.rearrange("(m p) a -> p m a", p=128), c_x)
            mpv = mps_sb.rearrange("p (m a) -> p a m", a=NCH)
        else:
            p.dma("sync", mps_sb[:, :], mps.rearrange("p a b -> p (a b)"),
                  c_x)
            mpv = mps_sb.rearrange("p (a b) -> p a b", b=8)
        xn_ld = c_x.val
        p.dma("gpsimd", keys_sb[:, :], keysT_in, c_cast)
        src = WqT_in.rearrange("(c p) f -> p c f", p=128)
        dst = wqt[:, :].rearrange("p (c f) -> p c f", f=128)
        p.dma("gpsimd", dst, src, c_cast)
        mq_ld, mc_ld = c_x.val, c_cast.val
        p.emit("gpsimd", lambda e: e.memset(identbf2[:, :], 0.0), inc=c_gp)
        p.emit("gpsimd", lambda e: e.affine_select(
            out=identbf2[:, :], in_=identbf2[:, :], compare_op=ALU.not_equal,
            fill=1.0, base=0, pattern=[[-1, 128]], channel_multiplier=1),
            after=((c_gp, c_gp.val),), inc=c_gp)
        p.emit("gpsimd", lambda e: e.memset(s64[:, :], 1.0), inc=c_gp)
        p.emit("gpsimd", lambda e: e.affine_select(
            out=s64[:, :], in_=s64[:, :], compare_op=ALU.is_ge, fill=0.0,
            base=0, pattern=[[1, 512]], channel_multiplier=-64),
            after=((c_gp, c_gp.val),), inc=c_gp)
        p.emit("gpsimd", lambda e: e.memset(s64b[:, :], 1.0), inc=c_gp)
        p.emit("gpsimd", lambda e: e.affine_select(
            out=s64b[:, :], in_=s64b[:, :], compare_op=ALU.is_ge, fill=0.0,
            base=-64, pattern=[[1, 512]], channel_multiplier=-64),
            after=((c_gp, c_gp.val),), inc=c_gp)
        p.emit("gpsimd", lambda e: e.tensor_sub(s64[:, :], s64[:, :],
                                                s64b[:, :]),
               after=((c_gp, c_gp.val),), inc=c_gp)
        gp_s = c_gp.val
        for c in range(NCH):
            p.emit("vector", lambda e, c=c: e.reduce_sum(
                meanc[:, c:c + 1], mpv[:, c, :], axis=AXX),
                after=((c_x, mq_ld),), inc=c_ev)
        p.emit("vector", lambda e: e.tensor_scalar(
            meanc[:, :], meanc[:, :], 1.0 / S, None, ALU.mult), inc=c_ev)
        p.emit("vector", lambda e: e.tensor_copy(meanbf[:, :], meanc[:, :]),
               inc=c_ev)
        mdone = c_ev.val
        with nc.psum_tensor("psQl", [128, 16], F32) as psQl, \
             nc.psum_tensor("psSc", [8, 16], F32) as psSc, \
             nc.psum_tensor("psCc", [8, 16], BF16) as psCc, \
             nc.psum_tensor("psRp", [128, 16], F32) as psRp:
            for vc in range(NCH):
                p.emit("tensor", lambda e, vc=vc: e.matmul(
                    psQl[:, 0:1], wqt[:, vc * 128:(vc + 1) * 128],
                    meanbf[:, vc:vc + 1],
                    start=(vc == 0), stop=(vc == NCH - 1)),
                    after=((c_cast, mc_ld), (c_ev, mdone)),
                    inc=(c_mm if vc == NCH - 1 else None))
            p.emit("vector", lambda e: e.tensor_copy(qbf[:, :], psQl[:, 0:1]),
                   after=((c_mm, c_mm.val),), inc=c_ev)
            p.emit("tensor", lambda e: e.matmul(
                psSc[:, 0:1], keys_sb[:, :], qbf[:, :], start=True, stop=True),
                after=((c_ev, c_ev.val), (c_cast, mc_ld)), inc=c_mm)
            p.emit("vector", lambda e: e.tensor_scalar(
                scol[:, :], psSc[:, 0:1], 1.0 / math.sqrt(128.0), None,
                ALU.mult), after=((c_mm, c_mm.val),), inc=c_ev)
            p.emit("vector", lambda e: e.tensor_copy(
                ccol_bf[:, :], scol[:, :]), inc=c_ev)
            p.emit("tensor", lambda e: e.transpose(
                psCc[0:1, 0:8], ccol_bf[:, :], identbf2[0:8, 0:8]),
                after=((c_ev, c_ev.val), (c_gp, gp_s)), inc=c_mm)
            p.emit("vector", lambda e: e.tensor_copy(
                crow[0:1, :], psCc[0:1, 0:8]),
                after=((c_mm, c_mm.val),), inc=c_ev)
            p.emit("vector", lambda e: e.reduce_max(
                env.cells[0:1, 0:1], crow[0:1, :], axis=AXX, negate=True),
                inc=c_ev)
            p.emit("scalar", lambda e: e.activation(
                crow[0:1, :], crow[0:1, :], AF.Exp, bias=env.cells[0:1, 0:1]),
                after=((c_ev, c_ev.val),), inc=c_sc)
            p.emit("vector", lambda e: e.reduce_sum(
                env.cells[0:1, 1:2], crow[0:1, :], axis=AXX),
                after=((c_sc, c_sc.val),), inc=c_ev)
            p.emit("vector", lambda e: e.reciprocal(
                env.cells[0:1, 2:3], env.cells[0:1, 1:2]), inc=c_ev)
            p.emit("vector", lambda e: e.tensor_scalar(
                crow[0:1, :], crow[0:1, :], env.cells[0:1, 2:3], None,
                ALU.mult), inc=c_ev)
            p.emit("vector", lambda e: e.tensor_copy(crow_bf[0:1, :],
                                                     crow[0:1, :]), inc=c_ev)
            p.emit("tensor", lambda e: e.transpose(
                psCc[:, 8:9], crow_bf[0:1, :], identbf2[0:1, 0:1]),
                after=((c_ev, c_ev.val),), inc=c_mm)
            p.emit("vector", lambda e: e.tensor_copy(
                ccol_bf[:, :], psCc[0:8, 8:9]),
                after=((c_mm, c_mm.val),), inc=c_ev)
            for ec in range(4):
                p.emit("tensor", lambda e, ec=ec: e.matmul(
                    psRp[:, ec:ec + 1], s64[:, ec * 128:(ec + 1) * 128],
                    ccol_bf[:, :], start=True, stop=True),
                    after=((c_ev, c_ev.val), (c_gp, gp_s)), inc=c_mm)
            p.emit("vector", lambda e: e.tensor_copy(
                coefrep[:, :], psRp[:, 0:4]),
                after=((c_mm, c_mm.val),), inc=c_ev)

    hw = p.sb("hw", [128, G4 * T], BF16)
    with nc.sbuf_tensor("downt", [128, NCH * 512], BF16) as downt:
        c_dt = p.ctr("dt")
        src = downT_in.rearrange("(c p) f -> p c f", p=128)
        dst = downt[:, :].rearrange("p (c f) -> p c f", f=512)
        p.dma("gpsimd", dst, src, c_dt,
              after=((c_ev, c_ev.val), (c_mm, c_mm.val)))
        d_ld = c_dt.val
        with nc.psum_tensor("psL0", [128, 2048], F32) as psL0, \
             nc.psum_tensor("psL1", [128, 2048], F32) as psL1:
            pps = [psL0, psL1]
            for et in range(G4):
                ps = pps[et % 2]
                for ic in range(NCH):
                    for lg in range(4):
                        p.emit("tensor", lambda e, et=et, ic=ic, lg=lg, ps=ps:
                               e.matmul(
                                   ps[:, lg * 512:(lg + 1) * 512],
                                   downt[:, ic * 512 + et * 128:
                                         ic * 512 + (et + 1) * 128],
                                   xn2[:, ic * T + lg * 512:
                                       ic * T + (lg + 1) * 512],
                                   start=(ic == 0), stop=(ic == NCH - 1)),
                               after=((c_dt, d_ld), (c_x, xn_ld),
                                      (c_ev, c_ev.val)),
                               inc=(c_mm if (ic == NCH - 1 and lg == 3)
                                    else None))
                mm_d = c_mm.val
                for lg in range(4):
                    p.emit("vector", lambda e, et=et, lg=lg, ps=ps:
                           e.tensor_scalar(
                               hw[:, et * T + lg * 512:et * T + (lg + 1) * 512],
                               ps[:, lg * 512:(lg + 1) * 512],
                               coefrep[:, et:et + 1], None, ALU.mult),
                           after=((c_mm, mm_d),), inc=c_ev)
    hw_done = c_ev.val

    with nc.sbuf_tensor("wgT0", [128, NCH * 256], BF16) as wgT0, \
         nc.sbuf_tensor("wgT1", [128, NCH * 256], BF16) as wgT1, \
         nc.sbuf_tensor("upt", [128, 4 * 256], BF16) as upt, \
         nc.sbuf_tensor("yab0", [128, 512], BF16) as yab0, \
         nc.sbuf_tensor("yab1", [128, 512], BF16) as yab1, \
         nc.sbuf_tensor("x2b0", [128, 512], F32) as x2b0, \
         nc.sbuf_tensor("x2b1", [128, 512], F32) as x2b1, \
         nc.sbuf_tensor("letmp", [128, 512], F32) as letmp, \
         nc.sbuf_tensor("lout0", [128, 512], F32) as lout0, \
         nc.sbuf_tensor("lout1", [128, 512], F32) as lout1:
        wgT = [wgT0, wgT1]
        yab = [yab0, yab1]
        x2b = [x2b0, x2b1]
        louts = [lout0, lout1]
        c_wb = p.ctr("wb2")
        c_xe = p.ctr("xe2")
        ev_at = {}

        def ldwo(og, aft):
            buf2 = wgT[og % 2]
            for hh in range(2):
                srcW = WorgT_in[hh * 2048:(hh + 1) * 2048,
                                og * 256:(og + 1) * 256]
                srcW = srcW.rearrange("(c p) f -> p c f", p=128)
                dstW = buf2[:, hh * 16 * 256:(hh + 1) * 16 * 256].rearrange(
                    "p (c f) -> p c f", f=256)
                p.dma("gpsimd", dstW, srcW, c_wb, after=aft)
                aft = ()

        def ldep2(j, aft):
            ct2, lg2 = j // 4, j % 4
            i2 = j % 2
            p.dma("scalar", yab[i2][:, :],
                  yaT[ct2 * 128:(ct2 + 1) * 128,
                      lg2 * 512:(lg2 + 1) * 512], c_xe, after=aft)
            p.dma("scalar", x2b[i2][:, :],
                  x2T[ct2 * 128:(ct2 + 1) * 128, lg2 * 512:(lg2 + 1) * 512],
                  c_xe, after=())

        ldwo(0, ((c_ev, hw_done),))
        mm_at_og = {}
        with nc.psum_tensor("psB0", [128, 2048], F32) as psB0, \
             nc.psum_tensor("psB1", [128, 2048], F32) as psB1:
            psB = [psB0, psB1]
            for og in range(16):
                buf = wgT[og % 2]
                if og + 1 < 16:
                    ldwo(og + 1, ((c_mm, mm_at_og.get(og - 1, 0)),))
                src = upF_in[:, og * 256:(og + 1) * 256].rearrange(
                    "(c p) f -> p c f", p=128)
                dst = upt[:, :].rearrange("p (c f) -> p c f", f=256)
                p.dma("gpsimd", dst, src, c_wb,
                      after=(((c_mm, mm_at_og[og - 1]),) if og > 0
                             else ((c_ev, hw_done),)))
                w_ld2 = c_wb.val
                for ot in range(2):
                    ps = psB[ot]
                    o_glob = og * 2 + ot
                    for dc in range(NCH):
                        for lg in range(4):
                            p.emit("tensor",
                                   lambda e, buf=buf, ps=ps, ot=ot, dc=dc,
                                   lg=lg: e.matmul(
                                       ps[:, lg * 512:(lg + 1) * 512],
                                       buf[:, dc * 256 + ot * 128:
                                           dc * 256 + (ot + 1) * 128],
                                       xn2[:, dc * T + lg * 512:
                                           dc * T + (lg + 1) * 512],
                                       start=(dc == 0), stop=False),
                                   after=((c_wb, w_ld2),
                                          ev_at.get(("p", og, ot),
                                                    (c_ev, hw_done))))
                    for ec in range(4):
                        for lg in range(4):
                            p.emit("tensor",
                                   lambda e, ps=ps, ot=ot, ec=ec, lg=lg:
                                   e.matmul(
                                       ps[:, lg * 512:(lg + 1) * 512],
                                       upt[:, ec * 256 + ot * 128:
                                           ec * 256 + (ot + 1) * 128],
                                       hw[:, ec * T + lg * 512:
                                          ec * T + (lg + 1) * 512],
                                       start=False,
                                       stop=(ec == 3)),
                                   after=((c_wb, w_ld2), (c_ev, hw_done)),
                                   inc=(c_mm if (ec == 3 and lg == 3)
                                        else None))
                    mm_d = c_mm.val
                    for lg in range(4):
                        j = o_glob * 4 + lg
                        i2 = j % 2
                        if j == 0:
                            ldep2(0, ((c_x, xn_ld),))
                        if j + 1 < 128:
                            ldep2(j + 1, ((c_ev, ev_at.get(("a", j - 1), 0)),))
                        ld = c_xe.val
                        p.emit("vector",
                               lambda e, ps=ps, lg=lg, i2=i2: e.tensor_add(
                                   letmp[:, :], ps[:, lg * 512:(lg + 1) * 512],
                                   yab[i2][:, :]),
                               after=((c_mm, mm_d), (c_xe, ld)), inc=c_ev)
                        p.emit("vector", lambda e, i2=i2: e.tensor_add(
                            louts[i2][:, :], letmp[:, :], x2b[i2][:, :]),
                            after=((c_out, c_out.val),), inc=c_ev)
                        ev_at[("a", j)] = c_ev.val
                        p.dma("sync",
                              o_out[o_glob * 128:(o_glob + 1) * 128,
                                    lg * 512:(lg + 1) * 512],
                              louts[i2][:, :], c_out,
                              after=((c_ev, c_ev.val),))
                    ev_at[("p", og + 1, ot)] = (c_ev, c_ev.val)
                mm_at_og[og] = c_mm.val
    p.emit("sync", lambda e: e.nop(), after=((c_out, c_out.val),))
    if standalone:
        return p.build()
    return c_out



def _ag(p, src, dst, gate, cnm="cc"):
    c = p.ctr(cnm)
    p.emit("gpsimd", lambda e, s=src, d=dst: e.collective_compute(
        "AllGather", ALU.bypass, replica_groups=[list(range(NCORES))],
        ins=[s.opt()], outs=[d.opt()]),
        after=gate, inc=c, n=1, dma=True)
    return c


def _pbar(p):
    c = p.ctr("bar")
    p.emit("sync", lambda e: e.nop(), after=p.barrier(), inc=c, n=1)
    p.phase_gate = ((c, 1),)
    return p.phase_gate


def build_full():
    p = Prog()
    io = {}
    io["xT"] = p.din("xT", [D, T], F32)
    io["Wqkv"] = p.din("Wqkv", [D, 3 * DH], F32)
    io["bqkv"] = p.din("bqkv", [3 * DH], F32)
    io["WbT"] = p.din("WbT", [D, D], F32)
    io["b_base"] = p.din("b_base", [D], F32)
    io["WgtT"] = p.din("WgtT", [D, 9], F32)
    io["b_thr"] = p.din("b_thr", [1], F32)
    io["AT"] = p.din("AT", [D, 128], F32)
    io["BmT"] = p.din("BmT", [128, D], F32)
    io["Wo"] = p.din("Wo", [DH, D], F32)
    io["bo"] = p.din("bo", [D], F32)
    io["WinT"] = p.din("WinT", [D, DIM], F32)
    io["bin"] = p.din("bin", [DIM], F32)
    io["l1p_in"] = p.din("l1p_in", [128, PCOLS], F32)
    io["kvec"] = p.din("kvec", [1, 8], F32)
    io["rkc"] = p.din("rkc", [1, 2], F32)
    io["W1T"] = p.din("W1T", [4, DIM, DFF], F32)
    io["W2T"] = p.din("W2T", [4, DFF, DIM], F32)
    io["WoutT"] = p.din("WoutT", [DIM, D], F32)
    io["WorgT"] = p.din("WorgT", [D, D], F32)
    io["downT"] = p.din("downT", [D, 512], F32)
    io["upF"] = p.din("upF", [512, D], F32)
    io["keysT"] = p.din("keysT", [128, 8], F32)
    io["WqT"] = p.din("WqT", [D, 128], F32)
    io["outT"] = p.dout("outT", [D, T], F32)
    io["xnT"] = p.dtmp("d_xnT", [D, T], BF16)
    io["qT"] = p.dtmp("d_qT", [DH, T], BF16)
    io["kT"] = p.dtmp("d_kT", [DH, T], BF16)
    io["V"] = p.dtmp("d_V", [T, DH], BF16)
    io["qm"] = p.dtmp("d_qm", [1, 1], F32)
    io["km"] = p.dtmp("d_km", [1, 1], F32)
    io["kTg"] = p.dtmp("d_kTg", [NCORES * DH, T], BF16,
                       addr_space="Shared")
    io["Vg"] = p.dtmp("d_Vg", [S, DH], BF16, addr_space="Shared")
    io["kms"] = p.dtmp("d_kms", [NCORES, 1], F32)
    io["x2T"] = p.dtmp("d_x2T", [D, T], F32)
    io["xn2T"] = p.dtmp("d_xn2T", [D, T], BF16)
    io["mp"] = p.dtmp("d_mp", [128, NCH], F32)
    d_mpg = p.dtmp("d_mpg", [NCORES * 128, NCH], F32)
    io["mps_ag"] = d_mpg
    d_E = p.dtmp("d_E", [128, 8], F32)
    d_Eg = p.dtmp("d_Eg", [NCORES * 128, 8], F32)
    d_h = [p.dtmp(f"d_h{j}", [DIM, T], F32) for j in range(2)]
    d_gre = [p.dtmp(f"d_gre{j}", [DIM, T], F32) for j in range(2)]
    d_gim = [p.dtmp(f"d_gim{j}", [DIM, T], F32) for j in range(2)]
    io["yaT"] = p.dtmp("d_yaT", [D, T], BF16)

    p.push_scope()
    build_l0a(p=p, io=io)
    p.pop_scope()
    d_w1bf = p.dtmp("d_w1bf", [4, DIM, DFF], BF16)
    d_w2bf = p.dtmp("d_w2bf", [4, DFF, DIM], BF16)
    c_wpre = p.ctr("wpre")
    for i4 in range(4):
        p.dma("gpsimd", d_w1bf[i4], io["W1T"][i4], c_wpre, nogate=True)
        p.dma("gpsimd", d_w2bf[i4], io["W2T"][i4], c_wpre, nogate=True)
    env_wpre = (c_wpre, c_wpre.val)

    _ag(p, io["kT"], io["kTg"], (io.pop("kv_stored"),), "cc_k")
    _ag(p, io["V"], io["Vg"], (io.pop("v_stored"),), "cc_v")
    _pbar(p)
    _ag(p, io["km"], io["kms"], p.phase_gate, "cc_km")
    _pbar(p)

    p.push_scope()
    build_l0b(p=p, io=io)
    p.pop_scope()
    _pbar(p)
    p.emit("gpsimd", lambda e: e.engine_nop(), after=p.phase_gate)

    env = L1Env(p, with_tables=False, with_rows=False)
    env.w1bf, env.w2bf, env.wpre = d_w1bf, d_w2bf, env_wpre
    _l1_env_setup(p, env, io)
    p.push_scope()
    env.jt = p.sb("jt", [128, T], F32)
    io["h_sb"] = p.sb("h_res", [128, G4 * T], F32)
    io["gre_sb"] = p.sb("gre_res", [128, G4 * T], F32)
    p.emit("gpsimd", lambda e: e.iota(
        env.jt[:, :], pattern=[[1, T]], base=0, channel_multiplier=0,
        allow_small_or_imprecise_dtypes=True), inc=env.c_gp)
    env.gp0 = env.c_gp.val

    p.push_scope()
    env.rows = p.sb("rows", [1, 3 * T], F32)
    io1 = dict(io)
    io1["gim"] = d_gim[0]
    io1["E"] = d_E
    build_l1a(p=p, io=io1, env=env)
    p.pop_scope()
    _pbar(p)
    _ag(p, io["mp"], d_mpg, p.phase_gate, "cc_mp")
    _ag(p, d_E, d_Eg, p.phase_gate, "cc_e0")
    _pbar(p)

    for li in range(4):
        last = (li == 3)
        io2 = dict(io)
        io2["gim"] = d_gim[li % 2]
        io2["Eall_ag"] = d_Eg
        if not last:
            io2["gim_o"] = d_gim[(li + 1) % 2]
            io2["E"] = d_E
        p.push_scope()
        build_l1s(li, last, p=p, io=io2, env=env)
        p.pop_scope()
        _pbar(p)
        if not last:
            _ag(p, d_E, d_Eg, p.phase_gate, f"cc_e{li + 1}")
            _pbar(p)

    p.pop_scope()
    for k in ("h_sb", "gre_sb"):
        io.pop(k, None)
    _pbar(p)
    p.push_scope()
    build_l1f(p=p, io=io, env=env)
    p.pop_scope()
    p.emit("sync", lambda e: e.nop(), after=p.barrier())
    return p.build()



def _shard_T(x2d):
    return [np.ascontiguousarray(x2d[i * T:(i + 1) * T, :].T) for i in range(NCORES)]


_BUILDS = {}


def _get_nc(key, fn):
    nc = _BUILDS.get(key)
    if nc is None:
        nc = fn()
        _BUILDS[key] = nc
    return nc


class _Res:
    def __init__(self, results):
        self.results = results


def _launch(nc, in_maps):
    import os as _oslt
    import time as _tlt
    t0 = _tlt.time()
    r = run_bass_kernel_spmd(nc, in_maps, core_ids=list(range(NCORES)))
    if _oslt.environ.get("KLOG"):
        print(f"[launch {_tlt.time():.2f}] took {_tlt.time() - t0:.2f}s",
              flush=True)
    return _Res([{k: np.array(v) for k, v in res.items()} for res in r.results])


def run_l0a(x2d, Wqkv, bqkv):
    nc = _get_nc("l0a", build_l0a)
    xts = _shard_T(x2d)
    in_maps = [{"xT": xts[i], "Wqkv": Wqkv, "bqkv": bqkv} for i in range(NCORES)]
    res = _launch(nc, in_maps)
    return res.results


def run_l0b(ins, l0a_res, upto="all"):
    import os
    upto = os.environ.get("L0B_UPTO", upto)
    nc = _get_nc(("l0b", upto), lambda: build_l0b(upto=upto))
    xts = _shard_T(ins["L0_attn_x"][0])
    kTg = np.concatenate([r["kT"] for r in l0a_res], axis=0)
    Vg = np.concatenate([r["V"] for r in l0a_res], axis=0)
    kms = np.stack([r["km"][0] for r in l0a_res], axis=0)
    WbT = np.ascontiguousarray(ins["L0_ffn_W_base"].T)
    WgtT = np.ascontiguousarray(
        np.concatenate([ins["L0_ffn_W_gate"], ins["L0_ffn_W_thr"]], axis=0).T)
    AT = np.ascontiguousarray(ins["L0_ffn_A"].reshape(128, D).T)
    BmT = np.ascontiguousarray(
        ins["L0_ffn_B"].transpose(0, 2, 1).reshape(128, D))
    in_maps = []
    for i in range(NCORES):
        in_maps.append({
            "xnT": l0a_res[i]["xnT"], "qT": l0a_res[i]["qT"],
            "kTg": kTg, "Vg": Vg, "qm": l0a_res[i]["qm"], "kms": kms,
            "xT": xts[i],
            "WbT": WbT, "b_base": ins["L0_ffn_b_base"],
            "WgtT": WgtT, "b_thr": ins["L0_ffn_b_thr"],
            "AT": AT, "BmT": BmT,
            "Wo": ins["L0_attn_Wo"], "bo": ins["L0_attn_bo"],
        })
    res = _launch(nc, in_maps)
    return res.results


def _pack_l1p(ins):
    arr = np.zeros((128, PCOLS), np.float32)

    def setcols(base, param):
        for g in range(param.size // 128):
            arr[:, base + g] = param[g * 128:(g + 1) * 128]
    for i in range(4):
        setcols(_pcol(i, "ln1_w"), ins["L1_attn_ln1_w"][i])
        setcols(_pcol(i, "ln1_b"), ins["L1_attn_ln1_b"][i])
        setcols(_pcol(i, "phz_re"), ins["L1_attn_phz_re"][i])
        setcols(_pcol(i, "phz_im"), ins["L1_attn_phz_im"][i])
        setcols(_pcol(i, "lc_re"), ins["L1_attn_lc_re"][i])
        setcols(_pcol(i, "lc_im"), ins["L1_attn_lc_im"][i])
        setcols(_pcol(i, "sc_lin"), ins["L1_attn_sc_lin"][i])
        setcols(_pcol(i, "ln2_w"), ins["L1_attn_ln2_w"][i])
        setcols(_pcol(i, "ln2_b"), ins["L1_attn_ln2_b"][i])
        setcols(_pcol(i, "b2"), ins["L1_attn_b2"][i])
        setcols(_pcol(i, "b1"), ins["L1_attn_b1"][i])
    setcols(PCOL_LNLW, ins["L1_attn_lnl_w"])
    setcols(PCOL_LNLB, ins["L1_attn_lnl_b"])
    setcols(PCOL_BOUT, ins["L1_attn_bout"])
    return arr


def _l1_percore(i):
    kvec = np.full((1, 8), 1e9, np.float32)
    for m in range(i):
        kvec[0, m] = i - 1 - m
    rkc = np.array([[2048.0 * i, float(i)]], np.float32)
    return kvec, rkc


def run_l1(ins, x2T_list):
    l1p = _pack_l1p(ins)
    WinT = np.ascontiguousarray(ins["L1_attn_Win"].T)
    W1T = np.ascontiguousarray(ins["L1_attn_W1"].transpose(0, 2, 1))
    W2T = np.ascontiguousarray(ins["L1_attn_W2"].transpose(0, 2, 1))
    WoutT = np.ascontiguousarray(ins["L1_attn_Wout"].T)
    percore = [_l1_percore(i) for i in range(NCORES)]

    nc = _get_nc("l1a", build_l1a)
    im = [{"x2T": x2T_list[i], "WinT": WinT, "bin": ins["L1_attn_bin"],
           "l1p_in": l1p, "kvec": percore[i][0], "rkc": percore[i][1]}
          for i in range(NCORES)]
    res = _launch(nc, im)
    st = res.results

    for li in range(4):
        Eall = np.ascontiguousarray(
            np.stack([st[i]["E"] for i in range(NCORES)], axis=2))
        last = (li == 3)
        nc = _get_nc(("l1s", li), lambda li=li, last=last: build_l1s(li, last))
        im = []
        for i in range(NCORES):
            d = {"hT": st[i]["hT"] if li == 0 else st[i]["hT_o"],
                 "gre": st[i]["gre"] if li == 0 else st[i]["gre_o"],
                 "gim": st[i]["gim"] if li == 0 else st[i]["gim_o"],
                 "Eall": Eall, "W1T": W1T, "W2T": W2T,
                 "l1p_in": l1p, "kvec": percore[i][0], "rkc": percore[i][1]}
            if last:
                d["WoutT"] = WoutT
            im.append(d)
        res = _launch(nc, im)
        st2 = res.results
        for i in range(NCORES):
            for kk in ("xn2T", "mp"):
                st2[i][kk] = st[i][kk]
        st = st2
    return st


def run_l1f(ins, x2T_list, st):
    WorgT = np.ascontiguousarray(ins["L1_ffn_W_org"].T)
    downT = np.ascontiguousarray(
        ins["L1_ffn_lora_down"].transpose(1, 0, 2).reshape(D, 512))
    upF = np.ascontiguousarray(ins["L1_ffn_lora_up"].reshape(512, D))
    keysT = np.ascontiguousarray(ins["L1_ffn_lora_keys"].T)
    WqT = np.ascontiguousarray(ins["L1_ffn_Wq"].T)
    mps = np.ascontiguousarray(
        np.stack([st[i]["mp"] for i in range(NCORES)], axis=2))
    nc = _get_nc("l1f", build_l1f)
    im = [{"x2T": x2T_list[i], "yaT": st[i]["yaT"], "xn2T": st[i]["xn2T"],
           "mps": mps, "WorgT": WorgT, "downT": downT, "upF": upF,
           "keysT": keysT, "WqT": WqT} for i in range(NCORES)]
    res = _launch(nc, im)
    return res.results


def kernel_unfused(**inputs):
    ins = {k: np.asarray(v, np.float32) for k, v in inputs.items()}
    x = ins["L0_attn_x"][0]
    l0a = run_l0a(x, ins["L0_attn_Wqkv"], ins["L0_attn_bqkv"])
    l0b = run_l0b(ins, l0a)
    x2T_list = [np.ascontiguousarray(r["x2T"]) for r in l0b]
    st = run_l1(ins, x2T_list)
    res = run_l1f(ins, x2T_list, st)
    out = np.concatenate([res[i]["outT"].T for i in range(NCORES)], axis=0)
    return np.ascontiguousarray(out.reshape(1, S, D), dtype=np.float32)


def kernel(**inputs):
    ins = {k: np.asarray(v, np.float32) for k, v in inputs.items()}
    xts = _shard_T(ins["L0_attn_x"][0])
    l1p = _pack_l1p(ins)
    WbT = np.ascontiguousarray(ins["L0_ffn_W_base"].T)
    WgtT = np.ascontiguousarray(
        np.concatenate([ins["L0_ffn_W_gate"], ins["L0_ffn_W_thr"]], axis=0).T)
    AT = np.ascontiguousarray(ins["L0_ffn_A"].reshape(128, D).T)
    BmT = np.ascontiguousarray(ins["L0_ffn_B"].transpose(0, 2, 1).reshape(128, D))
    WinT = np.ascontiguousarray(ins["L1_attn_Win"].T)
    W1T = np.ascontiguousarray(ins["L1_attn_W1"].transpose(0, 2, 1))
    W2T = np.ascontiguousarray(ins["L1_attn_W2"].transpose(0, 2, 1))
    WoutT = np.ascontiguousarray(ins["L1_attn_Wout"].T)
    WorgT = np.ascontiguousarray(ins["L1_ffn_W_org"].T)
    downT = np.ascontiguousarray(
        ins["L1_ffn_lora_down"].transpose(1, 0, 2).reshape(D, 512))
    upF = np.ascontiguousarray(ins["L1_ffn_lora_up"].reshape(512, D))
    keysT = np.ascontiguousarray(ins["L1_ffn_lora_keys"].T)
    WqT = np.ascontiguousarray(ins["L1_ffn_Wq"].T)
    nc = _get_nc("full", build_full)
    im = []
    for i in range(NCORES):
        kv, rk = _l1_percore(i)
        im.append({
            "xT": xts[i], "Wqkv": ins["L0_attn_Wqkv"],
            "bqkv": ins["L0_attn_bqkv"], "WbT": WbT,
            "b_base": ins["L0_ffn_b_base"], "WgtT": WgtT,
            "b_thr": ins["L0_ffn_b_thr"], "AT": AT, "BmT": BmT,
            "Wo": ins["L0_attn_Wo"], "bo": ins["L0_attn_bo"],
            "WinT": WinT, "bin": ins["L1_attn_bin"], "l1p_in": l1p,
            "kvec": kv, "rkc": rk, "W1T": W1T, "W2T": W2T, "WoutT": WoutT,
            "WorgT": WorgT, "downT": downT, "upF": upF, "keysT": keysT,
            "WqT": WqT})
    res = _launch(nc, im)
    out = np.concatenate([res.results[i]["outT"].T for i in range(NCORES)],
                         axis=0)
    return np.ascontiguousarray(out.reshape(1, S, D), dtype=np.float32)


# version 12 (final)
# speedup vs baseline: 1.0114x; 1.0114x over previous
import math
from contextlib import ExitStack
from functools import partial

import numpy as np
import ml_dtypes

import concourse.bass as bass
import concourse.mybir as mybir
from concourse.bass_utils import run_bass_kernel_spmd

F32 = mybir.dt.float32
BF16 = mybir.dt.bfloat16
AF = mybir.ActivationFunctionType
ALU = mybir.AluOpType
AXX = None

NCORES = 8
S = 16384
T = S // NCORES
D = 4096
DH = 1024
EPS = 1e-5
NCH = D // 128

PI = float(np.pi)
TWO_PI = float(2 * np.pi)
INV_2PI = float(1.0 / (2 * np.pi))
PI2_HI = 6.28125
PI2_LO = float(TWO_PI - PI2_HI)
MAGIC = 12582912.0

ENGINES = ("sync", "gpsimd", "tensor", "vector", "scalar")


class Ctr:
    __slots__ = ("sem", "val", "name")

    def __init__(self, sem, name):
        self.sem = sem
        self.val = 0
        self.name = name

    @property
    def now(self):
        return (self, self.val)


class Prog:

    CHAIN = ("vector", "scalar", "gpsimd")

    def __init__(self):
        self.nc = bass.Bass("TRN2", target_bir_lowering=False, debug=False)
        self.es = ExitStack()
        self.steps = {e: [] for e in ENGINES}
        self._nsem = 0
        self._auto = {}
        self._last = {}
        self._ctrs = []
        self._names = {}
        self._scopes = []
        self.phase_gate = ()

    def _uniq(self, name):
        k = self._names.get(name)
        if k is None:
            self._names[name] = 1
            return name
        self._names[name] = k + 1
        return f"{name}_{k}"

    def ctr(self, name=None):
        name = name or f"c{self._nsem}"
        self._nsem += 1
        c = Ctr(self.es.enter_context(self.nc.semaphore(self._uniq(name))),
                name)
        self._ctrs.append(c)
        return c

    def barrier(self):
        return tuple((c, c.val) for c in self._ctrs if c.val > 0)

    def push_scope(self):
        self._scopes.append(ExitStack())

    def pop_scope(self):
        self._scopes.pop().close()

    def sb(self, name, shape, dt):
        st = self._scopes[-1] if self._scopes else self.es
        return st.enter_context(self.nc.sbuf_tensor(self._uniq(name), shape,
                                                    dt))

    def nsb(self, name, shape, dt):
        return self.nc.sbuf_tensor(self._uniq(name), shape, dt)

    def nps(self, name, shape, dt):
        return self.nc.psum_tensor(self._uniq(name), shape, dt)

    def din(self, name, shape, dt):
        return self.nc.dram_tensor(name, shape, dt, kind="ExternalInput").ap()

    def dout(self, name, shape, dt):
        return self.nc.dram_tensor(name, shape, dt, kind="ExternalOutput").ap()

    def dtmp(self, name, shape, dt, **kw):
        return self.nc.dram_tensor(name, shape, dt, **kw).ap()

    def emit(self, eng, fn, after=(), inc=None, n=1, dma=False,
             nogate=False):
        after = tuple(after)
        if dma and self.phase_gate and not nogate:
            after = after + self.phase_gate
        if eng in self.CHAIN and not dma:
            prev = self._last.get(eng)
            if prev is not None:
                after = after + (prev,)
            if inc is None:
                if eng not in self._auto:
                    self._auto[eng] = self.ctr(f"auto_{eng}")
                inc = self._auto[eng]
        self.steps[eng].append((after, fn, inc, n))
        if inc is not None:
            inc.val += n
            if eng in self.CHAIN and not dma:
                self._last[eng] = (inc, inc.val)

    def dma(self, eng, dst, src, ctr, after=(), nogate=False, **kw):
        self.emit(eng, lambda e, d=dst, s=src: e.dma_start(out=d, in_=s, **kw),
                  after=after, inc=ctr, n=16, dma=True, nogate=nogate)

    def build(self):
        import os as _os
        self.check()
        if _os.environ.get("KCHECK"):
            self.check_dma_order()
        with self.nc.Block() as block:
            for eng in ENGINES:
                getattr(block, eng)(partial(self._replay, eng))
        self.es.close()
        return self.nc

    def _replay(self, eng, e):
        seen = {}
        for after, fn, inc, n in self.steps[eng]:
            for (c, v) in after:
                if v > 0 and seen.get(id(c), -1) < v:
                    e.wait_ge(c.sem, v)
                    seen[id(c)] = v
            inst = fn(e)
            if inc is not None:
                inst.then_inc(inc.sem, n)

    def check(self):
        vals = {}
        pos = {e: 0 for e in ENGINES}
        total = sum(len(v) for v in self.steps.values())
        done = 0
        progress = True
        while progress:
            progress = False
            for e in ENGINES:
                while pos[e] < len(self.steps[e]):
                    after, fn, inc, n = self.steps[e][pos[e]]
                    if any(vals.get(id(c), 0) < v for (c, v) in after):
                        break
                    if inc is not None:
                        vals[id(inc)] = vals.get(id(inc), 0) + n
                    pos[e] += 1
                    done += 1
                    progress = True
        if done != total:
            for e in ENGINES:
                if pos[e] < len(self.steps[e]):
                    after, fn, inc, n = self.steps[e][pos[e]]
                    blocked = [(c.name, v, vals.get(id(c), 0)) for (c, v) in after
                               if vals.get(id(c), 0) < v]
                    raise RuntimeError(
                        f"HANG: engine {e} step {pos[e]}/{len(self.steps[e])} "
                        f"blocked on {blocked}")
        return True

    def check_dma_order(self):
        prod = {}
        engs = {}
        for e in ENGINES:
            for idx, (after, fn, inc, n) in enumerate(self.steps[e]):
                if inc is not None:
                    prod.setdefault(id(inc), []).append((e, idx, n))
                    engs.setdefault(id(inc), set()).add(e)
        names = {}
        for e in ENGINES:
            for idx, (after, fn, inc, n) in enumerate(self.steps[e]):
                if inc is not None:
                    names[id(inc)] = inc.name
        for k, es in engs.items():
            plist = prod[k]
            if len(es) > 1 and any(n == 16 for (_, _, n) in plist):
                raise RuntimeError(f"dma ctr {names[k]} produced by {es}")
        succ = {}

        def add(a, b):
            succ.setdefault(a, []).append(b)

        nominal = {}
        for cid, plist in prod.items():
            cum = 0
            for (e, idx, n) in plist:
                cum += n
                nominal[(cid, cum)] = (e, idx)
        for e in ENGINES:
            for idx in range(1, len(self.steps[e])):
                add((e, idx - 1), (e, idx))
            for idx, (after, fn, inc, n) in enumerate(self.steps[e]):
                for (c, v) in after:
                    if v <= 0:
                        continue
                    plist = prod.get(id(c))
                    if plist is None:
                        raise RuntimeError(f"wait on unproduced ctr {c.name}")
                    cum = 0
                    pn = None
                    for (pe, pidx, n2) in plist:
                        cum += n2
                        if cum >= v:
                            pn = (pe, pidx)
                            break
                    if pn is None:
                        raise RuntimeError(
                            f"wait {c.name}>={v} never reached")
                    add(pn, (e, idx))
        import collections

        def reaches(src, dst):
            seen = set()
            dq = collections.deque([src])
            while dq:
                u = dq.popleft()
                if u == dst:
                    return True
                for w in succ.get(u, ()):
                    if w not in seen:
                        seen.add(w)
                        dq.append(w)
            return False

        bad = []
        for e in ENGINES:
            for idx, (after, fn, inc, n) in enumerate(self.steps[e]):
                for (c, v) in after:
                    plist = prod.get(id(c))
                    if not plist or plist[0][2] != 16:
                        continue
                    k = v // 16
                    if k >= len(plist):
                        continue
                    ue, uidx, _ = plist[k]
                    if not reaches((e, idx), (ue, uidx)):
                        bad.append((e, idx, c.name, v, f"{ue}[{uidx}]"))
        if bad:
            msg = "\n".join(
                f"  UNSOUND dma wait: {e}[{i}] waits {nm}>={v}; "
                f"uncovered dma at {u}" for (e, i, nm, v, u) in bad[:20])
            raise RuntimeError(
                f"{len(bad)} unsound DMA waits (completions can arrive "
                f"out of order):\n{msg}")
        return True


def make_identity(nc, identity):
    nc.gpsimd.memset(identity, 0.0)
    nc.gpsimd.affine_select(
        out=identity, in_=identity, compare_op=ALU.not_equal, fill=1.0,
        base=0, pattern=[[-1, identity.shape[0]]], channel_multiplier=1,
    )


AXX = mybir.AxisListType.X



def _pin(p, io, name, shape, dt):
    if io and name in io:
        return io[name]
    return p.din(name, shape, dt)


def _pout(p, io, name, shape, dt):
    if io and name in io:
        return io[name]
    return p.dout(name, shape, dt)


def build_l0a(p=None, io=None):
    standalone = p is None
    if standalone:
        p = Prog()
    nc = p.nc
    xT = _pin(p, io, "xT", [D, T], F32)
    Wqkv = _pin(p, io, "Wqkv", [D, 3 * DH], F32)
    bqkv = _pin(p, io, "bqkv", [3 * DH], F32)
    o_xnT = _pout(p, io, "xnT", [D, T], BF16)
    o_qT = _pout(p, io, "qT", [DH, T], BF16)
    o_kT = _pout(p, io, "kT", [DH, T], BF16)
    o_V = _pout(p, io, "V", [T, DH], BF16)
    o_qm = _pout(p, io, "qm", [1, 1], F32)
    o_km = _pout(p, io, "km", [1, 1], F32)

    shared_xn = False
    xball = p.sb("xball", [128, NCH * T], BF16)
    sqb = p.sb("sqb", [128, T], BF16)
    ones_col_bf = p.sb("ones_col_bf", [128, 1], BF16)
    ones_row_f = p.sb("ones_row_f", [1, 128], F32)
    row16 = p.sb("row16", [1, 16], F32)
    eps_cell = row16[0:1, 8:9]
    bvrow_f = p.sb("bvrow_f", [1, DH], F32)
    bqcols = p.sb("bqcols", [128, 16], F32)
    identbf = p.sb("identbf", [128, 128], BF16)

    c_x = p.ctr("x")
    c_xb = p.ctr("xb")
    c_sq = p.ctr("sq")
    c_st = p.ctr("st")
    c_fin = p.ctr("fin")
    c_xn = p.ctr("xn")
    c_w = p.ctr("w")
    c_mm = p.ctr("mm")
    c_ev = p.ctr("ev")
    c_out = p.ctr("out")
    c_gp = p.ctr("gp")
    c_tr = p.ctr("tr")
    c_kv = p.ctr("kv")
    c_vo = p.ctr("vo")

    p.emit("gpsimd", lambda e: e.memset(ones_col_bf[:, :], 1.0))
    p.emit("gpsimd", lambda e: e.memset(ones_row_f[:, :], 1.0))
    p.emit("gpsimd", lambda e: e.memset(eps_cell, EPS), inc=c_gp)

    p.emit("gpsimd", lambda e: e.memset(identbf[:, :], 0.0), inc=c_gp)
    p.emit("gpsimd", lambda e: e.affine_select(
        out=identbf[:, :], in_=identbf[:, :], compare_op=ALU.not_equal,
        fill=1.0, base=0, pattern=[[-1, 128]], channel_multiplier=1),
        after=((c_gp, c_gp.val),), inc=c_gp)
    p.dma("sync", bvrow_f[:, :], bqkv[None, 2 * DH:3 * DH], c_x)
    bqrow_scope = nc.sbuf_tensor("bqrow_bf", [1, 2 * DH], BF16)
    bqrow_bf = bqrow_scope.__enter__()
    p.dma("gpsimd", bqrow_bf[:, :], bqkv[None, 0:2 * DH], c_w)
    with nc.psum_tensor("psT", [128, 32], BF16) as psT:
        for j in range(16):
            p.emit("tensor",
                   lambda e, j=j: e.transpose(psT[:, 2 * j:2 * j + 1],
                                              bqrow_bf[0:1, j * 128:(j + 1) * 128],
                                              identbf[0:1, 0:1]),
                   after=(c_w.now, c_gp.now), inc=c_tr)
        p.emit("vector", lambda e: e.tensor_copy(bqcols[:, :], psT[:, 0:32:2]),
               after=(c_tr.now,), inc=c_ev)
    bqrow_scope.__exit__(None, None, None)

    with nc.sbuf_tensor("xf0", [128, T], F32) as xf0, \
         nc.sbuf_tensor("xf1", [128, T], F32) as xf1, \
         nc.sbuf_tensor("tmp4", [128, T], F32) as tmp, \
         nc.sbuf_tensor("ba", [128, T], F32) as ba, \
         nc.sbuf_tensor("bb", [128, T], F32) as bb, \
         nc.sbuf_tensor("l0rows", [1, 3 * T], F32) as rows:
        xf = [xf0, xf1]
        row_a = rows[0:1, 0:T]
        row_b = rows[0:1, T:2 * T]
        row_c = rows[0:1, 2 * T:3 * T]
        with nc.psum_tensor("psA", [128, 2048], F32) as psA, \
             nc.psum_tensor("psB", [128, 2048], F32) as psB:
            for c in range(2):
                p.dma("sync", xf[c % 2][:, :], xT[c * 128:(c + 1) * 128, :],
                      c_x, after=((c_tr, 16),))
            for c in range(NCH):
                buf = xf[c % 2]
                xld = c_x.val
                if c + 2 < NCH:
                    p.dma("sync", xf[(c + 2) % 2][:, :],
                          xT[(c + 2) * 128:(c + 3) * 128, :], c_x,
                          after=((c_xb, c + 1),))
                p.emit("vector",
                       lambda e, c=c, buf=buf: e.tensor_copy(
                           xball[:, c * T:(c + 1) * T], buf[:, :]),
                       after=((c_x, xld), (c_ev, 1)), inc=c_xb)
                p.emit("scalar",
                       lambda e, c=c: e.activation(
                           sqb[:, :], xball[:, c * T:(c + 1) * T], AF.Square),
                       after=((c_xb, c + 1), (c_st, 8 * c)), inc=c_sq)
                for j in range(4):
                    p.emit("tensor",
                           lambda e, c=c, j=j: e.matmul(
                               psA[0:1, j * 512:(j + 1) * 512], ones_col_bf[:, :],
                               xball[:, c * T + j * 512:c * T + (j + 1) * 512],
                               start=(c == 0), stop=(c == NCH - 1)),
                           after=((c_xb, c + 1),), inc=c_st)
                    p.emit("tensor",
                           lambda e, c=c, j=j: e.matmul(
                               psB[0:1, j * 512:(j + 1) * 512], ones_col_bf[:, :],
                               sqb[:, j * 512:(j + 1) * 512],
                               start=(c == 0), stop=(c == NCH - 1)),
                           after=((c_sq, c + 1),), inc=c_st)
            st_all = c_st.val
            p.emit("vector", lambda e: e.tensor_scalar(
                row_a, psA[0:1, :], 1.0 / D, None, ALU.mult),
                after=((c_st, st_all),), inc=c_fin)
            p.emit("vector", lambda e: e.tensor_scalar(
                row_b, psB[0:1, :], 1.0 / D, None, ALU.mult), inc=c_fin)
            p.emit("scalar", lambda e: e.activation(row_c, row_a, AF.Square),
                   after=((c_fin, c_fin.val),), inc=c_sq)
            p.emit("vector", lambda e: e.tensor_sub(row_b, row_b, row_c),
                   after=((c_sq, c_sq.val),), inc=c_fin)
            p.emit("scalar", lambda e: e.activation(row_b, row_b, AF.Ln, bias=eps_cell),
                   after=((c_fin, c_fin.val), (c_gp, 2)), inc=c_sq)
            p.emit("scalar", lambda e: e.activation(row_b, row_b, AF.Exp, scale=-0.5),
                   inc=c_sq)
            p.emit("vector", lambda e: e.tensor_mul(row_c, row_a, row_b),
                   after=((c_sq, c_sq.val),), inc=c_fin)
            p.emit("vector", lambda e: e.tensor_scalar(
                row_c, row_c, -1.0, None, ALU.mult), inc=c_fin)

        fin_done = c_fin.val
        with nc.psum_tensor("psA2", [128, 2048], F32) as psA2, \
             nc.psum_tensor("psB2", [128, 2048], F32) as psB2:
            for j in range(4):
                p.emit("tensor", lambda e, j=j: e.matmul(
                    psA2[:, j * 512:(j + 1) * 512], ones_row_f[:, :],
                    row_b[0:1, j * 512:(j + 1) * 512], start=True, stop=True),
                    after=((c_fin, fin_done),), inc=c_mm)
                p.emit("tensor", lambda e, j=j: e.matmul(
                    psB2[:, j * 512:(j + 1) * 512], ones_row_f[:, :],
                    row_c[0:1, j * 512:(j + 1) * 512], start=True, stop=True),
                    inc=c_mm)
            p.emit("vector", lambda e: e.tensor_copy(ba[:, :], psA2[:, :]),
                   after=((c_mm, c_mm.val),), inc=c_ev)
            p.emit("vector", lambda e: e.tensor_copy(bb[:, :], psB2[:, :]), inc=c_ev)

            bc_done = c_ev.val
            for c in range(NCH):
                p.emit("vector", lambda e, c=c: e.tensor_mul(
                    tmp[:, :], xball[:, c * T:(c + 1) * T], ba[:, :]),
                    after=((c_ev, bc_done),))
                p.emit("vector", lambda e, c=c: e.tensor_add(
                    xball[:, c * T:(c + 1) * T], tmp[:, :], bb[:, :]), inc=c_xn)
                if not shared_xn:
                    p.dma("sync", o_xnT[c * 128:(c + 1) * 128, :],
                          xball[:, c * T:(c + 1) * T], c_out,
                          after=((c_xn, c + 1),))
    xn_all = c_xn.val
    if shared_xn:
        io["xn_ready"] = (c_xn, xn_all)

    with nc.sbuf_tensor("wq0", [128, NCH * 128], BF16) as wq0, \
         nc.sbuf_tensor("wq1", [128, NCH * 128], BF16) as wq1, \
         nc.sbuf_tensor("rot0", [128, T], BF16) as rot0, \
         nc.sbuf_tensor("rot1", [128, T], BF16) as rot1, \
         nc.sbuf_tensor("wv", [128, NCH * 256], BF16) as wv, \
         nc.sbuf_tensor("sqacc_q", [128, T], BF16) as sqacc_q, \
         nc.sbuf_tensor("sqacc_k", [128, T], BF16) as sqacc_k:
        wq = [wq0, wq1]
        rot = [rot0, rot1]
        rot_guard = {}
        p.emit("gpsimd", lambda e: e.memset(sqacc_q[:, :], 0.0),
               after=((c_xn, xn_all),), inc=c_gp)
        p.emit("gpsimd", lambda e: e.memset(sqacc_k[:, :], 0.0), inc=c_gp)
        gp_sq = c_gp.val
        with nc.psum_tensor("psQA", [128, 2048], F32) as psQA, \
             nc.psum_tensor("psQB", [128, 2048], F32) as psQB:
            pq = [psQA, psQB]

            def ldwq(f, aft):
                buf2 = wq[f % 2]
                for h in range(2):
                    src = Wqkv[h * 2048:(h + 1) * 2048, f * 128:(f + 1) * 128]
                    src = src.rearrange("(c p) f -> p c f", p=128)
                    dst = buf2[:, h * 2048:(h + 1) * 2048].rearrange(
                        "p (c f) -> p c f", f=128)
                    p.dma("gpsimd", dst, src, c_w, after=aft)
                    aft = ()
            ldwq(0, ((c_xn, xn_all),))
            mm_at_f = {}
            for f in range(16):
                buf = wq[f % 2]
                if f + 1 < 16:
                    ldwq(f + 1, ((c_mm, mm_at_f.get(f - 1, 0)),))
                wq_ld = c_w.val
                ps = pq[f % 2]
                for c in range(NCH):
                    for lg in range(4):
                        p.emit("tensor", lambda e, c=c, lg=lg, buf=buf, ps=ps: e.matmul(
                            ps[:, lg * 512:(lg + 1) * 512],
                            buf[:, c * 128:(c + 1) * 128],
                            xball[:, c * T + lg * 512:c * T + (lg + 1) * 512],
                            start=(c == 0), stop=(c == NCH - 1)),
                            after=((c_w, wq_ld), (c_xn, xn_all),
                                   (c_ev, c_ev.val)),
                            inc=(c_mm if (c == NCH - 1 and lg == 3) else None))
                mm_at_f[f] = c_mm.val
                rb = rot[f % 2]
                is_q = f < 8
                mm_done = c_mm.val
                war = ((c_kv, c_kv.val), (c_sq, c_sq.val))
                for lg in range(4):
                    if is_q:
                        p.emit("vector", lambda e, lg=lg, f=f, ps=ps, rb=rb: e.tensor_scalar(
                            rb[:, lg * 512:(lg + 1) * 512],
                            ps[:, lg * 512:(lg + 1) * 512],
                            bqcols[:, f:f + 1], 1.0 / 32.0, ALU.add, ALU.mult),
                            after=((c_mm, mm_done),) + war,
                            inc=(c_ev if lg == 3 else None))
                    else:
                        p.emit("vector", lambda e, lg=lg, f=f, ps=ps, rb=rb: e.tensor_scalar(
                            rb[:, lg * 512:(lg + 1) * 512],
                            ps[:, lg * 512:(lg + 1) * 512],
                            bqcols[:, f:f + 1], None, ALU.add),
                            after=((c_mm, mm_done),) + war,
                            inc=(c_ev if lg == 3 else None))
                dst = o_qT if is_q else o_kT
                r0 = (f % 8) * 128
                p.dma("sync", dst[r0:r0 + 128, :], rb[:, :], c_kv,
                      after=((c_ev, c_ev.val),))
                p.emit("scalar", lambda e, rb=rb: e.activation(
                    sqb[:, :], rb[:, :], AF.Square),
                    after=((c_ev, c_ev.val), (c_sq, c_sq.val)), inc=c_sq)
                acc = sqacc_q if is_q else sqacc_k
                p.emit("vector", lambda e, acc=acc: e.tensor_add(
                    acc[:, :], acc[:, :], sqb[:, :]),
                    after=((c_sq, c_sq.val), (c_gp, gp_sq)), inc=c_ev)

        with nc.psum_tensor("psVA", [128, 2048], F32) as psVA, \
             nc.psum_tensor("psVB", [128, 2048], F32) as psVB:
            pv = [psVA, psVB]
            for vg in range(4):
                aft = ((c_ev, c_ev.val), (c_mm, c_mm.val))
                for h in range(2):
                    src = Wqkv[h * 2048:(h + 1) * 2048,
                               2048 + vg * 256:2048 + (vg + 1) * 256]
                    src = src.rearrange("(c p) f -> p c f", p=128)
                    dst = wv[:, h * 16 * 256:(h + 1) * 16 * 256].rearrange(
                        "p (c f) -> p c f", f=256)
                    p.dma("gpsimd", dst, src, c_w, after=aft)
                for lt in range(16):
                    ri = 16 + vg * 16 + lt
                    ps = pv[lt % 2][:, ((lt // 2) % 4) * 512:((lt // 2) % 4) * 512 + 256]
                    for c in range(NCH):
                        p.emit("tensor", lambda e, lt=lt, c=c, ps=ps: e.matmul(
                            ps, xball[:, c * T + lt * 128:c * T + lt * 128 + 128],
                            wv[:, c * 256:(c + 1) * 256],
                            start=(c == 0), stop=False),
                            after=((c_w, c_w.val), (c_ev, c_ev.val)))
                    p.emit("tensor", lambda e, vg=vg, ps=ps: e.matmul(
                        ps, ones_row_f[:, :],
                        bvrow_f[0:1, vg * 256:(vg + 1) * 256],
                        start=False, stop=True), inc=c_mm)
                    rb = rot[ri % 2]
                    p.emit("vector", lambda e, ps=ps, rb=rb: e.tensor_copy(
                        rb[:, 0:256], ps),
                        after=((c_mm, c_mm.val), (c_kv, c_kv.val),
                               (c_vo, c_vo.val)), inc=c_ev)
                    p.dma("sync",
                          o_V[lt * 128:(lt + 1) * 128, vg * 256:(vg + 1) * 256],
                          rb[:, 0:256], c_vo, after=((c_ev, c_ev.val),))

        with nc.psum_tensor("psN", [128, 2048], F32) as psN:
            for idx, acc in enumerate((sqacc_q, sqacc_k)):
                p.emit("vector", lambda e, acc=acc: e.tensor_copy(
                    sqb[:, :], acc[:, :]),
                    after=((c_ev, c_ev.val), (c_mm, c_mm.val),
                           (c_sq, c_sq.val), (c_fin, c_fin.val)), inc=c_ev)
                for j in range(4):
                    p.emit("tensor", lambda e, j=j: e.matmul(
                        psN[0:1, j * 512:(j + 1) * 512], ones_col_bf[:, :],
                        sqb[:, j * 512:(j + 1) * 512],
                        start=True, stop=True),
                        after=((c_ev, c_ev.val), (c_fin, c_fin.val)),
                        inc=(c_mm if j == 3 else None))
                p.emit("vector", lambda e, idx=idx: e.reduce_max(
                    row16[0:1, idx:idx + 1], psN[0:1, 0:2048], axis=AXX),
                    after=((c_mm, c_mm.val),), inc=c_fin)
            p.emit("vector", lambda e: e.tensor_scalar(
                row16[0:1, 0:2], row16[0:1, 0:2], 1.05, None, ALU.mult),
                after=((c_fin, c_fin.val),), inc=c_fin)
            p.dma("sync", o_qm, row16[0:1, 0:1], c_out, after=((c_fin, c_fin.val),))
            p.dma("sync", o_km, row16[0:1, 1:2], c_out)

    if io is not None:
        io["kv_stored"] = (c_kv, c_kv.val)
        io["v_stored"] = (c_vo, c_vo.val)
    p.emit("sync", lambda e: e.nop(),
           after=((c_out, c_out.val), (c_kv, c_kv.val), (c_vo, c_vo.val)))
    if standalone:
        return p.build()
    return c_out



def build_l0b(upto="all", p=None, io=None):
    standalone = p is None
    if standalone:
        p = Prog()
    nc = p.nc
    xnT_in = _pin(p, io, "xnT", [D, T], BF16)
    qT_in = _pin(p, io, "qT", [DH, T], BF16)
    kTg = _pin(p, io, "kTg", [NCORES * DH, T], BF16)
    Vg = _pin(p, io, "Vg", [S, DH], BF16)
    qm_in = _pin(p, io, "qm", [1, 1], F32)
    kms_in = _pin(p, io, "kms", [NCORES, 1], F32)
    xT = _pin(p, io, "xT", [D, T], F32)
    WbT_in = _pin(p, io, "WbT", [D, D], F32)
    b_base = _pin(p, io, "b_base", [D], F32)
    WgtT_in = _pin(p, io, "WgtT", [D, 9], F32)
    b_thr = _pin(p, io, "b_thr", [1], F32)
    AT_in = _pin(p, io, "AT", [D, 128], F32)
    BmT_in = _pin(p, io, "BmT", [128, D], F32)
    Wo_in = _pin(p, io, "Wo", [DH, D], F32)
    bo_in = _pin(p, io, "bo", [D], F32)
    o_x2T = _pout(p, io, "x2T", [D, T], F32)

    wo_bf = p.dtmp("wo_bf", [DH, D], BF16)
    yffn_hbm = p.dtmp("yffn_hbm", [D, T], BF16)
    ot_hbm = p.dtmp("ot_hbm", [DH, T], BF16)

    ones_col_bf = p.sb("ones_col_bf", [128, 1], BF16)
    ones_row_f = p.sb("ones_row_f", [1, 512], F32)
    ones_row_bf = p.sb("ones_row_bf", [1, 512], BF16)
    identbf = p.sb("identbf", [128, 128], BF16)
    cells = p.sb("cells", [1, 16], F32)
    krow = p.sb("krow", [1, 8], F32)
    negc_col = p.sb("negc_col", [128, 1], F32)
    thrb_col = p.sb("thrb_col", [128, 1], F32)

    c_x = p.ctr("x")
    c_xs = p.ctr("xs")
    c_wb = p.ctr("wb")
    c_vb = p.ctr("vb")
    c_xe = p.ctr("xe")
    c_cast = p.ctr("cast")
    c_w = p.ctr("w")
    c_mm = p.ctr("mm")
    c_ev = p.ctr("ev")
    c_sc = p.ctr("sc")
    c_gp = p.ctr("gp")
    c_out = p.ctr("out")
    c_fin = p.ctr("fin")

    p.emit("gpsimd", lambda e: e.memset(ones_col_bf[:, :], 1.0))
    p.emit("gpsimd", lambda e: e.memset(ones_row_f[:, :], 1.0))
    p.emit("gpsimd", lambda e: e.memset(ones_row_bf[:, :], 1.0))

    p.emit("gpsimd", lambda e: e.memset(identbf[:, :], 0.0), inc=c_gp)
    p.emit("gpsimd", lambda e: e.affine_select(
        out=identbf[:, :], in_=identbf[:, :], compare_op=ALU.not_equal,
        fill=1.0, base=0, pattern=[[-1, 128]], channel_multiplier=1),
        after=((c_gp, c_gp.val),), inc=c_gp)

    shared_xn = False
    c_xnb = p.ctr("xnb")
    with nc.sbuf_tensor("xn", [128, NCH * T], BF16) as xn:
        for c in range(NCH):
            p.dma("sync" if c % 2 == 0 else "scalar",
                  xn[:, c * T:(c + 1) * T],
                  xnT_in[c * 128:(c + 1) * 128, :],
                  c_x if c % 2 == 0 else c_xnb)
        p.dma("sync", cells[0:1, 0:1], qm_in, c_x)
        p.dma("sync", krow[:, :], kms_in.rearrange("a b -> b a"), c_x)
        p.dma("sync", cells[0:1, 4:5], b_thr[None, :], c_x)
        cs_ld = c_x.val
        xn_ld = c_x.val
        if shared_xn:
            xnrdy = io["xn_ready"]
            xnrdy2 = ()
        else:
            xnrdy = (c_x, xn_ld)
            xnrdy2 = ((c_xnb, c_xnb.val),)
        p.emit("vector", lambda e: e.reduce_max(cells[0:1, 1:2], krow[:, :],
                                                axis=AXX),
               after=((c_x, cs_ld),), inc=c_fin)
        p.emit("scalar", lambda e: e.activation(cells[0:1, 0:1], cells[0:1, 0:1],
                                                AF.Ln),
               after=((c_fin, c_fin.val),), inc=c_sc)
        p.emit("scalar", lambda e: e.activation(cells[0:1, 1:2], cells[0:1, 1:2],
                                                AF.Ln), inc=c_sc)
        p.emit("vector", lambda e: e.tensor_add(cells[0:1, 2:3], cells[0:1, 0:1],
                                                cells[0:1, 1:2]),
               after=((c_sc, c_sc.val),), inc=c_fin)
        p.emit("scalar", lambda e: e.activation(cells[0:1, 2:3], cells[0:1, 2:3],
                                                AF.Exp, scale=0.5),
               after=((c_fin, c_fin.val),), inc=c_sc)
        p.emit("vector", lambda e: e.tensor_scalar(cells[0:1, 3:4], cells[0:1, 2:3],
                                                   -1.0, None, ALU.mult),
               after=((c_sc, c_sc.val),), inc=c_fin)
        with nc.psum_tensor("psC", [128, 16], F32) as psC:
            p.emit("tensor", lambda e: e.matmul(psC[:, 0:1], ones_row_f[0:1, 0:128],
                                                cells[0:1, 3:4], start=True,
                                                stop=False),
                   after=((c_fin, c_fin.val), (c_x, cs_ld), (c_gp, c_gp.val)),
                   inc=c_mm)
            p.emit("tensor", lambda e: e.matmul(psC[:, 1:2], ones_row_f[0:1, 0:128],
                                                cells[0:1, 4:5], start=False,
                                                stop=True), inc=c_mm)
            p.emit("vector", lambda e: e.tensor_copy(negc_col[:, :], psC[:, 0:1]),
                   after=((c_mm, c_mm.val),), inc=c_ev)
            p.emit("vector", lambda e: e.tensor_copy(thrb_col[:, :], psC[:, 1:2]),
                   inc=c_ev)
        cshift_done = c_ev.val

        hts_scope = nc.sbuf_tensor("hts", [128, T], BF16)
        hts = hts_scope.__enter__()
        bbrow_scope = nc.sbuf_tensor("bbrow2", [1, D], BF16)
        bbrow2 = bbrow_scope.__enter__()
        bm_scope = nc.sbuf_tensor("bm_sb", [128, D], BF16)
        bm_sb = bm_scope.__enter__()
        with nc.sbuf_tensor("wgtT", [128, NCH * 16], BF16) as wgtT, \
             nc.sbuf_tensor("wT_sb", [8, T], BF16) as wT_sb, \
             nc.sbuf_tensor("s8", [8, 128], BF16) as s8, \
             nc.sbuf_tensor("s8b", [8, 128], BF16) as s8b, \
             nc.sbuf_tensor("wrep2", [128, T], BF16) as wrep2, \
             nc.sbuf_tensor("atT", [128, NCH * 128], BF16) as atT, \
             nc.sbuf_tensor("gbuf", [128, 16], F32) as gbuf, \
             nc.sbuf_tensor("gsc", [128, 8], F32) as gsc, \
             nc.sbuf_tensor("wbf_t", [128, 8], BF16) as wbf_t:

            src = WgtT_in.rearrange("(c p) f -> p c f", p=128)
            dstv = wgtT[:, 0:NCH * 16].rearrange("p (c f) -> p c f", f=16)
            p.dma("gpsimd", dstv[:, :, 0:9], src, c_cast)
            srcA = AT_in.rearrange("(c p) f -> p c f", p=128)
            dstA = atT[:, :].rearrange("p (c f) -> p c f", f=128)
            p.dma("gpsimd", dstA, srcA, c_cast)
            p.dma("gpsimd", bm_sb[:, :], BmT_in[:, :], c_cast)
            p.dma("gpsimd", wo_bf[:, :], Wo_in[:, :], c_cast)
            wo_cast = c_cast.val
            p.dma("gpsimd", bbrow2[:, :], b_base[None, :], c_cast)
            cast0 = c_cast.val

            p.emit("gpsimd", lambda e: e.memset(s8[:, :], 1.0), inc=c_gp)
            p.emit("gpsimd", lambda e: e.affine_select(
                out=s8[:, :], in_=s8[:, :], compare_op=ALU.is_ge, fill=0.0,
                base=0, pattern=[[1, 128]], channel_multiplier=-16),
                after=((c_gp, c_gp.val),), inc=c_gp)
            p.emit("gpsimd", lambda e: e.memset(s8b[:, :], 1.0), inc=c_gp)
            p.emit("gpsimd", lambda e: e.affine_select(
                out=s8b[:, :], in_=s8b[:, :], compare_op=ALU.is_ge, fill=0.0,
                base=-16, pattern=[[1, 128]], channel_multiplier=-16),
                after=((c_gp, c_gp.val),), inc=c_gp)
            p.emit("gpsimd", lambda e: e.tensor_sub(s8[:, :], s8[:, :], s8b[:, :]),
                   after=((c_gp, c_gp.val),), inc=c_gp)
            s8_done = c_gp.val

            with nc.psum_tensor("psG", [128, 1024], F32) as psG, \
                 nc.psum_tensor("psG2", [128, 2048], F32) as psG2:
                with nc.psum_tensor("psW", [8, T], BF16) as psW:
                    for lt in range(16):
                        ps = psG[:, (lt % 2) * 512:(lt % 2) * 512 + 16]
                        for c in range(NCH):
                            p.emit("tensor", lambda e, lt=lt, c=c, ps=ps: e.matmul(
                                ps[:, 0:9],
                                xn[:, c * T + lt * 128:c * T + lt * 128 + 128],
                                wgtT[:, c * 16:c * 16 + 9],
                                start=(c == 0), stop=(c == NCH - 1)),
                                after=((c_ev, c_ev.val), xnrdy,
                                       (c_cast, cast0)) + xnrdy2,
                                inc=(c_mm if c == NCH - 1 else None))
                        mm_d = c_mm.val
                        p.emit("vector", lambda e, ps=ps: e.tensor_copy(
                            gbuf[:, 0:9], ps[:, 0:9]), after=((c_mm, mm_d),),
                            inc=c_ev)
                        p.emit("vector", lambda e: e.reduce_max(
                            gsc[:, 0:1], gbuf[:, 0:8], axis=AXX, negate=True),
                            inc=c_ev)
                        p.emit("scalar", lambda e: e.activation(
                            gbuf[:, 0:8], gbuf[:, 0:8], AF.Exp, bias=gsc[:, 0:1]),
                            after=((c_ev, c_ev.val),), inc=c_sc)
                        p.emit("vector", lambda e: e.reduce_sum(
                            gsc[:, 1:2], gbuf[:, 0:8], axis=AXX),
                            after=((c_sc, c_sc.val),), inc=c_ev)
                        p.emit("vector", lambda e: e.reciprocal(
                            gsc[:, 2:3], gsc[:, 1:2]), inc=c_ev)
                        p.emit("scalar", lambda e: e.activation(
                            gsc[:, 3:4], gbuf[:, 8:9], AF.Sigmoid,
                            bias=thrb_col[:, 0:1]),
                            after=((c_ev, cshift_done),), inc=c_sc)
                        p.emit("vector", lambda e: e.tensor_scalar(
                            gsc[:, 3:4], gsc[:, 3:4], 0.125, None, ALU.mult),
                            after=((c_sc, c_sc.val),), inc=c_ev)
                        p.emit("vector", lambda e: e.tensor_scalar(
                            gbuf[:, 0:8], gbuf[:, 0:8], gsc[:, 2:3], gsc[:, 3:4],
                            ALU.mult, ALU.subtract), inc=c_ev)
                        p.emit("vector", lambda e: e.tensor_scalar(
                            gbuf[:, 0:8], gbuf[:, 0:8], 0.0, None, ALU.max),
                            inc=c_ev)
                        p.emit("vector", lambda e: e.reduce_sum(
                            gsc[:, 4:5], gbuf[:, 0:8], axis=AXX), inc=c_ev)
                        p.emit("vector", lambda e: e.tensor_scalar(
                            gsc[:, 5:6], gsc[:, 4:5], 0.0, None, ALU.is_equal),
                            inc=c_ev)
                        p.emit("vector", lambda e: e.tensor_add(
                            gsc[:, 4:5], gsc[:, 4:5], gsc[:, 5:6]), inc=c_ev)
                        p.emit("vector", lambda e: e.reciprocal(
                            gsc[:, 5:6], gsc[:, 4:5]), inc=c_ev)
                        p.emit("vector", lambda e: e.tensor_scalar(
                            wbf_t[:, 0:8], gbuf[:, 0:8], gsc[:, 5:6], None,
                            ALU.mult), inc=c_ev)
                        p.emit("tensor", lambda e, lt=lt: e.transpose(
                            psW[:, lt * 128:(lt + 1) * 128], wbf_t[:, 0:8],
                            identbf[:, :]),
                            after=((c_ev, c_ev.val),), inc=c_mm)
                        p.emit("vector", lambda e, lt=lt: e.tensor_copy(
                            wT_sb[:, lt * 128:(lt + 1) * 128],
                            psW[:, lt * 128:(lt + 1) * 128]),
                            after=((c_mm, c_mm.val),), inc=c_ev)

                for j in range(4):
                    p.emit("tensor", lambda e, j=j: e.matmul(
                        psG2[:, j * 512:(j + 1) * 512], s8[:, :],
                        wT_sb[:, j * 512:(j + 1) * 512], start=True, stop=True),
                        after=((c_ev, c_ev.val), (c_gp, s8_done)), inc=c_mm)
                p.emit("scalar", lambda e: e.activation(
                    wrep2[:, :], psG2[:, 0:T], AF.Copy, scale=2.0),
                    after=((c_mm, c_mm.val),), inc=c_sc)
                wrep_done = c_sc.val
                wrep2_keep = wrep2
                s8_keep = s8
                wT_keep = wT_sb
                wbf_keep = wbf_t
                gsc_keep = gsc
                gbuf_keep = gbuf

                for c in range(NCH):
                    for lg in range(4):
                        p.emit("tensor", lambda e, c=c, lg=lg: e.matmul(
                            psG2[:, lg * 512:(lg + 1) * 512],
                            atT[:, c * 128:(c + 1) * 128],
                            xn[:, c * T + lg * 512:c * T + (lg + 1) * 512],
                            start=(c == 0), stop=(c == NCH - 1)),
                            after=((c_cast, cast0), (c_sc, wrep_done)),
                            inc=(c_mm if (c == NCH - 1 and lg == 3) else None))
                p.emit("vector", lambda e: e.tensor_copy(hts[:, :], psG2[:, 0:T]),
                       after=((c_mm, c_mm.val),), inc=c_ev)
                p.emit("vector", lambda e: e.tensor_mul(
                    hts[:, :], hts[:, :], wrep2[:, :]), inc=c_ev)
            hw_done = c_ev.val




        with nc.sbuf_tensor("wbT0", [128, NCH * 256], BF16) as wbT0, \
             nc.sbuf_tensor("wbT1", [128, NCH * 256], BF16) as wbT1, \
             nc.sbuf_tensor("evb0", [128, 512], BF16) as evb0, \
             nc.sbuf_tensor("evb1", [128, 512], BF16) as evb1:
            wbT = [wbT0, wbT1]
            evb = [evb0, evb1]
            bbrow = bbrow2
            guard = {}
            with nc.psum_tensor("psF0", [128, 2048], F32) as psF0, \
                 nc.psum_tensor("psF1", [128, 2048], F32) as psF1:
                psF = [psF0, psF1]
                import os as _os
                n_og = int(_os.environ.get("L0B_OGS", "16"))

                def ldwb(og, aft):
                    buf2 = wbT[og % 2]
                    for h in range(2):
                        srcW = WbT_in[h * 2048:(h + 1) * 2048,
                                      og * 256:(og + 1) * 256]
                        srcW = srcW.rearrange("(c p) f -> p c f", p=128)
                        dstW = buf2[:, h * 16 * 256:
                                    (h + 1) * 16 * 256].rearrange(
                            "p (c f) -> p c f", f=256)
                        p.dma("gpsimd", dstW, srcW, c_wb, after=aft)
                        aft = ()
                if n_og and upto != "prep":
                    ldwb(0, ((c_ev, hw_done),))
                mm_at_og = {}
                for og in range(n_og if upto != "prep" else 0):
                    buf = wbT[og % 2]
                    if og + 1 < n_og:
                        ldwb(og + 1, ((c_mm, mm_at_og.get(og - 1, 0)),))
                    w_ld2 = c_wb.val
                    for ot in range(2):
                        ps = psF[ot]
                        o_glob = og * 2 + ot
                        for dc in range(NCH):
                            for lg in range(4):
                                p.emit("tensor",
                                       lambda e, buf=buf, ps=ps, ot=ot, dc=dc,
                                       lg=lg: e.matmul(
                                           ps[:, lg * 512:(lg + 1) * 512],
                                           buf[:, dc * 256 + ot * 128:
                                               dc * 256 + (ot + 1) * 128],
                                           xn[:, dc * T + lg * 512:
                                              dc * T + (lg + 1) * 512],
                                           start=(dc == 0), stop=False),
                                       after=((c_wb, w_ld2),
                                              guard.get(("p", og, ot),
                                                        (c_ev, hw_done))))
                        for lg in range(4):
                            p.emit("tensor",
                                   lambda e, ps=ps, o_glob=o_glob, lg=lg:
                                   e.matmul(
                                       ps[:, lg * 512:(lg + 1) * 512],
                                       bm_sb[:, o_glob * 128:(o_glob + 1) * 128],
                                       hts[:, lg * 512:(lg + 1) * 512],
                                       start=False, stop=False),
                                   after=((c_cast, cast0), (c_ev, hw_done)))
                            p.emit("tensor",
                                   lambda e, ps=ps, o_glob=o_glob, lg=lg:
                                   e.matmul(
                                       ps[:, lg * 512:(lg + 1) * 512],
                                       bbrow[0:1, o_glob * 128:(o_glob + 1) * 128],
                                       ones_row_bf[0:1, 0:512],
                                       start=False, stop=True),
                                   after=((c_cast, cast0), (c_gp, 1)),
                                   inc=(c_mm if lg == 3 else None))
                        eb = evb[o_glob % 2]
                        mm_d = c_mm.val
                        for lg in range(4):
                            p.emit("vector",
                                   lambda e, ps=ps, eb=eb, lg=lg: e.tensor_copy(
                                       eb[:, :], ps[:, lg * 512:(lg + 1) * 512]),
                                   after=((c_mm, mm_d), (c_out, c_out.val)),
                                   inc=c_ev)
                            p.dma("sync",
                                  yffn_hbm[o_glob * 128:(o_glob + 1) * 128,
                                           lg * 512:(lg + 1) * 512],
                                  eb[:, :], c_out, after=((c_ev, c_ev.val),))
                        guard[("p", og + 1, ot)] = (c_ev, c_ev.val)
                    mm_at_og[og] = c_mm.val
        bm_scope.__exit__(None, None, None)
        bbrow_scope.__exit__(None, None, None)
        hts_scope.__exit__(None, None, None)
    ffn_mm = c_mm.val
    ffn_out = c_out.val

    if upto == "prep":
        p.dma("gpsimd", o_x2T[0:128, :], hts[:, :], c_cast,
              after=((c_ev, c_ev.val),))
        p.dma("gpsimd", o_x2T[128:256, :], bm_sb[:, 0:T], c_cast)
        p.dma("gpsimd", o_x2T[256:384, :], wrep2_keep[:, :], c_cast)
        p.dma("gpsimd", o_x2T[384:392, 0:128], s8_keep[:, :], c_cast)
        p.dma("gpsimd", o_x2T[392:400, :], wT_keep[:, :], c_cast)
        p.dma("gpsimd", o_x2T[400:528, 0:8], wbf_keep[:, :], c_cast)
        p.dma("gpsimd", o_x2T[528:656, 0:8], gsc_keep[:, :], c_cast)
        p.dma("gpsimd", o_x2T[656:784, 0:16], gbuf_keep[:, :], c_cast)
        p.emit("sync", lambda e: e.nop(),
               after=((c_ev, c_ev.val), (c_cast, c_cast.val), (c_out, c_out.val)))
        return p.build()
    if upto == "ffn2":
        p.emit("sync", lambda e: e.nop(), after=((c_out, ffn_out),))
        return p.build()
    if upto == "ffn":
        for i in range(8):
            p.dma("gpsimd", o_x2T[i * 512:(i + 1) * 512, :],
                  yffn_hbm[i * 512:(i + 1) * 512, :], c_cast,
                  after=((c_out, ffn_out),))
        p.emit("sync", lambda e: e.nop(), after=((c_cast, c_cast.val),))
        return p.build()

    HQ = 1024
    NSB = S // 2048
    with nc.sbuf_tensor("kb0", [128, 8 * 2048], BF16) as kb0, \
         nc.sbuf_tensor("kb1", [128, 8 * 2048], BF16) as kb1, \
         nc.sbuf_tensor("vb", [128, 16 * DH], BF16) as vb, \
         nc.sbuf_tensor("pb", [128, 16 * HQ], BF16) as pb, \
         nc.sbuf_tensor("otacc", [128, 8 * HQ], F32) as otacc, \
         nc.sbuf_tensor("qh", [128, 8 * HQ], BF16) as qh, \
         nc.sbuf_tensor("denrow", [1, 2 * HQ], F32) as denrow, \
         nc.sbuf_tensor("denb", [128, HQ], F32) as denb:
        kb = [kb0, kb1]
        obf = qh
        att_guard = ((c_mm, ffn_mm), (c_out, ffn_out), (c_ev, c_ev.val))

        with nc.psum_tensor("psP0", [128, 1024], F32) as psP0, \
             nc.psum_tensor("psP1", [128, 1024], F32) as psP1, \
             nc.psum_tensor("psD", [1, 1024], F32) as psD, \
             nc.psum_tensor("psDB", [128, 1024], F32) as psDB:
            psP = [psP0, psP1]
            for half in range(2):
                q0 = half * HQ
                for dc in range(8):
                    aft = att_guard if (half == 0 and dc == 0) else \
                        (((c_ev, c_ev.val), (c_out, c_out.val)) if dc == 0 else ())
                    p.dma("scalar", qh[:, dc * HQ:(dc + 1) * HQ],
                          qT_in[dc * 128:(dc + 1) * 128, q0:q0 + HQ], c_xs,
                          after=aft)
                qh_ld = c_xs.val
                p.emit("gpsimd", lambda e: e.memset(otacc[:, :], 0.0),
                       after=((c_ev, c_ev.val),), inc=c_gp)
                p.emit("gpsimd", lambda e: e.memset(denrow[0:1, 0:HQ], 0.0),
                       inc=c_gp)
                acc_z = c_gp.val

                for sb in range(NSB):
                    k0 = sb * 2048
                    kbuf = kb[sb % 2]
                    rank, kc0 = k0 // T, k0 % T
                    for dc in range(8):
                        p.dma("scalar", kbuf[:, dc * 2048:(dc + 1) * 2048],
                              kTg[rank * DH + dc * 128:
                                  rank * DH + (dc + 1) * 128, kc0:kc0 + 2048],
                              c_xs,
                              after=(((c_mm, c_mm.val),) if dc == 0 else ()))
                    k_ld = c_xs.val
                    for kt in range(16):
                        p.dma("scalar", vb[:, kt * DH:(kt + 1) * DH],
                              Vg[k0 + kt * 128:k0 + (kt + 1) * 128, :], c_vb,
                              after=(((c_mm, c_mm.val),) if kt == 0 else ()))
                    v_ld = c_vb.val

                    for kt in range(16):
                        ps = psP[kt % 2]
                        for dc in range(8):
                            for qg in range(2):
                                p.emit("tensor",
                                       lambda e, kbuf=kbuf, ps=ps, kt=kt, dc=dc,
                                       qg=qg: e.matmul(
                                           ps[:, qg * 512:(qg + 1) * 512],
                                           kbuf[:, dc * 2048 + kt * 128:
                                                dc * 2048 + (kt + 1) * 128],
                                           qh[:, dc * HQ + qg * 512:
                                              dc * HQ + (qg + 1) * 512],
                                           start=(dc == 0), stop=(dc == 7)),
                                       after=((c_xs, k_ld),
                                              (c_sc, c_sc.val),
                                              (c_ev, c_ev.val)),
                                       inc=(c_mm if (dc == 7 and qg == 1)
                                            else None))
                        p.emit("scalar", lambda e, ps=ps, kt=kt: e.activation(
                            pb[:, kt * HQ:(kt + 1) * HQ], ps[:, 0:HQ], AF.Exp,
                            bias=negc_col[:, 0:1]),
                            after=((c_mm, c_mm.val), (c_ev, cshift_done)),
                            inc=c_sc)
                    expall = c_sc.val

                    for dv in range(8):
                        ps = psP[dv % 2]
                        for kc in range(16):
                            for qg in range(2):
                                p.emit("tensor",
                                       lambda e, ps=ps, dv=dv, kc=kc, qg=qg:
                                       e.matmul(
                                           ps[:, qg * 512:(qg + 1) * 512],
                                           vb[:, kc * DH + dv * 128:
                                              kc * DH + (dv + 1) * 128],
                                           pb[:, kc * HQ + qg * 512:
                                              kc * HQ + (qg + 1) * 512],
                                           start=(kc == 0), stop=(kc == 15)),
                                       after=((c_vb, v_ld), (c_sc, expall),
                                              (c_ev, c_ev.val)),
                                       inc=(c_mm if (kc == 15 and qg == 1)
                                            else None))
                        p.emit("vector", lambda e, ps=ps, dv=dv: e.tensor_add(
                            otacc[:, dv * HQ:(dv + 1) * HQ],
                            otacc[:, dv * HQ:(dv + 1) * HQ], ps[:, 0:HQ]),
                            after=((c_mm, c_mm.val), (c_gp, acc_z)), inc=c_ev)
                    for kc in range(16):
                        for qg in range(2):
                            p.emit("tensor", lambda e, kc=kc, qg=qg: e.matmul(
                                psD[0:1, qg * 512:(qg + 1) * 512],
                                ones_col_bf[:, :],
                                pb[:, kc * HQ + qg * 512:
                                   kc * HQ + (qg + 1) * 512],
                                start=(kc == 0), stop=(kc == 15)),
                                after=((c_sc, expall), (c_ev, c_ev.val)),
                                inc=(c_mm if (kc == 15 and qg == 1) else None))
                    p.emit("vector", lambda e: e.tensor_add(
                        denrow[0:1, 0:HQ], denrow[0:1, 0:HQ], psD[0:1, 0:HQ]),
                        after=((c_mm, c_mm.val), (c_gp, acc_z)), inc=c_ev)

                p.emit("vector", lambda e: e.reciprocal(
                    denrow[0:1, HQ:2 * HQ], denrow[0:1, 0:HQ]),
                    after=((c_ev, c_ev.val),), inc=c_ev)
                for j in range(2):
                    p.emit("tensor", lambda e, j=j: e.matmul(
                        psDB[:, j * 512:(j + 1) * 512], ones_row_f[0:1, 0:128],
                        denrow[0:1, HQ + j * 512:HQ + (j + 1) * 512],
                        start=True, stop=True),
                        after=((c_ev, c_ev.val),), inc=c_mm)
                p.emit("vector", lambda e: e.tensor_copy(denb[:, :], psDB[:, 0:HQ]),
                       after=((c_mm, c_mm.val),), inc=c_ev)
                for dv in range(8):
                    p.emit("vector", lambda e, dv=dv: e.tensor_mul(
                        obf[:, dv * HQ:(dv + 1) * HQ],
                        otacc[:, dv * HQ:(dv + 1) * HQ], denb[:, :]),
                        after=((c_ev, c_ev.val), (c_out, c_out.val)), inc=c_ev)
                    p.dma("sync", ot_hbm[dv * 128:(dv + 1) * 128, q0:q0 + HQ],
                          obf[:, dv * HQ:(dv + 1) * HQ], c_out,
                          after=((c_ev, c_ev.val),))
    att_out = c_out.val

    if upto == "att":
        p.dma("gpsimd", o_x2T[0:DH, :], ot_hbm[:, :], c_cast,
              after=((c_out, att_out),))
        p.emit("sync", lambda e: e.nop(), after=((c_cast, c_cast.val),))
        return p.build()

    with nc.sbuf_tensor("wo_sb", [128, 8 * D], BF16) as wo_sb, \
         nc.sbuf_tensor("ob_sb", [128, 8 * T], BF16) as ob_sb, \
         nc.sbuf_tensor("borow", [1, D], F32) as borow, \
         nc.sbuf_tensor("yfb0", [128, 512], BF16) as yfb0, \
         nc.sbuf_tensor("yfb1", [128, 512], BF16) as yfb1, \
         nc.sbuf_tensor("xfb0", [128, 512], F32) as xfb0, \
         nc.sbuf_tensor("xfb1", [128, 512], F32) as xfb1, \
         nc.sbuf_tensor("etmp", [128, 512], F32) as etmp, \
         nc.sbuf_tensor("exout0", [128, 512], F32) as exout0, \
         nc.sbuf_tensor("exout1", [128, 512], F32) as exout1:
        yfb = [yfb0, yfb1]
        xfb = [xfb0, xfb1]
        exout = [exout0, exout1]
        p.dma("sync", borow[:, :], bo_in[None, :], c_w, after=((c_out, att_out),))
        for dc in range(8):
            p.dma("sync", wo_sb[:, dc * D:(dc + 1) * D],
                  wo_bf[dc * 128:(dc + 1) * 128, :], c_w,
                  after=((c_cast, cast0), (c_out, att_out)))
        for dc in range(8):
            p.dma("sync", ob_sb[:, dc * T:(dc + 1) * T],
                  ot_hbm[dc * 128:(dc + 1) * 128, :], c_w,
                  after=((c_out, att_out),))
        wo_ld = c_w.val
        guard2 = {}
        ev_at_add = {}

        def ldep(j, aft):
            ct2, lg2 = j // 4, j % 4
            i2 = j % 2
            p.dma("scalar", yfb[i2][:, :],
                  yffn_hbm[ct2 * 128:(ct2 + 1) * 128,
                           lg2 * 512:(lg2 + 1) * 512], c_xe, after=aft)
            p.dma("scalar", xfb[i2][:, :],
                  xT[ct2 * 128:(ct2 + 1) * 128, lg2 * 512:(lg2 + 1) * 512],
                  c_xe, after=())
        with nc.psum_tensor("psE0", [128, 2048], F32) as psE0, \
             nc.psum_tensor("psE1", [128, 2048], F32) as psE1:
            psE = [psE0, psE1]
            for ct in range(32):
                ps = psE[ct % 2]
                for dc in range(8):
                    for lg in range(4):
                        p.emit("tensor",
                               lambda e, ps=ps, ct=ct, dc=dc, lg=lg: e.matmul(
                                   ps[:, lg * 512:(lg + 1) * 512],
                                   wo_sb[:, dc * D + ct * 128:
                                         dc * D + (ct + 1) * 128],
                                   ob_sb[:, dc * T + lg * 512:
                                         dc * T + (lg + 1) * 512],
                                   start=(dc == 0), stop=False),
                               after=((c_w, wo_ld),
                                      guard2.get(("p", ct), (c_w, wo_ld))))
                for lg in range(4):
                    p.emit("tensor", lambda e, ps=ps, ct=ct, lg=lg: e.matmul(
                        ps[:, lg * 512:(lg + 1) * 512],
                        borow[0:1, ct * 128:(ct + 1) * 128],
                        ones_row_f[0:1, 0:512], start=False, stop=True),
                        after=((c_w, wo_ld),),
                        inc=(c_mm if lg == 3 else None))
                mm_d = c_mm.val
                for lg in range(4):
                    j = ct * 4 + lg
                    i = j % 2
                    if j == 0:
                        ldep(0, ((c_out, att_out),))
                    if j + 1 < 128:
                        ldep(j + 1, ((c_ev, ev_at_add.get(j - 1, 0)),))
                    ld = c_xe.val
                    p.emit("vector", lambda e, ps=ps, lg=lg, i=i: e.tensor_add(
                        etmp[:, :], ps[:, lg * 512:(lg + 1) * 512],
                        yfb[i][:, :]),
                        after=((c_mm, mm_d), (c_xe, ld)), inc=c_ev)
                    p.emit("vector", lambda e, i=i: e.tensor_add(
                        exout[i][:, :], etmp[:, :], xfb[i][:, :]),
                        after=((c_out, c_out.val),), inc=c_ev)
                    ev_at_add[j] = c_ev.val
                    p.dma("sync", o_x2T[ct * 128:(ct + 1) * 128,
                                        lg * 512:(lg + 1) * 512],
                          exout[i][:, :], c_out, after=((c_ev, c_ev.val),))
                guard2[("p", ct + 2)] = (c_ev, c_ev.val)

    p.emit("sync", lambda e: e.nop(), after=((c_out, c_out.val),))
    if standalone:
        return p.build()
    return c_out


G4 = 4
DIM = 512
DFF = 2048
PC_I = 56


def _wm(p, ones_f32_row, psw, aft):
    n = min(512, int(ones_f32_row.shape[1]))
    p.emit("tensor", lambda e: e.matmul(
        psw[:, 0:n], ones_f32_row[0:1, 0:128], ones_f32_row[0:1, 0:n],
        start=True, stop=True), after=aft)


def _pcol(i, name, g=0):
    off = {"ln1_w": 0, "ln1_b": 4, "phz_re": 8, "phz_im": 12, "lc_re": 16,
           "lc_im": 20, "sc_lin": 24, "ln2_w": 28, "ln2_b": 32, "b2": 36,
           "b1": 40}[name]
    return i * PC_I + off + g


PCOLS = PC_I * 4 + 40
PCOL_LNLW = PC_I * 4
PCOL_LNLB = PC_I * 4 + 4
PCOL_BOUT = PC_I * 4 + 8


class L1Env:

    def __init__(self, p, with_tables=True, with_rows=True):
        nc = p.nc
        self.p = p
        self.l1p = p.sb("l1p", [128, PCOLS], F32)
        self.ones_col_bf = p.sb("ones_col_bf", [128, 1], BF16)
        self.ones_row_f = p.sb("ones_row_f", [1, 512], F32)
        self.ones_row_bf = p.sb("ones_row_bf", [1, 512], BF16)
        if with_rows:
            self.rows = p.sb("rows", [1, 3 * T], F32)
        self.cells = p.sb("cells", [1, 16], F32)
        self.colsc = p.sb("colsc", [128, 24], F32)
        self.c_x = p.ctr("x")
        self.c_cast = p.ctr("cast")
        self.c_w = p.ctr("w")
        self.c_mm = p.ctr("mm")
        self.c_ev = p.ctr("ev")
        self.c_sc = p.ctr("sc")
        self.c_gp = p.ctr("gp")
        self.c_out = p.ctr("out")
        if with_tables:
            self.jt = p.sb("jt", [128, T], F32)
            p.emit("gpsimd", lambda e: e.iota(
                self.jt[:, :], pattern=[[1, T]], base=0, channel_multiplier=0,
                allow_small_or_imprecise_dtypes=True), inc=self.c_gp)
        p.emit("gpsimd", lambda e: e.memset(self.ones_col_bf[:, :], 1.0),
               inc=self.c_gp)
        p.emit("gpsimd", lambda e: e.memset(self.ones_row_f[:, :], 1.0),
               inc=self.c_gp)
        p.emit("gpsimd", lambda e: e.memset(self.ones_row_bf[:, :], 1.0),
               inc=self.c_gp)
        self.halfpi = p.sb("halfpi", [128, 1], F32)
        p.emit("gpsimd", lambda e: e.memset(self.halfpi[:, :], PI / 2),
               inc=self.c_gp)
        self.gp0 = self.c_gp.val


def emit_main_ln(p, env, xin_f32, out_bf, o_mp=None):
    nc = p.nc
    c_x, c_ev, c_sc, c_mm, c_fin = env.c_x, env.c_ev, env.c_sc, env.c_mm, p.ctr("lnf")
    c_xb = p.ctr("lnxb")
    row_a = env.rows[0:1, 0:T]
    row_b = env.rows[0:1, T:2 * T]
    row_c = env.rows[0:1, 2 * T:3 * T]
    eps_cell = env.cells[0:1, 8:9]
    p.emit("gpsimd", lambda e: e.memset(eps_cell, EPS), inc=env.c_gp)
    gp_eps = env.c_gp.val
    with nc.sbuf_tensor("lx0", [128, T], F32) as lx0, \
         nc.sbuf_tensor("lx1", [128, T], F32) as lx1, \
         nc.sbuf_tensor("lsq", [128, T], BF16) as lsq, \
         nc.sbuf_tensor("ltmp", [128, T], F32) as ltmp, \
         nc.sbuf_tensor("lba", [128, T], F32) as lba, \
         nc.sbuf_tensor("lbb", [128, T], F32) as lbb:
        xf = [lx0, lx1]
        with nc.psum_tensor("lpsA", [128, 2048], F32) as psA, \
             nc.psum_tensor("lpsB", [128, 2048], F32) as psB:
            for c in range(2):
                p.dma("sync", xf[c % 2][:, :], xin_f32[c * 128:(c + 1) * 128, :],
                      c_x)
            for c in range(NCH):
                buf = xf[c % 2]
                xld = c_x.val
                if c + 2 < NCH:
                    p.dma("sync", xf[(c + 2) % 2][:, :],
                          xin_f32[(c + 2) * 128:(c + 3) * 128, :], c_x,
                          after=((c_xb, c + 1),))
                p.emit("vector", lambda e, c=c, buf=buf: e.tensor_copy(
                    out_bf[:, c * T:(c + 1) * T], buf[:, :]),
                    after=((c_x, xld),), inc=c_xb)
                p.emit("scalar", lambda e, c=c: e.activation(
                    lsq[:, :], out_bf[:, c * T:(c + 1) * T], AF.Square),
                    after=((c_xb, c + 1), (env.c_mm, env.c_mm.val)), inc=c_sc)
                for j in range(4):
                    p.emit("tensor", lambda e, c=c, j=j: e.matmul(
                        psA[0:1, j * 512:(j + 1) * 512], env.ones_col_bf[:, :],
                        out_bf[:, c * T + j * 512:c * T + (j + 1) * 512],
                        start=(c == 0), stop=(c == NCH - 1)),
                        after=((c_xb, c + 1),), inc=c_mm)
                    p.emit("tensor", lambda e, c=c, j=j: e.matmul(
                        psB[0:1, j * 512:(j + 1) * 512], env.ones_col_bf[:, :],
                        lsq[:, j * 512:(j + 1) * 512],
                        start=(c == 0), stop=(c == NCH - 1)),
                        after=((c_sc, c_sc.val),), inc=c_mm)
            st_all = c_mm.val
            p.emit("vector", lambda e: e.tensor_scalar(
                row_a, psA[0:1, :], 1.0 / D, None, ALU.mult),
                after=((c_mm, st_all),), inc=c_fin)
            p.emit("vector", lambda e: e.tensor_scalar(
                row_b, psB[0:1, :], 1.0 / D, None, ALU.mult), inc=c_fin)
            p.emit("scalar", lambda e: e.activation(row_c, row_a, AF.Square),
                   after=((c_fin, c_fin.val),), inc=c_sc)
            p.emit("vector", lambda e: e.tensor_sub(row_b, row_b, row_c),
                   after=((c_sc, c_sc.val),), inc=c_fin)
            p.emit("scalar", lambda e: e.activation(
                row_b, row_b, AF.Ln, bias=eps_cell),
                after=((c_fin, c_fin.val), (env.c_gp, gp_eps)), inc=c_sc)
            p.emit("scalar", lambda e: e.activation(
                row_b, row_b, AF.Exp, scale=-0.5), inc=c_sc)
            p.emit("vector", lambda e: e.tensor_mul(row_c, row_a, row_b),
                   after=((c_sc, c_sc.val),), inc=c_fin)
            p.emit("vector", lambda e: e.tensor_scalar(
                row_c, row_c, -1.0, None, ALU.mult), inc=c_fin)
        fin_done = c_fin.val
        with nc.psum_tensor("lpsA2", [128, 2048], F32) as psA2, \
             nc.psum_tensor("lpsB2", [128, 2048], F32) as psB2:
            for j in range(4):
                p.emit("tensor", lambda e, j=j: e.matmul(
                    psA2[:, j * 512:(j + 1) * 512], env.ones_row_f[0:1, 0:128],
                    row_b[0:1, j * 512:(j + 1) * 512], start=True, stop=True),
                    after=((c_fin, fin_done),), inc=c_mm)
                p.emit("tensor", lambda e, j=j: e.matmul(
                    psB2[:, j * 512:(j + 1) * 512], env.ones_row_f[0:1, 0:128],
                    row_c[0:1, j * 512:(j + 1) * 512], start=True, stop=True),
                    inc=c_mm)
            p.emit("vector", lambda e: e.tensor_copy(lba[:, :], psA2[:, :]),
                   after=((c_mm, c_mm.val),), inc=c_ev)
            p.emit("vector", lambda e: e.tensor_copy(lbb[:, :], psB2[:, :]),
                   inc=c_ev)
            bc_done = c_ev.val
            for c in range(NCH):
                p.emit("vector", lambda e, c=c: e.tensor_mul(
                    ltmp[:, :], out_bf[:, c * T:(c + 1) * T], lba[:, :]),
                    after=((c_ev, bc_done),))
                p.emit("vector", lambda e, c=c: e.tensor_add(
                    out_bf[:, c * T:(c + 1) * T], ltmp[:, :], lbb[:, :]),
                    inc=c_ev)
                if o_mp is not None:
                    p.emit("vector", lambda e, c=c: e.reduce_sum(
                        o_mp[:, c:c + 1], out_bf[:, c * T:(c + 1) * T],
                        axis=AXX), inc=c_ev)
    return c_ev.val


def emit_inner_ln(p, env, h, ybuf, wcol, bcol, silu, tag):
    nc = p.nc
    c_mm, c_ev, c_sc = env.c_mm, env.c_ev, env.c_sc
    c_fin = p.ctr(f"ilnf{tag}")
    row_a = env.rows[0:1, 0:T]
    row_b = env.rows[0:1, T:2 * T]
    row_c = env.rows[0:1, 2 * T:3 * T]
    eps_cell = env.cells[0:1, 8:9]
    p.emit("gpsimd", lambda e: e.memset(eps_cell, EPS), inc=env.c_gp)
    gp_eps = env.c_gp.val
    with nc.sbuf_tensor(f"ihbf{tag}", [128, T], BF16) as hbf, \
         nc.sbuf_tensor(f"ihsq{tag}", [128, T], BF16) as hsq, \
         nc.sbuf_tensor(f"iba{tag}", [128, T], F32) as iba, \
         nc.sbuf_tensor(f"ibb{tag}", [128, T], F32) as ibb, \
         nc.sbuf_tensor(f"itmp{tag}", [128, T], F32) as itmp:
        with nc.psum_tensor(f"ipsA{tag}", [128, 2048], F32) as psA, \
             nc.psum_tensor(f"ipsB{tag}", [128, 2048], F32) as psB:
            for g in range(G4):
                p.emit("vector", lambda e, g=g: e.tensor_copy(
                    hbf[:, :], h[:, g * T:(g + 1) * T]), inc=c_ev)
                p.emit("scalar", lambda e: e.activation(
                    hsq[:, :], hbf[:, :], AF.Square),
                    after=((c_ev, c_ev.val),), inc=c_sc)
                for j in range(4):
                    p.emit("tensor", lambda e, g=g, j=j: e.matmul(
                        psA[0:1, j * 512:(j + 1) * 512], env.ones_col_bf[:, :],
                        hbf[:, j * 512:(j + 1) * 512],
                        start=(g == 0), stop=(g == G4 - 1)),
                        after=((c_ev, c_ev.val),), inc=c_mm)
                    p.emit("tensor", lambda e, g=g, j=j: e.matmul(
                        psB[0:1, j * 512:(j + 1) * 512], env.ones_col_bf[:, :],
                        hsq[:, j * 512:(j + 1) * 512],
                        start=(g == 0), stop=(g == G4 - 1)),
                        after=((c_sc, c_sc.val),), inc=c_mm)
                p.emit("vector", lambda e: e.engine_nop(),
                       after=((c_mm, c_mm.val),), inc=c_ev)
            st_all = c_mm.val
            p.emit("vector", lambda e: e.tensor_scalar(
                row_a, psA[0:1, :], 1.0 / DIM, None, ALU.mult),
                after=((c_mm, st_all),), inc=c_fin)
            p.emit("vector", lambda e: e.tensor_scalar(
                row_b, psB[0:1, :], 1.0 / DIM, None, ALU.mult), inc=c_fin)
            p.emit("scalar", lambda e: e.activation(row_c, row_a, AF.Square),
                   after=((c_fin, c_fin.val),), inc=c_sc)
            p.emit("vector", lambda e: e.tensor_sub(row_b, row_b, row_c),
                   after=((c_sc, c_sc.val),), inc=c_fin)
            p.emit("scalar", lambda e: e.activation(
                row_b, row_b, AF.Ln, bias=eps_cell),
                after=((c_fin, c_fin.val), (env.c_gp, gp_eps)), inc=c_sc)
            p.emit("scalar", lambda e: e.activation(
                row_b, row_b, AF.Exp, scale=-0.5), inc=c_sc)
            p.emit("vector", lambda e: e.tensor_mul(row_c, row_a, row_b),
                   after=((c_sc, c_sc.val),), inc=c_fin)
            p.emit("vector", lambda e: e.tensor_scalar(
                row_c, row_c, -1.0, None, ALU.mult), inc=c_fin)
        fin_done = c_fin.val
        with nc.psum_tensor(f"ipsA2{tag}", [128, 2048], F32) as psA2, \
             nc.psum_tensor(f"ipsB2{tag}", [128, 2048], F32) as psB2:
            for j in range(4):
                p.emit("tensor", lambda e, j=j: e.matmul(
                    psA2[:, j * 512:(j + 1) * 512], env.ones_row_f[0:1, 0:128],
                    row_b[0:1, j * 512:(j + 1) * 512], start=True, stop=True),
                    after=((c_fin, fin_done),), inc=c_mm)
                p.emit("tensor", lambda e, j=j: e.matmul(
                    psB2[:, j * 512:(j + 1) * 512], env.ones_row_f[0:1, 0:128],
                    row_c[0:1, j * 512:(j + 1) * 512], start=True, stop=True),
                    inc=c_mm)
            p.emit("vector", lambda e: e.tensor_copy(iba[:, :], psA2[:, :]),
                   after=((c_mm, c_mm.val),), inc=c_ev)
            p.emit("vector", lambda e: e.tensor_copy(ibb[:, :], psB2[:, :]),
                   inc=c_ev)
            env.last_iba, env.last_ibb = iba, ibb
            for g in range(G4):
                p.emit("vector", lambda e, g=g: e.tensor_mul(
                    itmp[:, :], h[:, g * T:(g + 1) * T], iba[:, :]),
                    after=((c_ev, c_ev.val), (c_sc, c_sc.val)), inc=c_ev)
                p.emit("vector", lambda e, g=g: e.tensor_add(
                    itmp[:, :], itmp[:, :], ibb[:, :]), inc=c_ev)
                if wcol is not None:
                    p.emit("vector", lambda e, g=g: e.tensor_scalar(
                        itmp[:, :], itmp[:, :], wcol(g), bcol(g),
                        ALU.mult, ALU.add), inc=c_ev)
                if silu:
                    p.emit("scalar", lambda e, g=g: e.activation(
                        ybuf[:, g * T:(g + 1) * T], itmp[:, :], AF.Silu),
                        after=((c_ev, c_ev.val),), inc=c_sc)
                else:
                    p.emit("vector", lambda e, g=g: e.tensor_copy(
                        ybuf[:, g * T:(g + 1) * T], itmp[:, :]), inc=c_ev)
    return c_ev.val, c_sc.val


def emit_angle_tables(p, env, i, g, cth, sth, Ct, St, tag):
    c_ev, c_sc = env.c_ev, env.c_sc
    colsc = env.colsc
    pswctx = p.nps(f"pswt{tag}", [128, 512], F32)
    psw = pswctx.__enter__()
    wrdy = (env.c_gp, env.gp0)

    def W():
        _wm(p, env.ones_row_f, psw, ((c_ev, c_ev.val), wrdy))
    th = colsc[:, 8:9]
    thH = colsc[:, 9:10]
    thL = colsc[:, 10:11]
    tq = colsc[:, 11:12]
    uq = colsc[:, 12:13]
    vq = colsc[:, 13:14]
    mq = colsc[:, 14:15]
    p.emit("scalar", lambda e: e.activation(uq, cth, AF.Abs),
           after=((c_ev, c_ev.val),), inc=c_sc)
    p.emit("vector", lambda e: e.tensor_scalar(
        tq, uq, 1.0, None, ALU.add), after=((c_sc, c_sc.val),), inc=c_ev)
    p.emit("vector", lambda e: e.reciprocal(tq, tq), inc=c_ev)
    p.emit("vector", lambda e: e.tensor_mul(vq, sth, tq), inc=c_ev)
    p.emit("scalar", lambda e: e.activation(vq, vq, AF.Arctan),
           after=((c_ev, c_ev.val),), inc=c_sc)
    W()
    p.emit("vector", lambda e: e.tensor_scalar(
        mq, cth, 0.0, None, ALU.is_ge), after=((c_sc, c_sc.val),), inc=c_ev)
    p.emit("vector", lambda e: e.tensor_scalar(
        tq, mq, 2.0, -1.0, ALU.mult, ALU.add), inc=c_ev)
    p.emit("vector", lambda e: e.tensor_scalar(
        th, vq, 2.0, None, ALU.mult), inc=c_ev)
    p.emit("vector", lambda e: e.tensor_mul(th, th, tq), inc=c_ev)
    p.emit("vector", lambda e: e.tensor_scalar(
        mq, mq, -PI, PI, ALU.mult, ALU.add), inc=c_ev)
    p.emit("scalar", lambda e: e.activation(uq, sth, AF.Sign),
           after=((c_ev, c_ev.val),), inc=c_sc)
    p.emit("vector", lambda e: e.tensor_mul(mq, mq, uq),
           after=((c_sc, c_sc.val),), inc=c_ev)
    p.emit("vector", lambda e: e.tensor_add(th, th, mq), inc=c_ev)
    W()
    p.emit("vector", lambda e: e.tensor_scalar(
        thH, th, 256.0, MAGIC, ALU.mult, ALU.add), inc=c_ev)
    p.emit("vector", lambda e: e.tensor_scalar(
        thH, thH, MAGIC, 1.0 / 256.0, ALU.subtract, ALU.mult), inc=c_ev)
    p.emit("vector", lambda e: e.tensor_sub(thL, th, thH), inc=c_ev)
    W()
    p.emit("vector", lambda e: e.tensor_scalar(
        Ct[:, :], env.jt[:, :], env.rk_col, thH, ALU.add, ALU.mult),
        after=((env.c_gp, env.gp0),), inc=c_ev)
    p.emit("vector", lambda e: e.tensor_scalar(
        St[:, :], env.jt[:, :], env.rk_col, thL, ALU.add, ALU.mult), inc=c_ev)
    W()
    tm = env.tabm
    p.emit("vector", lambda e: e.tensor_scalar(
        tm[:, :], Ct[:, :], INV_2PI, None, ALU.mult), inc=c_ev)
    p.emit("vector", lambda e: e.scalar_tensor_tensor(
        tm[:, :], St[:, :], INV_2PI, tm[:, :], ALU.mult, ALU.add), inc=c_ev)
    p.emit("vector", lambda e: e.tensor_scalar(
        tm[:, :], tm[:, :], MAGIC, MAGIC, ALU.add, ALU.subtract), inc=c_ev)
    W()
    p.emit("vector", lambda e: e.scalar_tensor_tensor(
        Ct[:, :], tm[:, :], -PI2_HI, Ct[:, :], ALU.mult, ALU.add), inc=c_ev)
    p.emit("vector", lambda e: e.scalar_tensor_tensor(
        Ct[:, :], tm[:, :], -PI2_LO, Ct[:, :], ALU.mult, ALU.add), inc=c_ev)
    p.emit("vector", lambda e: e.tensor_add(Ct[:, :], Ct[:, :], St[:, :]),
           inc=c_ev)
    W()
    p.emit("vector", lambda e: e.tensor_scalar(
        Ct[:, :], Ct[:, :], -PI, PI, ALU.max, ALU.min), inc=c_ev)
    p.emit("scalar", lambda e: e.activation(St[:, :], Ct[:, :], AF.Sin),
           after=((c_ev, c_ev.val),), inc=c_sc)
    p.emit("scalar", lambda e: e.activation(tm[:, :], Ct[:, :], AF.Abs),
           inc=c_sc)
    p.emit("scalar", lambda e: e.activation(
        Ct[:, :], tm[:, :], AF.Sin, scale=-1.0, bias=env.halfpi[:, 0:1]),
        after=((env.c_gp, env.gp0),), inc=c_sc)
    _wm(p, env.ones_row_f, psw, ((c_sc, c_sc.val - 1), wrdy))
    pswctx.__exit__(None, None, None)
    return c_sc.val


def emit_chan_consts(p, env, i, g):
    c_ev, c_sc = env.c_ev, env.c_sc
    colsc = env.colsc
    pr = env.l1p[:, _pcol(i, "phz_re", g):_pcol(i, "phz_re", g) + 1]
    pi_ = env.l1p[:, _pcol(i, "phz_im", g):_pcol(i, "phz_im", g) + 1]
    r2, r, rho, lnrho = colsc[:, 0:1], colsc[:, 1:2], colsc[:, 2:3], colsc[:, 3:4]
    cth, sth, invr, ln2k = colsc[:, 4:5], colsc[:, 5:6], colsc[:, 6:7], colsc[:, 7:8]
    p.emit("vector", lambda e: e.tensor_mul(r2, pr, pr),
           after=(env.pl_wait,), inc=c_ev)
    p.emit("vector", lambda e: e.scalar_tensor_tensor(
        r2, pi_, pi_, r2, ALU.mult, ALU.add), inc=c_ev)
    p.emit("vector", lambda e: e.tensor_scalar(
        r2, r2, 1e-37, None, ALU.max), inc=c_ev)
    p.emit("scalar", lambda e: e.activation(r, r2, AF.Ln),
           after=((c_ev, c_ev.val),), inc=c_sc)
    p.emit("scalar", lambda e: e.activation(r, r, AF.Exp, scale=0.5), inc=c_sc)
    p.emit("scalar", lambda e: e.activation(rho, r, AF.Tanh), inc=c_sc)
    p.emit("scalar", lambda e: e.activation(lnrho, rho, AF.Ln), inc=c_sc)
    p.emit("vector", lambda e: e.reciprocal(invr, r),
           after=((c_sc, c_sc.val),), inc=c_ev)
    p.emit("vector", lambda e: e.tensor_mul(cth, pr, invr), inc=c_ev)
    p.emit("vector", lambda e: e.tensor_mul(sth, pi_, invr), inc=c_ev)
    p.emit("vector", lambda e: e.tensor_scalar(
        ln2k, lnrho, 2048.0, None, ALU.mult), inc=c_ev)
    return c_ev.val, c_sc.val


def emit_scan_part(p, env, i, ybuf, gre, gim, Ebuf):
    c_ev, c_sc = env.c_ev, env.c_sc
    CtL, StL = env.Ct, env.St
    pswctx = p.nps(f"pswsc{i}", [128, 512], F32)
    psws = pswctx.__enter__()

    def WS():
        _wm(p, env.ones_row_f, psws, ((c_ev, c_ev.val), (env.c_gp, env.gp0)))
    for g in range(G4):
        emit_chan_consts(p, env, i, g)
        rho = env.colsc[:, 2:3]
        emit_angle_tables(p, env, i, g, env.colsc[:, 4:5], env.colsc[:, 5:6],
                          CtL, StL, f"s{i}g{g}")
        ys = ybuf[:, g * T:(g + 1) * T]
        grs = gre[:, g * T:(g + 1) * T]
        gis = gim[:, g * T:(g + 1) * T]
        import os as _os2
        if _os2.environ.get("L1_NOROT"):
            p.emit("vector", lambda e, ys=ys, gis=gis: e.tensor_copy(gis, ys),
                   after=((c_sc, c_sc.val),), inc=c_ev)
            p.emit("vector", lambda e, ys=ys, grs=grs: e.tensor_copy(grs, ys),
                   inc=c_ev)
        else:
            p.emit("vector", lambda e, ys=ys, gis=gis: e.scalar_tensor_tensor(
                gis, StL[:, :], -1.0, ys, ALU.mult, ALU.mult),
                after=((c_sc, c_sc.val),), inc=c_ev)
            p.emit("vector", lambda e, ys=ys, grs=grs: e.tensor_mul(
                grs, ys, CtL[:, :]), inc=c_ev)
        WS()
        import os as _os
        if not _os.environ.get("L1_NOSCAN"):
            p.emit("vector", lambda e, grs=grs, rho=rho: e.tensor_tensor_scan(
                grs, rho.broadcast_to([128, T]), grs, 0.0, ALU.mult, ALU.add),
                inc=c_ev)
            WS()
            p.emit("vector", lambda e, gis=gis, rho=rho: e.tensor_tensor_scan(
                gis, rho.broadcast_to([128, T]), gis, 0.0, ALU.mult, ALU.add),
                inc=c_ev)
        p.emit("vector", lambda e, g=g, grs=grs: e.tensor_copy(
            Ebuf[:, 2 * g:2 * g + 1], grs[:, T - 1:T]), inc=c_ev)
        p.emit("vector", lambda e, g=g, gis=gis: e.tensor_copy(
            Ebuf[:, 2 * g + 1:2 * g + 2], gis[:, T - 1:T]), inc=c_ev)
        WS()
    pswctx.__exit__(None, None, None)
    return c_ev.val


def emit_combine_part(p, env, i, h, gre, gim, Eall, kvecb):
    c_ev, c_sc = env.c_ev, env.c_sc
    colsc = env.colsc
    coefs = env.coefs
    CtL, StL, tabmL, tabrL = env.Ct, env.St, env.tabm, env.tabr
    pswctx = p.nps(f"pswcb{i}", [128, 512], F32)
    pswc = pswctx.__enter__()

    def WC():
        _wm(p, env.ones_row_f, pswc, ((c_ev, c_ev.val), (env.c_gp, env.gp0)))
    for g in range(G4):
        emit_chan_consts(p, env, i, g)
        WC()
        rho, lnrho, cth, sth, ln2k = (colsc[:, 2:3], colsc[:, 3:4],
                                      colsc[:, 4:5], colsc[:, 5:6],
                                      colsc[:, 7:8])
        lcr = env.l1p[:, _pcol(i, "lc_re", g):_pcol(i, "lc_re", g) + 1]
        lci = env.l1p[:, _pcol(i, "lc_im", g):_pcol(i, "lc_im", g) + 1]
        scl = env.l1p[:, _pcol(i, "sc_lin", g):_pcol(i, "sc_lin", g) + 1]
        Gre, Gim, c0 = colsc[:, 16:17], colsc[:, 17:18], colsc[:, 18:19]
        t1 = colsc[:, 19:20]
        p.emit("vector", lambda e: e.tensor_scalar(
            coefs[:, :], kvecb[:, :], ln2k, None, ALU.mult),
            after=((env.c_x, env.p_ld), env.pl_wait), inc=c_ev)
        p.emit("scalar", lambda e: e.activation(coefs[:, :], coefs[:, :],
                                                AF.Exp),
               after=((c_ev, c_ev.val),), inc=c_sc)
        p.emit("vector", lambda e, g=g: e.tensor_mul(
            env.ecol[:, 0:8], coefs[:, :], Eall[:, 2 * g, :]),
            after=((c_sc, c_sc.val),), inc=c_ev)
        p.emit("vector", lambda e: e.reduce_sum(Gre, env.ecol[:, 0:8],
                                                axis=AXX), inc=c_ev)
        p.emit("vector", lambda e, g=g: e.tensor_mul(
            env.ecol[:, 0:8], coefs[:, :], Eall[:, 2 * g + 1, :]), inc=c_ev)
        p.emit("vector", lambda e: e.reduce_sum(Gim, env.ecol[:, 0:8],
                                                axis=AXX), inc=c_ev)
        WC()
        p.emit("vector", lambda e: e.tensor_scalar(
            c0, ln2k, env.rkn_col, None, ALU.mult), inc=c_ev)
        p.emit("scalar", lambda e: e.activation(c0, c0, AF.Exp),
               after=((c_ev, c_ev.val),), inc=c_sc)
        t2 = colsc[:, 20:21]
        p.emit("vector", lambda e, lcr=lcr: e.tensor_mul(t1, lcr, cth),
               after=((c_sc, c_sc.val),), inc=c_ev)
        p.emit("vector", lambda e, lci=lci: e.tensor_mul(t2, lci, sth),
               inc=c_ev)
        p.emit("vector", lambda e: e.tensor_sub(t1, t1, t2), inc=c_ev)
        p.emit("vector", lambda e: e.scalar_tensor_tensor(
            Gre, t1, c0, Gre, ALU.mult, ALU.add), inc=c_ev)
        p.emit("vector", lambda e, lcr=lcr: e.tensor_mul(t1, lcr, sth),
               inc=c_ev)
        p.emit("vector", lambda e, lci=lci: e.scalar_tensor_tensor(
            t1, lci, cth, t1, ALU.mult, ALU.add), inc=c_ev)
        p.emit("vector", lambda e: e.scalar_tensor_tensor(
            Gim, t1, c0, Gim, ALU.mult, ALU.add), inc=c_ev)
        emit_angle_tables(p, env, i, g, cth, sth, CtL, StL, f"c{i}g{g}")
        rp = tabrL
        p.emit("vector", lambda e: e.tensor_scalar(
            rp[:, :], env.jt[:, :], 1.0, lnrho, ALU.add, ALU.mult),
            after=((c_sc, c_sc.val),), inc=c_ev)
        p.emit("scalar", lambda e: e.activation(rp[:, :], rp[:, :], AF.Exp),
               after=((c_ev, c_ev.val),), inc=c_sc)
        grs = gre[:, g * T:(g + 1) * T]
        gis = gim[:, g * T:(g + 1) * T]
        hs = h[:, g * T:(g + 1) * T]
        p.emit("vector", lambda e, grs=grs: e.scalar_tensor_tensor(
            grs, rp[:, :], Gre, grs, ALU.mult, ALU.add),
            after=((c_sc, c_sc.val),), inc=c_ev)
        p.emit("vector", lambda e, gis=gis: e.scalar_tensor_tensor(
            gis, rp[:, :], Gim, gis, ALU.mult, ALU.add), inc=c_ev)
        WC()
        p.emit("vector", lambda e, grs=grs: e.tensor_mul(
            rp[:, :], grs, CtL[:, :]), inc=c_ev)
        p.emit("vector", lambda e, gis=gis: e.tensor_mul(
            tabmL[:, :], gis, StL[:, :]), inc=c_ev)
        WC()
        p.emit("vector", lambda e: e.tensor_sub(
            rp[:, :], rp[:, :], tabmL[:, :]), inc=c_ev)
        p.emit("vector", lambda e, hs=hs, scl=scl: e.scalar_tensor_tensor(
            hs, rp[:, :], scl, hs, ALU.mult, ALU.add), inc=c_ev)
        WC()
    pswctx.__exit__(None, None, None)
    return c_ev.val


def emit_ffn_part(p, env, i, h, W1T_in, W2T_in, gate=()):
    nc = p.nc
    c_ev, c_sc, c_mm, c_cast = env.c_ev, env.c_sc, env.c_mm, env.c_cast
    with nc.sbuf_tensor(f"fy2{i}", [128, G4 * T], BF16) as y2bf:
        wc = lambda g: env.l1p[:, _pcol(i, "ln2_w", g):_pcol(i, "ln2_w", g) + 1]
        bc = lambda g: env.l1p[:, _pcol(i, "ln2_b", g):_pcol(i, "ln2_b", g) + 1]
        emit_inner_ln(p, env, h, y2bf, wc, bc, False, f"f{i}")
        ln_done = env.c_ev.val
        ffn_scope = ExitStack()
        w1t = ffn_scope.enter_context(
            nc.sbuf_tensor(f"fw1{i}", [128, G4 * T], BF16))
        zbf = ffn_scope.enter_context(
            nc.sbuf_tensor(f"fzb{i}", [128, 16 * T], BF16))
        w2t = ffn_scope.enter_context(
            nc.sbuf_tensor(f"fw2{i}", [128, 16 * 512], BF16))
        c_w1 = p.ctr(f"w1c{i}")
        c_w2c = p.ctr(f"w2c{i}")
        ln_gate = ((c_ev, ln_done), (c_sc, c_sc.val))
        src = W1T_in[i].rearrange("(c p) f -> p c f", p=128)
        dst = w1t[:, :].rearrange("p (c f) -> p c f", f=DFF)
        p.dma("gpsimd", dst, src, c_w1, after=ln_gate)
        src2 = W2T_in[i].rearrange("(c p) f -> p c f", p=128)
        dst2 = w2t[:, :].rearrange("p (c f) -> p c f", f=512)
        p.dma("gpsimd", dst2, src2, c_w2c, after=ln_gate)
        w1_ld = c_w1.val
        with nc.psum_tensor(f"fps0{i}", [128, 2048], F32) as ps0, \
             nc.psum_tensor(f"fps1{i}", [128, 2048], F32) as ps1:
            pps = [ps0, ps1]
            for ft in range(16):
                ps = pps[ft % 2]
                for dc in range(G4):
                    for lg in range(4):
                        p.emit("tensor", lambda e, ft=ft, dc=dc, lg=lg, ps=ps:
                               e.matmul(
                                   ps[:, lg * 512:(lg + 1) * 512],
                                   w1t[:, dc * DFF + ft * 128:
                                       dc * DFF + (ft + 1) * 128],
                                   y2bf[:, dc * T + lg * 512:
                                        dc * T + (lg + 1) * 512],
                                   start=(dc == 0), stop=(dc == G4 - 1)),
                               after=((c_w1, w1_ld), (c_ev, c_ev.val),
                                      (c_sc, c_sc.val)),
                               inc=(c_mm if (dc == G4 - 1 and lg == 3)
                                    else None))
                mm_d = c_mm.val
                b1c = env.l1p[:, _pcol(i, "b1") + ft:_pcol(i, "b1") + ft + 1]
                for lg in range(4):
                    p.emit("scalar", lambda e, ft=ft, lg=lg, ps=ps, b1c=b1c:
                           e.activation(
                               zbf[:, ft * T + lg * 512:ft * T + (lg + 1) * 512],
                               ps[:, lg * 512:(lg + 1) * 512], AF.Silu,
                               bias=b1c),
                           after=((c_mm, mm_d),), inc=c_sc)
            w2_ld = c_w2c.val
            z_done = c_sc.val
            for ct in range(G4):
                ps = pps[ct % 2]
                for fc in range(16):
                    for lg in range(4):
                        p.emit("tensor", lambda e, ct=ct, fc=fc, lg=lg, ps=ps:
                               e.matmul(
                                   ps[:, lg * 512:(lg + 1) * 512],
                                   w2t[:, fc * 512 + ct * 128:
                                       fc * 512 + (ct + 1) * 128],
                                   zbf[:, fc * T + lg * 512:
                                       fc * T + (lg + 1) * 512],
                                   start=(fc == 0), stop=(fc == 15)),
                               after=((c_w2c, w2_ld), (c_sc, z_done),
                                      (c_ev, c_ev.val)),
                               inc=(c_mm if (fc == 15 and lg == 3) else None))
                mm_d = c_mm.val
                b2c = env.l1p[:, _pcol(i, "b2", ct):_pcol(i, "b2", ct) + 1]
                for lg in range(4):
                    p.emit("vector", lambda e, ct=ct, lg=lg, ps=ps, b2c=b2c:
                           e.scalar_tensor_tensor(
                               h[:, ct * T + lg * 512:ct * T + (lg + 1) * 512],
                               ps[:, lg * 512:(lg + 1) * 512], b2c,
                               h[:, ct * T + lg * 512:ct * T + (lg + 1) * 512],
                               ALU.add, ALU.add),
                           after=((c_mm, mm_d),), inc=c_ev)
        ffn_scope.close()
    return c_ev.val


def _l1_common_inputs(p):
    ins = {}
    ins["l1p_in"] = p.din("l1p_in", [128, PCOLS], F32)
    ins["kvec"] = p.din("kvec", [1, 8], F32)
    ins["rkc"] = p.din("rkc", [1, 2], F32)
    return ins


def _l1_env_setup(p, env, ins):
    nc = p.nc
    c_pl = p.ctr("pl")
    env.krow_sb = p.sb("krow_sb2", [1, 8], F32)
    p.dma("sync", env.krow_sb[0:1, :], ins["kvec"], c_pl)
    p.dma("sync", env.l1p[:, :], ins["l1p_in"], c_pl)
    p.dma("sync", env.cells[0:1, 0:2], ins["rkc"], c_pl)
    env.pl_wait = (c_pl, c_pl.val)
    env.rkcols = p.sb("rkcols", [128, 2], F32)
    env.kvecb = p.sb("kvecb", [128, 8], F32)
    env.ecol = p.sb("ecol", [128, 8], F32)
    env.coefs = p.sb("coefs", [128, 8], F32)
    with nc.psum_tensor("psRK", [128, 16], F32) as psRK:
        p.emit("tensor", lambda e: e.matmul(
            psRK[:, 0:2], env.ones_row_f[0:1, 0:128], env.cells[0:1, 0:2],
            start=True, stop=False),
            after=(env.pl_wait, (env.c_gp, env.gp0)), inc=env.c_mm)
        p.emit("tensor", lambda e: e.matmul(
            psRK[:, 2:10], env.ones_row_f[0:1, 0:128],
            env.krow_sb[0:1, 0:8], start=False, stop=True),
            inc=env.c_mm)
        p.emit("vector", lambda e: e.tensor_copy(env.rkcols[:, :], psRK[:, 0:2]),
               after=((env.c_mm, env.c_mm.val),), inc=env.c_ev)
        p.emit("vector", lambda e: e.tensor_copy(env.kvecb[:, :], psRK[:, 2:10]),
               inc=env.c_ev)
    env.rk_col = env.rkcols[:, 0:1]
    env.rkn_col = env.rkcols[:, 1:2]
    env.setup_ev = env.c_ev.val


def build_l1a(p=None, io=None, env=None):
    standalone = p is None
    if standalone:
        p = Prog()
    nc = p.nc
    x2T = _pin(p, io, "x2T", [D, T], F32)
    WinT_in = _pin(p, io, "WinT", [D, DIM], F32)
    bin_in = _pin(p, io, "bin", [DIM], F32)
    if io and "l1p_in" in io:
        ins = {k: io[k] for k in ("l1p_in", "kvec", "rkc")}
    else:
        ins = _l1_common_inputs(p)
    o_xn2T = _pout(p, io, "xn2T", [D, T], BF16)
    o_mp = _pout(p, io, "mp", [128, NCH], F32)
    if not (io and "h_sb" in io):
        o_hT = _pout(p, io, "hT", [DIM, T], F32)
        o_gre = _pout(p, io, "gre", [DIM, T], F32)
    o_gim = _pout(p, io, "gim", [DIM, T], F32)
    o_E = _pout(p, io, "E", [128, 8], F32)

    if env is None:
        env = L1Env(p)
        _l1_env_setup(p, env, ins)
    res_sb = bool(io) and "h_sb" in io
    mp_sb = p.sb("mp_sb", [128, NCH], F32)
    if res_sb:
        h, gre = io["h_sb"], io["gre_sb"]
        gim = p.sb("gim_s", [128, G4 * T], F32)
    else:
        h = p.sb("h", [128, G4 * T], F32)
        gre = p.sb("gre_s", [128, G4 * T], F32)
        gim = p.sb("gim_s", [128, G4 * T], F32)
    Ebuf = p.sb("Ebuf", [128, 8], F32)

    emit_main_ln_stream(p, env, x2T, o_xn2T, mp_sb)
    p.dma("sync", o_mp, mp_sb[:, :], env.c_out,
          after=((env.c_ev, env.c_ev.val),))
    ln_out = env.c_out.val

    with nc.sbuf_tensor("wint", [128, NCH * 512], BF16) as wint, \
         nc.sbuf_tensor("binr", [1, DIM], BF16) as binr, \
         nc.sbuf_tensor("xnb0", [128, T], BF16) as xnb0, \
         nc.sbuf_tensor("xnb1", [128, T], BF16) as xnb1:
        xnb = [xnb0, xnb1]
        src = WinT_in.rearrange("(c p) f -> p c f", p=128)
        dst = wint[:, :].rearrange("p (c f) -> p c f", f=DIM)
        p.dma("gpsimd", dst, src, env.c_cast,
              after=((env.c_out, ln_out), (env.c_ev, env.c_ev.val)))
        p.dma("gpsimd", binr[0:1, :], bin_in[None, :], env.c_cast)
        w_ld = env.c_cast.val
        c_xn2 = p.ctr("xn2st")
        mm_at = {}
        with nc.psum_tensor("psH0", [128, 2048], F32) as psH0, \
             nc.psum_tensor("psH1", [128, 2048], F32) as psH1:
            pps = [psH0, psH1]
            for dt_ in range(G4):
                ps = pps[dt_ % 2]
                for vc in range(NCH):
                    k = dt_ * NCH + vc
                    if k == 0:
                        p.dma("scalar", xnb[0][:, :], o_xn2T[0:128, :],
                              c_xn2, after=((env.c_out, ln_out),))
                    if k + 1 < G4 * NCH:
                        vc2 = (k + 1) % NCH
                        p.dma("scalar", xnb[(k + 1) % 2][:, :],
                              o_xn2T[vc2 * 128:(vc2 + 1) * 128, :], c_xn2,
                              after=((env.c_mm, mm_at.get(k - 1, 0)),))
                    x_ld = c_xn2.val
                    for lg in range(4):
                        p.emit("tensor",
                               lambda e, dt_=dt_, vc=vc, lg=lg, ps=ps, k=k:
                               e.matmul(
                                   ps[:, lg * 512:(lg + 1) * 512],
                                   wint[:, vc * 512 + dt_ * 128:
                                        vc * 512 + (dt_ + 1) * 128],
                                   xnb[k % 2][:, lg * 512:(lg + 1) * 512],
                                   start=(vc == 0), stop=False),
                               after=((env.c_cast, w_ld), (c_xn2, x_ld),
                                      (env.c_ev, env.c_ev.val)),
                               inc=(env.c_mm if lg == 3 else None))
                    mm_at[k] = env.c_mm.val
                for lg in range(4):
                    p.emit("tensor", lambda e, dt_=dt_, lg=lg, ps=ps:
                           e.matmul(
                               ps[:, lg * 512:(lg + 1) * 512],
                               binr[0:1, dt_ * 128:(dt_ + 1) * 128],
                               env.ones_row_bf[0:1, 0:512],
                               start=False, stop=True),
                           after=((env.c_cast, w_ld), (env.c_gp, env.gp0)),
                           inc=(env.c_mm if lg == 3 else None))
                mm_d = env.c_mm.val
                mm_at[dt_ * NCH + NCH - 1] = mm_d
                for lg in range(4):
                    p.emit("vector", lambda e, dt_=dt_, lg=lg, ps=ps:
                           e.tensor_copy(
                               h[:, dt_ * T + lg * 512:dt_ * T + (lg + 1) * 512],
                               ps[:, lg * 512:(lg + 1) * 512]),
                           after=((env.c_mm, mm_d),), inc=env.c_ev)
    import os as _os3
    l1a_upto = _os3.environ.get("L1A_UPTO", "all")
    with nc.sbuf_tensor("Ct", [128, T], F32) as Ct, \
         nc.sbuf_tensor("St", [128, T], F32) as St, \
         nc.sbuf_tensor("tabm", [128, T], F32) as tabm:
        env.Ct, env.St, env.tabm = Ct, St, tabm
        ybuf = gre
        if l1a_upto in ("ln1", "scan", "all"):
            wc = lambda g: env.l1p[:, _pcol(0, "ln1_w", g):
                                   _pcol(0, "ln1_w", g) + 1]
            bc = lambda g: env.l1p[:, _pcol(0, "ln1_b", g):
                                   _pcol(0, "ln1_b", g) + 1]
            emit_inner_ln(p, env, h, ybuf, wc, bc, True, "a0")
        if l1a_upto in ("scan", "all"):
            emit_scan_part(p, env, 0, ybuf, gre, gim, Ebuf)
        done = ((env.c_ev, env.c_ev.val), (env.c_sc, env.c_sc.val))
        for g in range(G4):
            if not res_sb:
                p.dma("sync", o_hT[g * 128:(g + 1) * 128, :],
                      h[:, g * T:(g + 1) * T], env.c_out, after=done)
                p.dma("sync", o_gre[g * 128:(g + 1) * 128, :],
                      gre[:, g * T:(g + 1) * T], env.c_out, after=done)
            p.dma("sync", o_gim[g * 128:(g + 1) * 128, :],
                  gim[:, g * T:(g + 1) * T], env.c_out, after=done)
        p.dma("sync", o_E, Ebuf[:, :], env.c_out, after=done)
    p.emit("sync", lambda e: e.nop(), after=((env.c_out, env.c_out.val),))
    if standalone:
        return p.build()
    return env.c_out


def build_l1s(i, last, p=None, io=None, env=None):
    standalone = p is None
    if standalone:
        p = Prog()
    nc = p.nc
    res_sb = bool(io) and "h_sb" in io
    if not res_sb:
        hT_in = _pin(p, io, "hT", [DIM, T], F32)
        gre_in = _pin(p, io, "gre", [DIM, T], F32)
    gim_in = _pin(p, io, "gim", [DIM, T], F32)
    eall_ag = bool(io) and "Eall_ag" in io
    if eall_ag:
        Eall_in = io["Eall_ag"]
    else:
        Eall_in = _pin(p, io, "Eall", [128, 8, 8], F32)
    W1T_in = _pin(p, io, "W1T", [4, DIM, DFF], F32)
    W2T_in = _pin(p, io, "W2T", [4, DFF, DIM], F32)
    if not (io and "l1p_in" in io) and env is None:
        ins = _l1_common_inputs(p)
    if last:
        WoutT_in = _pin(p, io, "WoutT", [DIM, D], F32)
        o_ya = _pout(p, io, "yaT", [D, T], BF16)
    else:
        if not res_sb:
            o_hT = _pout(p, io, "hT_o", [DIM, T], F32)
            o_gre = _pout(p, io, "gre_o", [DIM, T], F32)
        o_gim = _pout(p, io, "gim_o", [DIM, T], F32)
        o_E = _pout(p, io, "E", [128, 8], F32)

    if env is None:
        env = L1Env(p)
        _l1_env_setup(p, env, ins)
    if res_sb:
        h = io["h_sb"]
    else:
        h = p.sb("h", [128, G4 * T], F32)
        for g in range(G4):
            p.dma("sync", h[:, g * T:(g + 1) * T],
                  hT_in[g * 128:(g + 1) * 128, :], env.c_x)
    h_ld = env.c_x.val

    from contextlib import nullcontext as _nullctx2
    with (_nullctx2(io["gre_sb"]) if res_sb
          else p.nsb("gre1", [128, G4 * T], F32)) as gre1, \
         p.nsb("gim1", [128, G4 * T], F32) as gim1, \
         p.nsb("Eall_sb", [128, 8 * 8], F32) as Eall, \
         p.nsb("Ct1", [128, T], F32) as Ct1, \
         p.nsb("St1", [128, T], F32) as St1, \
         p.nsb("tabm1", [128, T], F32) as tabm1, \
         p.nsb("tabr1", [128, T], F32) as tabr1:
        env.Ct, env.St, env.tabm, env.tabr = Ct1, St1, tabm1, tabr1
        for g in range(G4):
            if not res_sb:
                p.dma("sync", gre1[:, g * T:(g + 1) * T],
                      gre_in[g * 128:(g + 1) * 128, :], env.c_x)
            p.dma("sync", gim1[:, g * T:(g + 1) * T],
                  gim_in[g * 128:(g + 1) * 128, :], env.c_x)
        if eall_ag:
            p.dma("sync",
                  Eall[:, :].rearrange("p (m a) -> p m a", a=8),
                  Eall_in.rearrange("(m p) a -> p m a", p=128), env.c_x)
            EallV = Eall.rearrange("p (m a) -> p a m", a=8)
        else:
            p.dma("sync", Eall[:, :], Eall_in.rearrange("p a b -> p (a b)"),
                  env.c_x)
            EallV = Eall.rearrange("p (a b) -> p a b", b=8)
        env.p_ld = env.c_x.val
        emit_combine_part(p, env, i, h, gre1, gim1, EallV, env.kvecb)
    comb_done = ((env.c_ev, env.c_ev.val), (env.c_sc, env.c_sc.val))

    import os as _os4
    if not _os4.environ.get("L1S_NOFFN"):
        emit_ffn_part(p, env, i, h, W1T_in, W2T_in, gate=comb_done)

    if not last:
        with (_nullctx2(io["gre_sb"]) if res_sb
              else p.nsb("gre2", [128, G4 * T], F32)) as gre2, \
             p.nsb("gim2", [128, G4 * T], F32) as gim2, \
             p.nsb("Ebuf", [128, 8], F32) as Ebuf, \
             p.nsb("Ct2", [128, T], F32) as Ct2, \
             p.nsb("St2", [128, T], F32) as St2, \
             p.nsb("tabm2", [128, T], F32) as tabm2:
            env.Ct, env.St, env.tabm = Ct2, St2, tabm2
            ybuf = gre2
            wc = lambda g: env.l1p[:, _pcol(i + 1, "ln1_w", g):
                                   _pcol(i + 1, "ln1_w", g) + 1]
            bc = lambda g: env.l1p[:, _pcol(i + 1, "ln1_b", g):
                                   _pcol(i + 1, "ln1_b", g) + 1]
            emit_inner_ln(p, env, h, ybuf, wc, bc, True, f"s{i}")
            emit_scan_part(p, env, i + 1, ybuf, gre2, gim2, Ebuf)
            done = ((env.c_ev, env.c_ev.val), (env.c_sc, env.c_sc.val))
            for g in range(G4):
                if not res_sb:
                    p.dma("sync", o_hT[g * 128:(g + 1) * 128, :],
                          h[:, g * T:(g + 1) * T], env.c_out, after=done)
                    p.dma("sync", o_gre[g * 128:(g + 1) * 128, :],
                          gre2[:, g * T:(g + 1) * T], env.c_out, after=done)
                p.dma("sync", o_gim[g * 128:(g + 1) * 128, :],
                      gim2[:, g * T:(g + 1) * T], env.c_out, after=done)
            p.dma("sync", o_E, Ebuf[:, :], env.c_out, after=done)
    else:
        with nc.sbuf_tensor("hFbf", [128, G4 * T], BF16) as hFbf:
            wc = lambda g: env.l1p[:, PCOL_LNLW + g:PCOL_LNLW + g + 1]
            bc = lambda g: env.l1p[:, PCOL_LNLB + g:PCOL_LNLB + g + 1]
            emit_inner_ln(p, env, h, hFbf, wc, bc, False, "fin")
            ln_done = env.c_ev.val
            with nc.sbuf_tensor("wout_sb", [128, G4 * D], BF16) as wout_sb, \
                 nc.sbuf_tensor("yev0", [128, 512], BF16) as yev0, \
                 nc.sbuf_tensor("yev1", [128, 512], BF16) as yev1:
                yev = [yev0, yev1]
                src = WoutT_in.rearrange("(c p) f -> p c f", p=128)
                dst = wout_sb[:, :].rearrange("p (c f) -> p c f", f=D)
                c_woc = p.ctr("woc")
                p.dma("gpsimd", dst, src, c_woc,
                      after=((env.c_ev, ln_done), (env.c_sc, env.c_sc.val)))
                w_ld = c_woc.val
                with nc.psum_tensor("psY0", [128, 2048], F32) as psY0, \
                     nc.psum_tensor("psY1", [128, 2048], F32) as psY1:
                    pps = [psY0, psY1]
                    ev_vt = {}
                    for vt in range(32):
                        ps = pps[vt % 2]
                        for dc in range(G4):
                            for lg in range(4):
                                p.emit("tensor",
                                       lambda e, vt=vt, dc=dc, lg=lg, ps=ps:
                                       e.matmul(
                                           ps[:, lg * 512:(lg + 1) * 512],
                                           wout_sb[:, dc * D + vt * 128:
                                                   dc * D + (vt + 1) * 128],
                                           hFbf[:, dc * T + lg * 512:
                                                dc * T + (lg + 1) * 512],
                                           start=(dc == 0),
                                           stop=(dc == G4 - 1)),
                                       after=((c_woc, w_ld),
                                              (env.c_ev,
                                               ev_vt.get(vt - 2, ln_done)),
                                              (env.c_sc, env.c_sc.val)),
                                       inc=(env.c_mm
                                            if (dc == G4 - 1 and lg == 3)
                                            else None))
                        mm_d = env.c_mm.val
                        bcol = env.l1p[:, PCOL_BOUT + vt:PCOL_BOUT + vt + 1]
                        for lg in range(4):
                            yv = yev[(vt * 4 + lg) % 2]
                            p.emit("vector",
                                   lambda e, lg=lg, ps=ps, bcol=bcol, yv=yv:
                                   e.tensor_scalar(
                                       yv[:, :],
                                       ps[:, lg * 512:(lg + 1) * 512],
                                       bcol, None, ALU.add),
                                   after=((env.c_mm, mm_d),
                                          (env.c_out, env.c_out.val)),
                                   inc=env.c_ev)
                            p.dma("sync",
                                  o_ya[vt * 128:(vt + 1) * 128,
                                       lg * 512:(lg + 1) * 512],
                                  yv[:, :], env.c_out,
                                  after=((env.c_ev, env.c_ev.val),))
                        ev_vt[vt] = env.c_ev.val
    p.emit("sync", lambda e: e.nop(), after=((env.c_out, env.c_out.val),))
    if standalone:
        return p.build()
    return env.c_out


def emit_main_ln_stream(p, env, xin_f32, o_xnT_dram, o_mp_sb):
    nc = p.nc
    c_x, c_ev, c_sc, c_mm = env.c_x, env.c_ev, env.c_sc, env.c_mm
    c_fin = p.ctr("slnf")
    c_xb = p.ctr("slnxb")
    row_a = env.rows[0:1, 0:T]
    row_b = env.rows[0:1, T:2 * T]
    row_c = env.rows[0:1, 2 * T:3 * T]
    eps_cell = env.cells[0:1, 8:9]
    p.emit("gpsimd", lambda e: e.memset(eps_cell, EPS), inc=env.c_gp)
    gp_eps = env.c_gp.val
    with nc.sbuf_tensor("sx0", [128, T], F32) as sx0, \
         nc.sbuf_tensor("sx1", [128, T], F32) as sx1, \
         nc.sbuf_tensor("sbf", [128, T], BF16) as sbf, \
         nc.sbuf_tensor("ssq", [128, T], BF16) as ssq, \
         nc.sbuf_tensor("so0", [128, T], BF16) as so0, \
         nc.sbuf_tensor("so1", [128, T], BF16) as so1, \
         nc.sbuf_tensor("stmp", [128, T], F32) as stmp, \
         nc.sbuf_tensor("sba", [128, T], F32) as sba, \
         nc.sbuf_tensor("sbb", [128, T], F32) as sbb:
        xf = [sx0, sx1]
        so = [so0, so1]
        with nc.psum_tensor("spsA", [128, 2048], F32) as psA, \
             nc.psum_tensor("spsB", [128, 2048], F32) as psB:
            for c in range(2):
                p.dma("sync", xf[c % 2][:, :],
                      xin_f32[c * 128:(c + 1) * 128, :], c_x)
            for c in range(NCH):
                buf = xf[c % 2]
                xld = c_x.val
                if c + 2 < NCH:
                    p.dma("sync", xf[(c + 2) % 2][:, :],
                          xin_f32[(c + 2) * 128:(c + 3) * 128, :], c_x,
                          after=((c_xb, c + 1),))
                p.emit("vector", lambda e, buf=buf: e.tensor_copy(
                    sbf[:, :], buf[:, :]),
                    after=((c_x, xld), (c_mm, c_mm.val)), inc=c_xb)
                p.emit("scalar", lambda e: e.activation(
                    ssq[:, :], sbf[:, :], AF.Square),
                    after=((c_xb, c + 1), (c_mm, c_mm.val)), inc=c_sc)
                for j in range(4):
                    p.emit("tensor", lambda e, c=c, j=j: e.matmul(
                        psA[0:1, j * 512:(j + 1) * 512], env.ones_col_bf[:, :],
                        sbf[:, j * 512:(j + 1) * 512],
                        start=(c == 0), stop=(c == NCH - 1)),
                        after=((c_xb, c + 1),), inc=c_mm)
                    p.emit("tensor", lambda e, c=c, j=j: e.matmul(
                        psB[0:1, j * 512:(j + 1) * 512], env.ones_col_bf[:, :],
                        ssq[:, j * 512:(j + 1) * 512],
                        start=(c == 0), stop=(c == NCH - 1)),
                        after=((c_sc, c_sc.val),), inc=c_mm)
            st_all = c_mm.val
            p.emit("vector", lambda e: e.tensor_scalar(
                row_a, psA[0:1, :], 1.0 / D, None, ALU.mult),
                after=((c_mm, st_all),), inc=c_fin)
            p.emit("vector", lambda e: e.tensor_scalar(
                row_b, psB[0:1, :], 1.0 / D, None, ALU.mult), inc=c_fin)
            p.emit("scalar", lambda e: e.activation(row_c, row_a, AF.Square),
                   after=((c_fin, c_fin.val),), inc=c_sc)
            p.emit("vector", lambda e: e.tensor_sub(row_b, row_b, row_c),
                   after=((c_sc, c_sc.val),), inc=c_fin)
            p.emit("scalar", lambda e: e.activation(
                row_b, row_b, AF.Ln, bias=eps_cell),
                after=((c_fin, c_fin.val), (env.c_gp, gp_eps)), inc=c_sc)
            p.emit("scalar", lambda e: e.activation(
                row_b, row_b, AF.Exp, scale=-0.5), inc=c_sc)
            p.emit("vector", lambda e: e.tensor_mul(row_c, row_a, row_b),
                   after=((c_sc, c_sc.val),), inc=c_fin)
            p.emit("vector", lambda e: e.tensor_scalar(
                row_c, row_c, -1.0, None, ALU.mult), inc=c_fin)
        fin_done = c_fin.val
        with nc.psum_tensor("spsA2", [128, 2048], F32) as psA2, \
             nc.psum_tensor("spsB2", [128, 2048], F32) as psB2:
            for j in range(4):
                p.emit("tensor", lambda e, j=j: e.matmul(
                    psA2[:, j * 512:(j + 1) * 512], env.ones_row_f[0:1, 0:128],
                    row_b[0:1, j * 512:(j + 1) * 512], start=True, stop=True),
                    after=((c_fin, fin_done),), inc=c_mm)
                p.emit("tensor", lambda e, j=j: e.matmul(
                    psB2[:, j * 512:(j + 1) * 512], env.ones_row_f[0:1, 0:128],
                    row_c[0:1, j * 512:(j + 1) * 512], start=True, stop=True),
                    inc=c_mm)
            p.emit("vector", lambda e: e.tensor_copy(sba[:, :], psA2[:, :]),
                   after=((c_mm, c_mm.val),), inc=c_ev)
            p.emit("vector", lambda e: e.tensor_copy(sbb[:, :], psB2[:, :]),
                   inc=c_ev)
        c_x2 = p.ctr("slnx2")
        c_xb2 = p.ctr("slnxb2")
        with p.nps("psw_sln", [128, 512], F32) as psw:
            for c in range(2):
                p.dma("scalar", xf[c % 2][:, :],
                      xin_f32[c * 128:(c + 1) * 128, :], c_x2,
                      after=((c_fin, fin_done),))
            for c in range(NCH):
                buf = xf[c % 2]
                ob = so[c % 2]
                xld = c_x2.val
                if c + 2 < NCH:
                    p.dma("scalar", xf[(c + 2) % 2][:, :],
                          xin_f32[(c + 2) * 128:(c + 3) * 128, :], c_x2,
                          after=((c_xb2, c + 1),))
                p.emit("vector", lambda e, buf=buf: e.tensor_mul(
                    stmp[:, :], buf[:, :], sba[:, :]),
                    after=((c_x2, xld), (c_ev, c_ev.val),
                           (env.c_out, env.c_out.val)), inc=c_xb2)
                _wm(p, env.ones_row_f, psw,
                    ((c_xb2, c + 1), (env.c_gp, env.gp0)))
                p.emit("vector", lambda e, ob=ob: e.tensor_add(
                    ob[:, :], stmp[:, :], sbb[:, :]), inc=c_ev)
                p.emit("vector", lambda e, c=c, ob=ob: e.reduce_sum(
                    o_mp_sb[:, c:c + 1], ob[:, :], axis=AXX), inc=c_ev)
                _wm(p, env.ones_row_f, psw,
                    ((c_ev, c_ev.val), (env.c_gp, env.gp0)))
                p.dma("sync", o_xnT_dram[c * 128:(c + 1) * 128, :], ob[:, :],
                      env.c_out, after=((c_ev, c_ev.val),))
    return env.c_out.val


def build_l1f(p=None, io=None, env=None):
    standalone = p is None
    if standalone:
        p = Prog()
    nc = p.nc
    x2T = _pin(p, io, "x2T", [D, T], F32)
    yaT = _pin(p, io, "yaT", [D, T], BF16)
    xn2T = _pin(p, io, "xn2T", [D, T], BF16)
    mps_ag = bool(io) and "mps_ag" in io
    if mps_ag:
        mps = io["mps_ag"]
    else:
        mps = _pin(p, io, "mps", [128, NCH, 8], F32)
    WorgT_in = _pin(p, io, "WorgT", [D, D], F32)
    downT_in = _pin(p, io, "downT", [D, 512], F32)
    upF_in = _pin(p, io, "upF", [512, D], F32)
    keysT_in = _pin(p, io, "keysT", [128, 8], F32)
    WqT_in = _pin(p, io, "WqT", [D, 128], F32)
    o_out = _pout(p, io, "outT", [D, T], F32)

    if env is None:
        env = L1Env(p, with_tables=False, with_rows=False)
    c_x, c_cast, c_mm, c_ev, c_sc = (env.c_x, env.c_cast, env.c_mm, env.c_ev,
                                     env.c_sc)
    c_out, c_gp = env.c_out, env.c_gp
    eps_cell = env.cells[0:1, 8:9]

    xn2 = p.sb("xn2f", [128, NCH * T], BF16)
    for c in range(NCH):
        p.dma("sync", xn2[:, c * T:(c + 1) * T],
              xn2T[c * 128:(c + 1) * 128, :], c_x)

    meanc = p.sb("meanc", [128, NCH], F32)
    meanbf = p.sb("meanbf", [128, NCH], BF16)
    qbf = p.sb("qbf", [128, 1], BF16)
    crow = p.sb("crow", [1, 8], F32)
    crow_bf = p.sb("crow_bf", [1, 8], BF16)
    ccol_bf = p.sb("ccol_bf", [8, 1], BF16)
    coefrep = p.sb("coefrep", [128, 4], F32)
    s64 = p.sb("s64", [8, 512], BF16)
    s64b = p.sb("s64b", [8, 512], BF16)
    keys_sb = p.sb("keys_sb", [128, 8], BF16)
    identbf2 = p.sb("identbf2", [128, 128], BF16)
    scol = p.sb("scol", [8, 1], F32)
    with nc.sbuf_tensor("mps_sb", [128, NCH * 8], F32) as mps_sb, \
         nc.sbuf_tensor("wqt", [128, NCH * 128], BF16) as wqt:
        if mps_ag:
            p.dma("sync", mps_sb[:, :].rearrange("p (m a) -> p m a", a=NCH),
                  mps.rearrange("(m p) a -> p m a", p=128), c_x)
            mpv = mps_sb.rearrange("p (m a) -> p a m", a=NCH)
        else:
            p.dma("sync", mps_sb[:, :], mps.rearrange("p a b -> p (a b)"),
                  c_x)
            mpv = mps_sb.rearrange("p (a b) -> p a b", b=8)
        xn_ld = c_x.val
        p.dma("gpsimd", keys_sb[:, :], keysT_in, c_cast)
        src = WqT_in.rearrange("(c p) f -> p c f", p=128)
        dst = wqt[:, :].rearrange("p (c f) -> p c f", f=128)
        p.dma("gpsimd", dst, src, c_cast)
        mq_ld, mc_ld = c_x.val, c_cast.val
        p.emit("gpsimd", lambda e: e.memset(identbf2[:, :], 0.0), inc=c_gp)
        p.emit("gpsimd", lambda e: e.affine_select(
            out=identbf2[:, :], in_=identbf2[:, :], compare_op=ALU.not_equal,
            fill=1.0, base=0, pattern=[[-1, 128]], channel_multiplier=1),
            after=((c_gp, c_gp.val),), inc=c_gp)
        p.emit("gpsimd", lambda e: e.memset(s64[:, :], 1.0), inc=c_gp)
        p.emit("gpsimd", lambda e: e.affine_select(
            out=s64[:, :], in_=s64[:, :], compare_op=ALU.is_ge, fill=0.0,
            base=0, pattern=[[1, 512]], channel_multiplier=-64),
            after=((c_gp, c_gp.val),), inc=c_gp)
        p.emit("gpsimd", lambda e: e.memset(s64b[:, :], 1.0), inc=c_gp)
        p.emit("gpsimd", lambda e: e.affine_select(
            out=s64b[:, :], in_=s64b[:, :], compare_op=ALU.is_ge, fill=0.0,
            base=-64, pattern=[[1, 512]], channel_multiplier=-64),
            after=((c_gp, c_gp.val),), inc=c_gp)
        p.emit("gpsimd", lambda e: e.tensor_sub(s64[:, :], s64[:, :],
                                                s64b[:, :]),
               after=((c_gp, c_gp.val),), inc=c_gp)
        gp_s = c_gp.val
        for c in range(NCH):
            p.emit("vector", lambda e, c=c: e.reduce_sum(
                meanc[:, c:c + 1], mpv[:, c, :], axis=AXX),
                after=((c_x, mq_ld),), inc=c_ev)
        p.emit("vector", lambda e: e.tensor_scalar(
            meanc[:, :], meanc[:, :], 1.0 / S, None, ALU.mult), inc=c_ev)
        p.emit("vector", lambda e: e.tensor_copy(meanbf[:, :], meanc[:, :]),
               inc=c_ev)
        mdone = c_ev.val
        with nc.psum_tensor("psQl", [128, 16], F32) as psQl, \
             nc.psum_tensor("psSc", [8, 16], F32) as psSc, \
             nc.psum_tensor("psCc", [8, 16], BF16) as psCc, \
             nc.psum_tensor("psRp", [128, 16], F32) as psRp:
            for vc in range(NCH):
                p.emit("tensor", lambda e, vc=vc: e.matmul(
                    psQl[:, 0:1], wqt[:, vc * 128:(vc + 1) * 128],
                    meanbf[:, vc:vc + 1],
                    start=(vc == 0), stop=(vc == NCH - 1)),
                    after=((c_cast, mc_ld), (c_ev, mdone)),
                    inc=(c_mm if vc == NCH - 1 else None))
            p.emit("vector", lambda e: e.tensor_copy(qbf[:, :], psQl[:, 0:1]),
                   after=((c_mm, c_mm.val),), inc=c_ev)
            p.emit("tensor", lambda e: e.matmul(
                psSc[:, 0:1], keys_sb[:, :], qbf[:, :], start=True, stop=True),
                after=((c_ev, c_ev.val), (c_cast, mc_ld)), inc=c_mm)
            p.emit("vector", lambda e: e.tensor_scalar(
                scol[:, :], psSc[:, 0:1], 1.0 / math.sqrt(128.0), None,
                ALU.mult), after=((c_mm, c_mm.val),), inc=c_ev)
            p.emit("vector", lambda e: e.tensor_copy(
                ccol_bf[:, :], scol[:, :]), inc=c_ev)
            p.emit("tensor", lambda e: e.transpose(
                psCc[0:1, 0:8], ccol_bf[:, :], identbf2[0:8, 0:8]),
                after=((c_ev, c_ev.val), (c_gp, gp_s)), inc=c_mm)
            p.emit("vector", lambda e: e.tensor_copy(
                crow[0:1, :], psCc[0:1, 0:8]),
                after=((c_mm, c_mm.val),), inc=c_ev)
            p.emit("vector", lambda e: e.reduce_max(
                env.cells[0:1, 0:1], crow[0:1, :], axis=AXX, negate=True),
                inc=c_ev)
            p.emit("scalar", lambda e: e.activation(
                crow[0:1, :], crow[0:1, :], AF.Exp, bias=env.cells[0:1, 0:1]),
                after=((c_ev, c_ev.val),), inc=c_sc)
            p.emit("vector", lambda e: e.reduce_sum(
                env.cells[0:1, 1:2], crow[0:1, :], axis=AXX),
                after=((c_sc, c_sc.val),), inc=c_ev)
            p.emit("vector", lambda e: e.reciprocal(
                env.cells[0:1, 2:3], env.cells[0:1, 1:2]), inc=c_ev)
            p.emit("vector", lambda e: e.tensor_scalar(
                crow[0:1, :], crow[0:1, :], env.cells[0:1, 2:3], None,
                ALU.mult), inc=c_ev)
            p.emit("vector", lambda e: e.tensor_copy(crow_bf[0:1, :],
                                                     crow[0:1, :]), inc=c_ev)
            p.emit("tensor", lambda e: e.transpose(
                psCc[:, 8:9], crow_bf[0:1, :], identbf2[0:1, 0:1]),
                after=((c_ev, c_ev.val),), inc=c_mm)
            p.emit("vector", lambda e: e.tensor_copy(
                ccol_bf[:, :], psCc[0:8, 8:9]),
                after=((c_mm, c_mm.val),), inc=c_ev)
            for ec in range(4):
                p.emit("tensor", lambda e, ec=ec: e.matmul(
                    psRp[:, ec:ec + 1], s64[:, ec * 128:(ec + 1) * 128],
                    ccol_bf[:, :], start=True, stop=True),
                    after=((c_ev, c_ev.val), (c_gp, gp_s)), inc=c_mm)
            p.emit("vector", lambda e: e.tensor_copy(
                coefrep[:, :], psRp[:, 0:4]),
                after=((c_mm, c_mm.val),), inc=c_ev)

    hw = p.sb("hw", [128, G4 * T], BF16)
    with nc.sbuf_tensor("downt", [128, NCH * 512], BF16) as downt:
        c_dt = p.ctr("dt")
        src = downT_in.rearrange("(c p) f -> p c f", p=128)
        dst = downt[:, :].rearrange("p (c f) -> p c f", f=512)
        p.dma("gpsimd", dst, src, c_dt,
              after=((c_ev, c_ev.val), (c_mm, c_mm.val)))
        d_ld = c_dt.val
        with nc.psum_tensor("psL0", [128, 2048], F32) as psL0, \
             nc.psum_tensor("psL1", [128, 2048], F32) as psL1:
            pps = [psL0, psL1]
            for et in range(G4):
                ps = pps[et % 2]
                for ic in range(NCH):
                    for lg in range(4):
                        p.emit("tensor", lambda e, et=et, ic=ic, lg=lg, ps=ps:
                               e.matmul(
                                   ps[:, lg * 512:(lg + 1) * 512],
                                   downt[:, ic * 512 + et * 128:
                                         ic * 512 + (et + 1) * 128],
                                   xn2[:, ic * T + lg * 512:
                                       ic * T + (lg + 1) * 512],
                                   start=(ic == 0), stop=(ic == NCH - 1)),
                               after=((c_dt, d_ld), (c_x, xn_ld),
                                      (c_ev, c_ev.val)),
                               inc=(c_mm if (ic == NCH - 1 and lg == 3)
                                    else None))
                mm_d = c_mm.val
                for lg in range(4):
                    p.emit("vector", lambda e, et=et, lg=lg, ps=ps:
                           e.tensor_scalar(
                               hw[:, et * T + lg * 512:et * T + (lg + 1) * 512],
                               ps[:, lg * 512:(lg + 1) * 512],
                               coefrep[:, et:et + 1], None, ALU.mult),
                           after=((c_mm, mm_d),), inc=c_ev)
    hw_done = c_ev.val

    with nc.sbuf_tensor("wgT0", [128, NCH * 256], BF16) as wgT0, \
         nc.sbuf_tensor("wgT1", [128, NCH * 256], BF16) as wgT1, \
         nc.sbuf_tensor("upt", [128, 4 * 256], BF16) as upt, \
         nc.sbuf_tensor("yab0", [128, 512], BF16) as yab0, \
         nc.sbuf_tensor("yab1", [128, 512], BF16) as yab1, \
         nc.sbuf_tensor("x2b0", [128, 512], F32) as x2b0, \
         nc.sbuf_tensor("x2b1", [128, 512], F32) as x2b1, \
         nc.sbuf_tensor("letmp", [128, 512], F32) as letmp, \
         nc.sbuf_tensor("lout0", [128, 512], F32) as lout0, \
         nc.sbuf_tensor("lout1", [128, 512], F32) as lout1:
        wgT = [wgT0, wgT1]
        yab = [yab0, yab1]
        x2b = [x2b0, x2b1]
        louts = [lout0, lout1]
        c_wb = p.ctr("wb2")
        c_xe = p.ctr("xe2")
        ev_at = {}

        def ldwo(og, aft):
            buf2 = wgT[og % 2]
            for hh in range(2):
                srcW = WorgT_in[hh * 2048:(hh + 1) * 2048,
                                og * 256:(og + 1) * 256]
                srcW = srcW.rearrange("(c p) f -> p c f", p=128)
                dstW = buf2[:, hh * 16 * 256:(hh + 1) * 16 * 256].rearrange(
                    "p (c f) -> p c f", f=256)
                p.dma("gpsimd", dstW, srcW, c_wb, after=aft)
                aft = ()

        def ldep2(j, aft):
            ct2, lg2 = j // 4, j % 4
            i2 = j % 2
            p.dma("scalar", yab[i2][:, :],
                  yaT[ct2 * 128:(ct2 + 1) * 128,
                      lg2 * 512:(lg2 + 1) * 512], c_xe, after=aft)
            p.dma("scalar", x2b[i2][:, :],
                  x2T[ct2 * 128:(ct2 + 1) * 128, lg2 * 512:(lg2 + 1) * 512],
                  c_xe, after=())

        ldwo(0, ((c_ev, hw_done),))
        mm_at_og = {}
        with nc.psum_tensor("psB0", [128, 2048], F32) as psB0, \
             nc.psum_tensor("psB1", [128, 2048], F32) as psB1:
            psB = [psB0, psB1]
            for og in range(16):
                buf = wgT[og % 2]
                if og + 1 < 16:
                    ldwo(og + 1, ((c_mm, mm_at_og.get(og - 1, 0)),))
                src = upF_in[:, og * 256:(og + 1) * 256].rearrange(
                    "(c p) f -> p c f", p=128)
                dst = upt[:, :].rearrange("p (c f) -> p c f", f=256)
                p.dma("gpsimd", dst, src, c_wb,
                      after=(((c_mm, mm_at_og[og - 1]),) if og > 0
                             else ((c_ev, hw_done),)))
                w_ld2 = c_wb.val
                for ot in range(2):
                    ps = psB[ot]
                    o_glob = og * 2 + ot
                    for dc in range(NCH):
                        for lg in range(4):
                            p.emit("tensor",
                                   lambda e, buf=buf, ps=ps, ot=ot, dc=dc,
                                   lg=lg: e.matmul(
                                       ps[:, lg * 512:(lg + 1) * 512],
                                       buf[:, dc * 256 + ot * 128:
                                           dc * 256 + (ot + 1) * 128],
                                       xn2[:, dc * T + lg * 512:
                                           dc * T + (lg + 1) * 512],
                                       start=(dc == 0), stop=False),
                                   after=((c_wb, w_ld2),
                                          ev_at.get(("p", og, ot),
                                                    (c_ev, hw_done))))
                    for ec in range(4):
                        for lg in range(4):
                            p.emit("tensor",
                                   lambda e, ps=ps, ot=ot, ec=ec, lg=lg:
                                   e.matmul(
                                       ps[:, lg * 512:(lg + 1) * 512],
                                       upt[:, ec * 256 + ot * 128:
                                           ec * 256 + (ot + 1) * 128],
                                       hw[:, ec * T + lg * 512:
                                          ec * T + (lg + 1) * 512],
                                       start=False,
                                       stop=(ec == 3)),
                                   after=((c_wb, w_ld2), (c_ev, hw_done)),
                                   inc=(c_mm if (ec == 3 and lg == 3)
                                        else None))
                    mm_d = c_mm.val
                    for lg in range(4):
                        j = o_glob * 4 + lg
                        i2 = j % 2
                        if j == 0:
                            ldep2(0, ((c_x, xn_ld),))
                        if j + 1 < 128:
                            ldep2(j + 1, ((c_ev, ev_at.get(("a", j - 1), 0)),))
                        ld = c_xe.val
                        p.emit("vector",
                               lambda e, ps=ps, lg=lg, i2=i2: e.tensor_add(
                                   letmp[:, :], ps[:, lg * 512:(lg + 1) * 512],
                                   yab[i2][:, :]),
                               after=((c_mm, mm_d), (c_xe, ld)), inc=c_ev)
                        p.emit("vector", lambda e, i2=i2: e.tensor_add(
                            louts[i2][:, :], letmp[:, :], x2b[i2][:, :]),
                            after=((c_out, c_out.val),), inc=c_ev)
                        ev_at[("a", j)] = c_ev.val
                        p.dma("sync",
                              o_out[o_glob * 128:(o_glob + 1) * 128,
                                    lg * 512:(lg + 1) * 512],
                              louts[i2][:, :], c_out,
                              after=((c_ev, c_ev.val),))
                    ev_at[("p", og + 1, ot)] = (c_ev, c_ev.val)
                mm_at_og[og] = c_mm.val
    p.emit("sync", lambda e: e.nop(), after=((c_out, c_out.val),))
    if standalone:
        return p.build()
    return c_out



def _ag(p, src, dst, gate, cnm="cc"):
    c = p.ctr(cnm)
    p.emit("gpsimd", lambda e, s=src, d=dst: e.collective_compute(
        "AllGather", ALU.bypass, replica_groups=[list(range(NCORES))],
        ins=[s.opt()], outs=[d.opt()]),
        after=gate, inc=c, n=1, dma=True)
    return c


def _pbar(p):
    c = p.ctr("bar")
    p.emit("sync", lambda e: e.nop(), after=p.barrier(), inc=c, n=1)
    p.phase_gate = ((c, 1),)
    return p.phase_gate


def build_full():
    p = Prog()
    io = {}
    io["xT"] = p.din("xT", [D, T], F32)
    io["Wqkv"] = p.din("Wqkv", [D, 3 * DH], F32)
    io["bqkv"] = p.din("bqkv", [3 * DH], F32)
    io["WbT"] = p.din("WbT", [D, D], F32)
    io["b_base"] = p.din("b_base", [D], F32)
    io["WgtT"] = p.din("WgtT", [D, 9], F32)
    io["b_thr"] = p.din("b_thr", [1], F32)
    io["AT"] = p.din("AT", [D, 128], F32)
    io["BmT"] = p.din("BmT", [128, D], F32)
    io["Wo"] = p.din("Wo", [DH, D], F32)
    io["bo"] = p.din("bo", [D], F32)
    io["WinT"] = p.din("WinT", [D, DIM], F32)
    io["bin"] = p.din("bin", [DIM], F32)
    io["l1p_in"] = p.din("l1p_in", [128, PCOLS], F32)
    io["kvec"] = p.din("kvec", [1, 8], F32)
    io["rkc"] = p.din("rkc", [1, 2], F32)
    io["W1T"] = p.din("W1T", [4, DIM, DFF], F32)
    io["W2T"] = p.din("W2T", [4, DFF, DIM], F32)
    io["WoutT"] = p.din("WoutT", [DIM, D], F32)
    io["WorgT"] = p.din("WorgT", [D, D], F32)
    io["downT"] = p.din("downT", [D, 512], F32)
    io["upF"] = p.din("upF", [512, D], F32)
    io["keysT"] = p.din("keysT", [128, 8], F32)
    io["WqT"] = p.din("WqT", [D, 128], F32)
    io["outT"] = p.dout("outT", [D, T], F32)
    io["xnT"] = p.dtmp("d_xnT", [D, T], BF16)
    io["qT"] = p.dtmp("d_qT", [DH, T], BF16)
    io["kT"] = p.dtmp("d_kT", [DH, T], BF16)
    io["V"] = p.dtmp("d_V", [T, DH], BF16)
    io["qm"] = p.dtmp("d_qm", [1, 1], F32)
    io["km"] = p.dtmp("d_km", [1, 1], F32)
    io["kTg"] = p.dtmp("d_kTg", [NCORES * DH, T], BF16,
                       addr_space="Shared")
    io["Vg"] = p.dtmp("d_Vg", [S, DH], BF16, addr_space="Shared")
    io["kms"] = p.dtmp("d_kms", [NCORES, 1], F32)
    io["x2T"] = p.dtmp("d_x2T", [D, T], F32)
    io["xn2T"] = p.dtmp("d_xn2T", [D, T], BF16)
    io["mp"] = p.dtmp("d_mp", [128, NCH], F32)
    d_mpg = p.dtmp("d_mpg", [NCORES * 128, NCH], F32)
    io["mps_ag"] = d_mpg
    d_E = p.dtmp("d_E", [128, 8], F32)
    d_Eg = p.dtmp("d_Eg", [NCORES * 128, 8], F32)
    d_h = [p.dtmp(f"d_h{j}", [DIM, T], F32) for j in range(2)]
    d_gre = [p.dtmp(f"d_gre{j}", [DIM, T], F32) for j in range(2)]
    d_gim = [p.dtmp(f"d_gim{j}", [DIM, T], F32) for j in range(2)]
    io["yaT"] = p.dtmp("d_yaT", [D, T], BF16)

    p.push_scope()
    build_l0a(p=p, io=io)
    p.pop_scope()
    _ag(p, io["kT"], io["kTg"], (io.pop("kv_stored"),), "cc_k")
    _ag(p, io["V"], io["Vg"], (io.pop("v_stored"),), "cc_v")
    _pbar(p)
    _ag(p, io["km"], io["kms"], p.phase_gate, "cc_km")
    _pbar(p)

    p.push_scope()
    build_l0b(p=p, io=io)
    p.pop_scope()
    _pbar(p)
    p.emit("gpsimd", lambda e: e.engine_nop(), after=p.phase_gate)

    env = L1Env(p, with_tables=False, with_rows=False)
    _l1_env_setup(p, env, io)
    p.push_scope()
    env.jt = p.sb("jt", [128, T], F32)
    io["h_sb"] = p.sb("h_res", [128, G4 * T], F32)
    io["gre_sb"] = p.sb("gre_res", [128, G4 * T], F32)
    p.emit("gpsimd", lambda e: e.iota(
        env.jt[:, :], pattern=[[1, T]], base=0, channel_multiplier=0,
        allow_small_or_imprecise_dtypes=True), inc=env.c_gp)
    env.gp0 = env.c_gp.val

    p.push_scope()
    env.rows = p.sb("rows", [1, 3 * T], F32)
    io1 = dict(io)
    io1["gim"] = d_gim[0]
    io1["E"] = d_E
    build_l1a(p=p, io=io1, env=env)
    p.pop_scope()
    _pbar(p)
    _ag(p, io["mp"], d_mpg, p.phase_gate, "cc_mp")
    _ag(p, d_E, d_Eg, p.phase_gate, "cc_e0")
    _pbar(p)

    for li in range(4):
        last = (li == 3)
        io2 = dict(io)
        io2["gim"] = d_gim[li % 2]
        io2["Eall_ag"] = d_Eg
        if not last:
            io2["gim_o"] = d_gim[(li + 1) % 2]
            io2["E"] = d_E
        p.push_scope()
        build_l1s(li, last, p=p, io=io2, env=env)
        p.pop_scope()
        _pbar(p)
        if not last:
            _ag(p, d_E, d_Eg, p.phase_gate, f"cc_e{li + 1}")
            _pbar(p)

    p.pop_scope()
    for k in ("h_sb", "gre_sb"):
        io.pop(k, None)
    _pbar(p)
    p.push_scope()
    build_l1f(p=p, io=io, env=env)
    p.pop_scope()
    p.emit("sync", lambda e: e.nop(), after=p.barrier())
    return p.build()



def _shard_T(x2d):
    return [np.ascontiguousarray(x2d[i * T:(i + 1) * T, :].T) for i in range(NCORES)]


_BUILDS = {}


def _get_nc(key, fn):
    nc = _BUILDS.get(key)
    if nc is None:
        nc = fn()
        _BUILDS[key] = nc
    return nc


class _Res:
    def __init__(self, results):
        self.results = results


def _launch(nc, in_maps):
    import os as _oslt
    import time as _tlt
    t0 = _tlt.time()
    r = run_bass_kernel_spmd(nc, in_maps, core_ids=list(range(NCORES)))
    if _oslt.environ.get("KLOG"):
        print(f"[launch {_tlt.time():.2f}] took {_tlt.time() - t0:.2f}s",
              flush=True)
    return _Res([{k: np.array(v) for k, v in res.items()} for res in r.results])


def run_l0a(x2d, Wqkv, bqkv):
    nc = _get_nc("l0a", build_l0a)
    xts = _shard_T(x2d)
    in_maps = [{"xT": xts[i], "Wqkv": Wqkv, "bqkv": bqkv} for i in range(NCORES)]
    res = _launch(nc, in_maps)
    return res.results


def run_l0b(ins, l0a_res, upto="all"):
    import os
    upto = os.environ.get("L0B_UPTO", upto)
    nc = _get_nc(("l0b", upto), lambda: build_l0b(upto=upto))
    xts = _shard_T(ins["L0_attn_x"][0])
    kTg = np.concatenate([r["kT"] for r in l0a_res], axis=0)
    Vg = np.concatenate([r["V"] for r in l0a_res], axis=0)
    kms = np.stack([r["km"][0] for r in l0a_res], axis=0)
    WbT = np.ascontiguousarray(ins["L0_ffn_W_base"].T)
    WgtT = np.ascontiguousarray(
        np.concatenate([ins["L0_ffn_W_gate"], ins["L0_ffn_W_thr"]], axis=0).T)
    AT = np.ascontiguousarray(ins["L0_ffn_A"].reshape(128, D).T)
    BmT = np.ascontiguousarray(
        ins["L0_ffn_B"].transpose(0, 2, 1).reshape(128, D))
    in_maps = []
    for i in range(NCORES):
        in_maps.append({
            "xnT": l0a_res[i]["xnT"], "qT": l0a_res[i]["qT"],
            "kTg": kTg, "Vg": Vg, "qm": l0a_res[i]["qm"], "kms": kms,
            "xT": xts[i],
            "WbT": WbT, "b_base": ins["L0_ffn_b_base"],
            "WgtT": WgtT, "b_thr": ins["L0_ffn_b_thr"],
            "AT": AT, "BmT": BmT,
            "Wo": ins["L0_attn_Wo"], "bo": ins["L0_attn_bo"],
        })
    res = _launch(nc, in_maps)
    return res.results


def _pack_l1p(ins):
    arr = np.zeros((128, PCOLS), np.float32)

    def setcols(base, param):
        for g in range(param.size // 128):
            arr[:, base + g] = param[g * 128:(g + 1) * 128]
    for i in range(4):
        setcols(_pcol(i, "ln1_w"), ins["L1_attn_ln1_w"][i])
        setcols(_pcol(i, "ln1_b"), ins["L1_attn_ln1_b"][i])
        setcols(_pcol(i, "phz_re"), ins["L1_attn_phz_re"][i])
        setcols(_pcol(i, "phz_im"), ins["L1_attn_phz_im"][i])
        setcols(_pcol(i, "lc_re"), ins["L1_attn_lc_re"][i])
        setcols(_pcol(i, "lc_im"), ins["L1_attn_lc_im"][i])
        setcols(_pcol(i, "sc_lin"), ins["L1_attn_sc_lin"][i])
        setcols(_pcol(i, "ln2_w"), ins["L1_attn_ln2_w"][i])
        setcols(_pcol(i, "ln2_b"), ins["L1_attn_ln2_b"][i])
        setcols(_pcol(i, "b2"), ins["L1_attn_b2"][i])
        setcols(_pcol(i, "b1"), ins["L1_attn_b1"][i])
    setcols(PCOL_LNLW, ins["L1_attn_lnl_w"])
    setcols(PCOL_LNLB, ins["L1_attn_lnl_b"])
    setcols(PCOL_BOUT, ins["L1_attn_bout"])
    return arr


def _l1_percore(i):
    kvec = np.full((1, 8), 1e9, np.float32)
    for m in range(i):
        kvec[0, m] = i - 1 - m
    rkc = np.array([[2048.0 * i, float(i)]], np.float32)
    return kvec, rkc


def run_l1(ins, x2T_list):
    l1p = _pack_l1p(ins)
    WinT = np.ascontiguousarray(ins["L1_attn_Win"].T)
    W1T = np.ascontiguousarray(ins["L1_attn_W1"].transpose(0, 2, 1))
    W2T = np.ascontiguousarray(ins["L1_attn_W2"].transpose(0, 2, 1))
    WoutT = np.ascontiguousarray(ins["L1_attn_Wout"].T)
    percore = [_l1_percore(i) for i in range(NCORES)]

    nc = _get_nc("l1a", build_l1a)
    im = [{"x2T": x2T_list[i], "WinT": WinT, "bin": ins["L1_attn_bin"],
           "l1p_in": l1p, "kvec": percore[i][0], "rkc": percore[i][1]}
          for i in range(NCORES)]
    res = _launch(nc, im)
    st = res.results

    for li in range(4):
        Eall = np.ascontiguousarray(
            np.stack([st[i]["E"] for i in range(NCORES)], axis=2))
        last = (li == 3)
        nc = _get_nc(("l1s", li), lambda li=li, last=last: build_l1s(li, last))
        im = []
        for i in range(NCORES):
            d = {"hT": st[i]["hT"] if li == 0 else st[i]["hT_o"],
                 "gre": st[i]["gre"] if li == 0 else st[i]["gre_o"],
                 "gim": st[i]["gim"] if li == 0 else st[i]["gim_o"],
                 "Eall": Eall, "W1T": W1T, "W2T": W2T,
                 "l1p_in": l1p, "kvec": percore[i][0], "rkc": percore[i][1]}
            if last:
                d["WoutT"] = WoutT
            im.append(d)
        res = _launch(nc, im)
        st2 = res.results
        for i in range(NCORES):
            for kk in ("xn2T", "mp"):
                st2[i][kk] = st[i][kk]
        st = st2
    return st


def run_l1f(ins, x2T_list, st):
    WorgT = np.ascontiguousarray(ins["L1_ffn_W_org"].T)
    downT = np.ascontiguousarray(
        ins["L1_ffn_lora_down"].transpose(1, 0, 2).reshape(D, 512))
    upF = np.ascontiguousarray(ins["L1_ffn_lora_up"].reshape(512, D))
    keysT = np.ascontiguousarray(ins["L1_ffn_lora_keys"].T)
    WqT = np.ascontiguousarray(ins["L1_ffn_Wq"].T)
    mps = np.ascontiguousarray(
        np.stack([st[i]["mp"] for i in range(NCORES)], axis=2))
    nc = _get_nc("l1f", build_l1f)
    im = [{"x2T": x2T_list[i], "yaT": st[i]["yaT"], "xn2T": st[i]["xn2T"],
           "mps": mps, "WorgT": WorgT, "downT": downT, "upF": upF,
           "keysT": keysT, "WqT": WqT} for i in range(NCORES)]
    res = _launch(nc, im)
    return res.results


def kernel_unfused(**inputs):
    ins = {k: np.asarray(v, np.float32) for k, v in inputs.items()}
    x = ins["L0_attn_x"][0]
    l0a = run_l0a(x, ins["L0_attn_Wqkv"], ins["L0_attn_bqkv"])
    l0b = run_l0b(ins, l0a)
    x2T_list = [np.ascontiguousarray(r["x2T"]) for r in l0b]
    st = run_l1(ins, x2T_list)
    res = run_l1f(ins, x2T_list, st)
    out = np.concatenate([res[i]["outT"].T for i in range(NCORES)], axis=0)
    return np.ascontiguousarray(out.reshape(1, S, D), dtype=np.float32)


def kernel(**inputs):
    ins = {k: np.asarray(v, np.float32) for k, v in inputs.items()}
    xts = _shard_T(ins["L0_attn_x"][0])
    l1p = _pack_l1p(ins)
    WbT = np.ascontiguousarray(ins["L0_ffn_W_base"].T)
    WgtT = np.ascontiguousarray(
        np.concatenate([ins["L0_ffn_W_gate"], ins["L0_ffn_W_thr"]], axis=0).T)
    AT = np.ascontiguousarray(ins["L0_ffn_A"].reshape(128, D).T)
    BmT = np.ascontiguousarray(ins["L0_ffn_B"].transpose(0, 2, 1).reshape(128, D))
    WinT = np.ascontiguousarray(ins["L1_attn_Win"].T)
    W1T = np.ascontiguousarray(ins["L1_attn_W1"].transpose(0, 2, 1))
    W2T = np.ascontiguousarray(ins["L1_attn_W2"].transpose(0, 2, 1))
    WoutT = np.ascontiguousarray(ins["L1_attn_Wout"].T)
    WorgT = np.ascontiguousarray(ins["L1_ffn_W_org"].T)
    downT = np.ascontiguousarray(
        ins["L1_ffn_lora_down"].transpose(1, 0, 2).reshape(D, 512))
    upF = np.ascontiguousarray(ins["L1_ffn_lora_up"].reshape(512, D))
    keysT = np.ascontiguousarray(ins["L1_ffn_lora_keys"].T)
    WqT = np.ascontiguousarray(ins["L1_ffn_Wq"].T)
    nc = _get_nc("full", build_full)
    im = []
    for i in range(NCORES):
        kv, rk = _l1_percore(i)
        im.append({
            "xT": xts[i], "Wqkv": ins["L0_attn_Wqkv"],
            "bqkv": ins["L0_attn_bqkv"], "WbT": WbT,
            "b_base": ins["L0_ffn_b_base"], "WgtT": WgtT,
            "b_thr": ins["L0_ffn_b_thr"], "AT": AT, "BmT": BmT,
            "Wo": ins["L0_attn_Wo"], "bo": ins["L0_attn_bo"],
            "WinT": WinT, "bin": ins["L1_attn_bin"], "l1p_in": l1p,
            "kvec": kv, "rkc": rk, "W1T": W1T, "W2T": W2T, "WoutT": WoutT,
            "WorgT": WorgT, "downT": downT, "upF": upF, "keysT": keysT,
            "WqT": WqT})
    res = _launch(nc, im)
    out = np.concatenate([res.results[i]["outT"].T for i in range(NCORES)],
                         axis=0)
    return np.ascontiguousarray(out.reshape(1, S, D), dtype=np.float32)
